# version 02
# baseline (speedup 1.0000x reference)
import numpy as np
import ml_dtypes
from contextlib import ExitStack
from collections import defaultdict

import concourse.bass as bass
import concourse.mybir as mybir
from concourse.bass_utils import run_bass_kernel_spmd
from concourse.alu_op_type import AluOpType as OP

F32 = mybir.dt.float32
BF16 = mybir.dt.bfloat16
I32 = mybir.dt.int32
AF = mybir.ActivationFunctionType
AX = mybir.AxisListType
BFNP = ml_dtypes.bfloat16

D = 4096
TOK = 1024
NCORES = 8
SEQ = 2048
NEG = -1.0e30


def _tile_lhsT(wT, nchunk_f, nblk_d):
    F, O = wT.shape
    assert F == nchunk_f * 128 and O == nblk_d * 128
    return np.ascontiguousarray(
        wT.reshape(nchunk_f, 128, nblk_d, 128).transpose(2, 1, 0, 3).reshape(nblk_d, 128, nchunk_f * 128))


def _tile_rhs(wT, nchunk_f, width):
    F, O = wT.shape
    assert F == nchunk_f * 128
    return np.ascontiguousarray(wT.reshape(nchunk_f, 128, O))


def _tile_obff(wT, nchunk_k, nob, obw=512):
    K, O = wT.shape
    assert K == nchunk_k * 128 and O == nob * obw
    return np.ascontiguousarray(
        wT.reshape(nchunk_k, 128, nob, obw).transpose(2, 0, 1, 3).reshape(nob, nchunk_k, 128, obw))


def prep_inputs(inp):
    f32 = lambda a: np.ascontiguousarray(np.asarray(a), dtype=np.float32)
    x = f32(inp["L0_attn_x"])

    sh = {}
    Wqk = f32(inp["L0_attn_Wqk_w"]).reshape(8, 64, 2, D)
    Wq = Wqk[:, :, 0, :].reshape(512, D)
    Wk = Wqk[:, :, 1, :].reshape(512, D)
    qkb = f32(inp["L0_attn_Wqk_b"]).reshape(8, 64, 2)
    sh["wq0"] = _tile_lhsT(np.ascontiguousarray(Wq.T), 32, 4)
    sh["wk0"] = _tile_lhsT(np.ascontiguousarray(Wk.T), 32, 4)
    sh["qb0"] = np.ascontiguousarray(qkb[:, :, 0].reshape(1, 512))
    sh["kb0"] = np.ascontiguousarray(qkb[:, :, 1].reshape(1, 512))
    sh["wv0"] = _tile_rhs(np.ascontiguousarray(f32(inp["L0_attn_Wv_w"]).T), 32, 512)
    sh["vb0"] = f32(inp["L0_attn_Wv_b"]).reshape(1, 512)
    sh["wo0"] = _tile_obff(np.ascontiguousarray(f32(inp["L0_attn_Wo_w"]).T), 4, 8)
    sh["ob0"] = f32(inp["L0_attn_Wo_b"]).reshape(1, D)
    sh["wf"] = _tile_obff(np.ascontiguousarray(f32(inp["L0_ffn_weight"]).T), 32, 8)
    sh["fb"] = f32(inp["L0_ffn_bias"]).reshape(1, D)
    sh["cent"] = _tile_rhs(np.ascontiguousarray(f32(inp["L0_ffn_centroids"]).T), 32, 64)
    sh["assign"] = np.ascontiguousarray(np.asarray(inp["L0_ffn_assignments"], dtype=np.int32).reshape(1, D))
    sh["wg"] = _tile_lhsT(np.ascontiguousarray(f32(inp["L1_ffn_w_gate"]).T), 32, 86)
    sh["wu"] = _tile_lhsT(np.ascontiguousarray(f32(inp["L1_ffn_w_up"]).T), 32, 86)
    sh["bg"] = f32(inp["L1_ffn_b_gate"]).reshape(1, 11008)
    sh["bu"] = f32(inp["L1_ffn_b_up"]).reshape(1, 11008)
    sh["wd"] = _tile_obff(np.ascontiguousarray(f32(inp["L1_ffn_w_down"]).T), 86, 8)
    sh["bd"] = f32(inp["L1_ffn_b_down"]).reshape(1, D)
    sh["ident"] = np.eye(128, dtype=np.float32)
    sh["ones"] = np.ones((1, 1024), dtype=np.float32)
    sh["iota64"] = np.arange(64, dtype=np.float32).reshape(64, 1)
    cm = np.zeros((4, 128, 512), dtype=np.float32)
    for r in range(4):
        i = np.arange(128)[:, None]
        j = np.arange(512)[None, :]
        cm[r] = np.where(j <= r * 128 + i, 0.0, NEG)
    sh["cmask"] = np.ascontiguousarray(cm.transpose(1, 0, 2).reshape(128, 2048))

    wq1_full = f32(inp["L1_attn_w_q"])
    wk1_full = f32(inp["L1_attn_w_k"])
    wv1_full = f32(inp["L1_attn_w_v"])
    wo1_full = f32(inp["L1_attn_w_o"])

    cores = []
    for c in range(NCORES):
        b, h = c // 2, c % 2
        d = {}
        if h == 0:
            tok_idx = np.r_[0:512, 1536:2048]
            heads = list(range(16))
            kvh = list(range(4))
        else:
            tok_idx = np.r_[512:1536]
            heads = list(range(16, 32))
            kvh = list(range(4, 8))
        d["x"] = np.ascontiguousarray(x[b][tok_idx])
        wq_m = np.stack([
            _tile_lhsT(np.ascontiguousarray(wq1_full[:, hh * 128:(hh + 1) * 128]), 32, 1)[0]
            for hh in heads])
        d["wq1"] = np.ascontiguousarray(wq_m)
        wk_m = np.stack([
            _tile_lhsT(np.ascontiguousarray(wk1_full[:, g * 128:(g + 1) * 128]), 32, 1)[0]
            for g in kvh])
        d["wk1"] = np.ascontiguousarray(wk_m)
        vcols = np.concatenate([np.arange(g * 128, (g + 1) * 128) for g in kvh])
        d["wv1"] = _tile_rhs(np.ascontiguousarray(wv1_full[:, vcols]), 32, 512)
        order = heads + [hh for hh in range(32) if hh not in heads]
        wo_p = np.concatenate([wo1_full[hh * 128:(hh + 1) * 128] for hh in order], axis=0)
        d["wo1"] = _tile_obff(wo_p, 32, 8)
        bl = np.zeros((128, 2), dtype=np.float32)
        bl[:, 0] = 1.0 if h == 0 else 0.0
        bl[:, 1] = 0.0 if h == 0 else 1.0
        d["blend"] = bl
        cores.append(d)
    return sh, cores


def scatter_output(res_list):
    out = np.zeros((4, 2048, D), dtype=np.float32)
    for c in range(NCORES):
        b, h = c // 2, c % 2
        r = res_list[c]["out"]
        if h == 0:
            out[b, 0:512] = r[0:512]
            out[b, 1536:2048] = r[512:1024]
        else:
            out[b, 512:1536] = r
    return out



ENGINES = ("tensor", "vector", "scalar", "sync", "gpsimd")


class Plan:
    def __init__(self, nc, stack):
        self.nc = nc
        self.stack = stack
        self.streams = {e: [] for e in ENGINES}
        self.sems = {}
        self.counts = defaultdict(int)

    def sem(self, name):
        if name not in self.sems:
            self.sems[name] = self.stack.enter_context(self.nc.semaphore(name))
        return self.sems[name]

    def emit(self, eng, fn):
        self.streams[eng].append([fn, False])

    def wait(self, eng, sname, val):
        s = self.sem(sname)
        self.emit(eng, lambda e, s=s, v=val: e.wait_ge(s, v))

    def dma(self, eng, out_ap, in_ap, sname):
        s = self.sem(sname)
        self.counts[sname] += 16
        v = self.counts[sname]
        self.emit(eng, lambda e, o=out_ap, i=in_ap, s=s: e.dma_start(out=o, in_=i).then_inc(s, 16))
        self.streams[eng][-1][1] = True
        return v

    def op(self, eng, fn, sname=None):
        if sname is None:
            self.emit(eng, fn)
            return None
        s = self.sem(sname)
        self.counts[sname] += 1
        v = self.counts[sname]

        def wrapped(e, fn=fn, s=s):
            fn(e).then_inc(s, 1)
        self.emit(eng, wrapped)
        self.streams[eng][-1][1] = True
        return v

    def collective(self, kind, op, groups, in_ap, out_ap, sname):
        s = self.sem(sname)
        self.counts[sname] += 1
        v = self.counts[sname]

        def fn(e, s=s):
            e.collective_compute(kind, op, replica_groups=groups,
                                 ins=[in_ap], outs=[out_ap]).then_inc(s)
        self.emit("gpsimd", fn)
        self.streams["gpsimd"][-1][1] = True
        return v

    def mark(self, eng, sname):
        s = self.sem(sname)
        self.counts[sname] += 1
        v = self.counts[sname]
        last = self.streams[eng][-1]
        assert not last[1], f"mark({eng}): last instruction already carries an inc"
        oldfn = last[0]

        def wrapped(e, oldfn=oldfn, s=s):
            inst = oldfn(e)
            inst.then_inc(s, 1)
        last[0] = wrapped
        last[1] = True
        return v

    def barrier_mark(self, eng, sname):
        s = self.sem(sname)
        self.counts[sname] += 1
        v = self.counts[sname]
        self.emit(eng, lambda e: e.drain())
        self.emit(eng, lambda e, s=s: e.nop().then_inc(s, 1))
        return v


class Arena:
    def __init__(self, nc, stack, n_bf16):
        self.t = stack.enter_context(nc.sbuf_tensor("arena", [128, n_bf16], BF16))
        self.n = n_bf16

    def view(self, off_bf16, length, dtype):
        if dtype == BF16:
            ap = self.t[:, off_bf16:off_bf16 + length]
            return ap
        elif dtype == F32:
            ap = self.t[:, off_bf16:off_bf16 + 2 * length].bitcast(F32)
            return ap
        raise ValueError(dtype)


PAIR_GROUPS = [[0, 1], [2, 3], [4, 5], [6, 7]]
ALL_GROUPS = [list(range(8))]


XNT_O, XNT_L = 0, 32768
W0_O, W0_L = 32768, 4096
W1_O, W1_L = 36864, 4096
KT_O, KT_L = 40960, 8192
V_O, V_L = 49152, 8192
Q_O, Q_L = 57344, 8192
P_O, P_L = 65536, 2048
PT_O, PT_L = 67584, 2048
ATT_O, ATT_L = 69632, 8192
R0_O = 77824
R1_O = 86016
ARENA_LEN = 94208

ALL_ENG = list(ENGINES)


def build_launch(which):
    nc = bass.Bass("TRN2", target_bir_lowering=False)
    stack = ExitStack()

    def din(name, shape, dt=F32):
        return nc.dram_tensor(name, list(shape), dt, kind="ExternalInput").ap()

    def dout(name, shape, dt=F32):
        return nc.dram_tensor(name, list(shape), dt, kind="ExternalOutput").ap()

    def dint(name, shape, dt=F32):
        return nc.dram_tensor(name, list(shape), dt).ap()

    T = {}
    fused = which == "ALL"
    segs = {"A": ["ln1", "l0kvq", "l0attn", "l0oproj"],
            "B": ["ln2", "l0sel", "l0dense"],
            "C": ["ln3", "l1kv", "l1groups", "l1oproj"],
            "D": ["ln4", "l1ffnA", "l1ffnB"]}
    if isinstance(which, (list, tuple)):
        phases = list(which)
        fused = False
    else:
        phases = segs[which] if not fused else segs["A"] + segs["B"] + segs["C"] + segs["D"]

    need = set(phases)

    if fused:
        T["rin"] = din("rin", (TOK, D))
        T["ra"] = dint("Ra", (TOK, D))
        T["rb"] = dint("Rb", (TOK, D))
        T["rout"] = dout("out", (TOK, D))
        resid = {"ln1": ("rin", None), "l0oproj": ("rin", "ra"),
                 "ln2": ("ra", None), "l0dense": ("ra", "rb"),
                 "ln3": ("rb", None), "l1oproj": ("rb", "ra"),
                 "ln4": ("ra", None), "l1ffnB": ("ra", "rout")}
    else:
        T["rin"] = din("rin", (TOK, D))
        T["rout"] = dout("out", (TOK, D))
        resid = {}
        for p in ["ln1", "ln2", "ln3", "ln4"]:
            resid[p] = ("rin", None)
        for p in ["l0oproj", "l0dense", "l1oproj", "l1ffnB"]:
            resid[p] = ("rin", "rout")

    T["ident_d"] = din("ident", (128, 128))
    T["ones_d"] = din("ones", (1, 1024))

    if "l0kvq" in need:
        T["wq0"] = din("wq0", (4, 128, 4096)); T["wk0"] = din("wk0", (4, 128, 4096))
        T["wv0"] = din("wv0", (8, 128, 2048))
        T["qb0"] = din("qb0", (1, 512)); T["kb0"] = din("kb0", (1, 512)); T["vb0"] = din("vb0", (1, 512))
        T["cc0k_in"] = dint("cc0k_in", (128, 4096), BF16)
        T["cc0k_out"] = dint("cc0k_out", (256, 4096), BF16)
        T["cc0v_in"] = dint("cc0v_in", (128, 4096), BF16)
        T["cc0v_out"] = dint("cc0v_out", (256, 4096), BF16)
    if "l0oproj" in need:
        T["wo0"] = din("wo0", (8, 1, 128, 2048))
        T["ob0"] = din("ob0", (1, D))
    if "l0sel" in need:
        T["cent"] = din("cent", (128, 2048))
        T["assign"] = din("assign", (1, D), I32)
        T["iota64_d"] = din("iota64", (64, 1))
        T["hits_in"] = dint("hits_in", (64, 1))
        T["hits_out"] = dint("hits_out", (64, 1))
    if "l0dense" in need:
        T["wf"] = din("wf", (8, 8, 128, 2048))
        T["fb"] = din("fb", (1, D))
    if "ln3" in need or "l1kv" in need:
        T["cc1_in"] = dint("cc1_in", (256, 16384), BF16)
        T["cc1_out"] = dint("cc1_out", (512, 16384), BF16)
    if "l1kv" in need:
        T["wk1"] = din("wk1", (4, 128, 4096)); T["wv1"] = din("wv1", (8, 128, 2048))
    if "l1groups" in need:
        T["wq1"] = din("wq1", (16, 128, 4096))
        T["cmask_d"] = din("cmask", (128, 2048))
        for g in range(4):
            T[f"cc2_in{g}"] = dint(f"cc2_in{g}", (128, 8192), BF16)
            T[f"cc2_out{g}"] = dint(f"cc2_out{g}", (256, 8192), BF16)
    if "l1oproj" in need:
        T["wo1"] = din("wo1", (8, 8, 128, 2048))
        T["blend_d"] = din("blend", (128, 2))
    if "l1ffnA" in need:
        T["wg"] = din("wg", (86, 128, 4096)); T["wu"] = din("wu", (86, 128, 4096))
        T["bg"] = din("bg", (1, 11008)); T["bu"] = din("bu", (1, 11008))
        T["m_d"] = dint("m_d", (86 * 128, 1024), BF16)
    if "l1ffnB" in need:
        if "m_d" not in T:
            T["m_d"] = dint("m_d", (86 * 128, 1024), BF16)
        T["wd"] = din("wd", (8, 86, 128, 512))
        T["bd"] = din("bd", (1, D))

    with stack:
        ar = Arena(nc, stack, ARENA_LEN)
        e = stack.enter_context
        sb = {}
        sb["ident"] = e(nc.sbuf_tensor("identsb", [128, 128], BF16))
        sb["ones"] = e(nc.sbuf_tensor("onessb", [1, 1024], BF16))
        sb["stat"] = e(nc.sbuf_tensor("stat", [128, 32], F32))
        sb["qsel"] = e(nc.sbuf_tensor("qselsb", [128, 8], F32))
        sb["blend"] = e(nc.sbuf_tensor("blendsb", [128, 2], F32))
        sb["iota"] = e(nc.sbuf_tensor("iotasb", [64, 1], F32))
        sb["c64a"] = e(nc.sbuf_tensor("c64a", [64, 2], F32))
        sb["c64bf"] = e(nc.sbuf_tensor("c64bf", [64, 1], BF16))
        psA = e(nc.psum_tensor("psA", [128, 2048], F32))
        psB = e(nc.psum_tensor("psB", [128, 2048], F32))

        P = Plan(nc, stack)
        sb["rows"] = ar.t[0:1, KT_O:KT_O + 11008]
        sb["rows2"] = ar.t[0:1, KT_O + 11008:KT_O + 22016]
        sb["rows3"] = ar.t[0:1, KT_O + 22016:KT_O + 26112]
        sb["cmask"] = ar.t[:, W0_O:W0_O + 4096].bitcast(F32)
        sb["onehot"] = ar.t[0:64, ATT_O:ATT_O + 4096]
        sb["arowi"] = ar.t[0:1, KT_O + 1024:KT_O + 9216].bitcast(I32)
        sb["arow"] = ar.t[0:1, KT_O + 9216:KT_O + 17408].bitcast(F32)
        sb["arbf"] = ar.t[0:1, V_O + 1024:V_O + 5120]
        T["sb"] = sb
        T["ar"] = ar
        T["psA"] = psA
        T["psB"] = psB

        def bank(i):
            return (psA if i < 4 else psB)[:, (i % 4) * 512:(i % 4) * 512 + 512]

        def bankbf(i):
            return (psA if i < 4 else psB)[:].bitcast(BF16)[:, (i % 4) * 1024:(i % 4) * 1024 + 1024]

        T["bank"] = bank
        T["bankbf"] = bankbf

        P.dma("gpsimd", sb["ident"][:], T["ident_d"], "gd")
        P.dma("gpsimd", sb["ones"][:], T["ones_d"], "gd")
        if "l1oproj" in need:
            P.dma("sync", sb["blend"][:], T["blend_d"], "sd")
        if "l0sel" in need:
            P.dma("sync", sb["iota"][:], T["iota64_d"], "sd")
        pre_gd = P.counts["gd"]
        pre_sd = P.counts["sd"]
        for eng in ("tensor", "vector", "scalar"):
            P.wait(eng, "gd", pre_gd)
            P.wait(eng, "sd", pre_sd)
        P.wait("gpsimd", "sd", pre_sd)
        P.wait("sync", "gd", pre_gd)

        bar_n = [0]

        def barrier():
            bar_n[0] += 1
            P.wait("sync", "sd", P.counts["sd"])
            P.wait("gpsimd", "gd", P.counts["gd"])
            vals = {}
            for eng in ALL_ENG:
                vals[eng] = P.barrier_mark(eng, "bar_" + eng)
            for eng in ALL_ENG:
                for oth in ALL_ENG:
                    if oth != eng:
                        P.wait(eng, "bar_" + oth, vals[oth])
        T["barrier"] = barrier

        for ph in phases:
            emit_phase(P, T, ph, resid.get(ph), fused)
            barrier()

        P.wait("sync", "sd", P.counts["sd"])
        P.wait("gpsimd", "gd", P.counts["gd"])

        with nc.Block() as block:
            @block.tensor
            def _(t):
                for fn, _ in P.streams["tensor"]:
                    fn(t)

            @block.vector
            def _(v):
                for fn, _ in P.streams["vector"]:
                    fn(v)

            @block.scalar
            def _(s):
                for fn, _ in P.streams["scalar"]:
                    fn(s)

            @block.sync
            def _(s):
                for fn, _ in P.streams["sync"]:
                    fn(s)

            @block.gpsimd
            def _(g):
                for fn, _ in P.streams["gpsimd"]:
                    fn(g)

    return nc



def _xnt3(T):
    return T["ar"].t[:, XNT_O:XNT_O + 32768].rearrange("p (c u) -> p c u", u=1024)


def emit_transposes(P, T, pairs, setA=(0, 1, 2, 3), setB=(4, 5, 6, 7), evac_sname="dv"):
    bankbf = T["bankbf"]
    sb = T["sb"]
    sets = [list(setA), list(setB)]
    P.wait("tensor", evac_sname, P.counts[evac_sname])
    last_ev = [0, 0]
    vev = 0
    i = 0
    rnd = 0
    while i < len(pairs):
        s = rnd % 2
        banks = sets[s]
        group = pairs[i:i + len(banks)]
        if last_ev[s]:
            P.wait("tensor", evac_sname, last_ev[s])
        for j, (srcap, dstap) in enumerate(group):
            P.op("tensor", lambda e, srcap=srcap, bk=banks[j]:
                 e.transpose(bankbf(bk)[:, 0:128], srcap, sb["ident"][:]))
        vtr = P.mark("tensor", "pe")
        P.wait("vector", "pe", vtr)
        for j, (srcap, dstap) in enumerate(group):
            if j < len(group) - 1:
                P.op("vector", lambda e, dstap=dstap, bk=banks[j]:
                     e.tensor_copy(dstap, bankbf(bk)[:, 0:128]))
            else:
                vev = P.op("vector", lambda e, dstap=dstap, bk=banks[j]:
                           e.tensor_copy(dstap, bankbf(bk)[:, 0:128]), evac_sname)
        last_ev[s] = vev
        i += len(banks)
        rnd += 1
    return vev


def emit_ln(P, T, src_name):
    import os
    depth = int(os.environ.get("LN_DEPTH", "4"))
    ar, sb, bank, bankbf = T["ar"], T["sb"], T["bank"], T["bankbf"]
    src = T[src_name]
    st = lambda i: sb["stat"][:, i:i + 1]
    xnt = _xnt3(T)
    rbuf = [ar.view(R0_O, 4096, F32), ar.view(R1_O, 4096, F32)]
    sq = ar.view(KT_O, 4096, F32)
    xbf = ar.view(W0_O, 4096, BF16)

    P.op("vector", lambda e: e.memset(st(7), 1e-5))
    P.op("vector", lambda e: e.drain())
    last_xn_ac = [0, 0]

    for t in range(8):
        bi = t % 2
        if last_xn_ac[bi]:
            P.wait("sync", "ac", last_xn_ac[bi])
        vload = P.dma("sync", rbuf[bi][:], src[t * 128:(t + 1) * 128, :], "sd")
        rt = rbuf[bi]
        P.wait("vector", "sd", vload)
        P.op("vector", lambda e, rt=rt: e.reduce_sum(st(0), rt[:], axis=AX.X))
        P.op("vector", lambda e: e.drain())
        P.op("vector", lambda e: e.tensor_scalar(st(1), st(0), 1.0 / D, None, OP.mult))
        P.op("vector", lambda e: e.drain())
        P.wait("scalar", "sd", vload)
        vsq = P.op("scalar", lambda e, rt=rt: e.activation(sq[:], rt[:], AF.Square, accum_out=st(2)), "ac")
        P.wait("vector", "ac", vsq)
        P.op("vector", lambda e: e.tensor_tensor(st(3), st(1), st(1), OP.mult))
        P.op("vector", lambda e: e.drain())
        vvar = P.op("vector", lambda e: e.scalar_tensor_tensor(st(4), st(2), 1.0 / D, st(3), OP.mult, OP.subtract), "dv")
        P.wait("scalar", "dv", vvar)
        vsqv = P.op("scalar", lambda e: e.activation(st(5), st(4), AF.Sqrt, bias=st(7)), "ac")
        P.wait("vector", "ac", vsqv)
        P.op("vector", lambda e: e.reciprocal(st(6), st(5)))
        P.op("vector", lambda e: e.drain())
        vnb = P.op("vector", lambda e: e.tensor_scalar(st(8), st(1), -1.0, st(6), OP.mult, OP.mult), "dv")
        if depth < 2:
            last_xn_ac[bi] = vsq
            continue
        P.wait("scalar", "dv", vnb)
        P.wait("scalar", "pe", P.counts["pe"])
        vxn = P.op("scalar", lambda e, rt=rt: e.activation(xbf[:], rt[:], AF.Identity, bias=st(8), scale=st(6)), "ac")
        last_xn_ac[bi] = vxn
        if depth < 3:
            continue
        P.wait("tensor", "ac", vxn)
        pairs = [(xbf[:, c * 128:(c + 1) * 128], xnt[:, c, t * 128:t * 128 + 128])
                 for c in range(32)]
        emit_transposes(P, T, pairs)


def emit_l0kvq(P, T):
    ar, sb, bank, bankbf = T["ar"], T["sb"], T["bank"], T["bankbf"]
    xnt = _xnt3(T)
    wbuf = [ar.view(W0_O, 4096, BF16), ar.view(W1_O, 4096, BF16)]
    qtmp = ar.view(Q_O, 4096, BF16)
    vtmp = ar.view(Q_O + 4096, 4096, BF16)
    rows = sb["rows"]
    rows2 = sb["rows2"]
    rows3 = sb["rows3"]
    P.dma("gpsimd", rows[0:1, 0:512], T["kb0"], "gd")
    P.dma("gpsimd", rows2[0:1, 0:512], T["qb0"], "gd")
    P.dma("gpsimd", rows3[0:1, 0:512], T["vb0"], "gd")
    vrows = P.counts["gd"]

    def proj_fm(wd_dram, nchunk, brow, dst_view, dst_off_fn, sname_stage, init_lastuse=(0, 0)):
        stage_dmas = []
        lastuse = list(init_lastuse)
        P.wait("tensor", "ac", P.counts["ac"])
        for kc in range(nchunk):
            wb = kc % 2
            if lastuse[wb]:
                P.wait("gpsimd", "pe", lastuse[wb])
            vw = P.dma("gpsimd", wbuf[wb][:], wd_dram[kc], "gd")
            P.wait("tensor", "gd", vw)
            P.wait("tensor", "gd", vrows)
            for th in range(2):
                bk = (kc * 2 + th) % 4
                P.op("tensor", lambda e, bk=bk, kc=kc, brow=brow, th=th:
                     e.matmul(bank(bk), brow[0:1, kc * 128:kc * 128 + 128],
                              sb["ones"][0:1, th * 512:th * 512 + 512], start=True, stop=False))
                for f in range(32):
                    P.op("tensor", lambda e, bk=bk, f=f, th=th, wb=wb:
                         e.matmul(bank(bk), wbuf[wb][:, f * 128:(f + 1) * 128],
                                  xnt[:, f, th * 512:th * 512 + 512], start=False, stop=(f == 31)))
                vmm = P.mark("tensor", "pe")
                if th == 1:
                    lastuse[wb] = vmm
                P.wait("vector", "pe", vmm)
                vev = P.op("vector", lambda e, bk=bk, kc=kc, th=th, dst_view=dst_view:
                           e.tensor_copy(dst_view[:, kc * 1024 + th * 512: kc * 1024 + th * 512 + 512],
                                         bank(bk)), "ac")
            if sname_stage is not None:
                P.wait("sync", "ac", P.counts["ac"])
                stage_dmas.append(P.dma("sync", T[sname_stage][:, kc * 1024:(kc + 1) * 1024],
                                        dst_view[:, kc * 1024:(kc + 1) * 1024], "sd"))
        return lastuse

    luK = proj_fm(T["wk0"], 4, rows, qtmp, None, "cc0k_in")

    P.wait("tensor", "ac", P.counts["ac"])
    lastw = list(luK)
    for l in range(8):
        wb = l % 2
        if lastw[wb]:
            P.wait("gpsimd", "pe", lastw[wb])
        vw = P.dma("gpsimd", wbuf[wb][:, 0:2048], T["wv0"][l], "gd")
        P.wait("tensor", "gd", vw)
        for t in range(8):
            if l == 0:
                P.op("tensor", lambda e, t=t:
                     e.matmul(bank(t), sb["ones"][0:1, 0:128],
                              rows3[0:1, 0:512], start=True, stop=False))
            for fc in range(4):
                f = l * 4 + fc
                P.op("tensor", lambda e, t=t, f=f, fc=fc, wb=wb:
                     e.matmul(bank(t), xnt[:, f, t * 128:t * 128 + 128],
                              wbuf[wb][:, fc * 512:fc * 512 + 512], start=False, stop=(f == 31)))
        lastw[wb] = P.mark("tensor", "pe")
    vvdone = P.counts["pe"]
    stv = []
    for t in range(8):
        P.wait("scalar", "pe", vvdone)
        P.op("scalar", lambda e, t=t: e.activation(vtmp[:, t * 512:(t + 1) * 512], bank(t), AF.Identity), "ac")
    P.wait("sync", "ac", P.counts["ac"])
    P.dma("sync", T["cc0v_in"][:, :], vtmp[:], "sd")
    vstage = P.counts["sd"]

    P.wait("gpsimd", "sd", vstage)
    vk = P.collective("AllGather", OP.bypass, PAIR_GROUPS, T["cc0k_in"].opt(), T["cc0k_out"].opt(), "cck")
    vv = P.collective("AllGather", OP.bypass, PAIR_GROUPS, T["cc0v_in"].opt(), T["cc0v_out"].opt(), "ccv")

    import os as _os
    if not _os.environ.get("KVQ_NOQ"):
        P.wait("vector", "sd", vstage)
        qdst = ar.view(Q_O, 4096, BF16)
        proj_fm(T["wq0"], 4, rows2, qdst, None, None, init_lastuse=lastw)

    P.wait("sync", "pe", P.counts["pe"])
    P.wait("sync", "cck", vk)
    kvk = ar.t[:, KT_O:KT_O + 8192].rearrange("p (kc u) -> p kc u", u=2048)
    for s in range(2):
        P.dma("sync", kvk[:, :, 1024 * s:1024 * s + 1024],
              T["cc0k_out"][128 * s:128 * s + 128, :].rearrange("p (kc u) -> p kc u", u=1024), "sd")
    P.wait("sync", "ccv", vv)
    for s in range(2):
        P.dma("sync", ar.t[:, V_O + 4096 * s:V_O + 4096 * s + 4096],
              T["cc0v_out"][128 * s:128 * s + 128, :], "sd")
    vready = P.counts["sd"]
    P.wait("tensor", "sd", vready)
    P.wait("vector", "sd", vready)
    P.wait("scalar", "sd", vready)


def emit_l0attn(P, T):
    ar, sb, bank, bankbf = T["ar"], T["sb"], T["bank"], T["bankbf"]
    st = lambda i: sb["stat"][:, i:i + 1]
    q3 = ar.t[:, Q_O:Q_O + 4096].rearrange("p (hc u) -> p hc u", u=1024)
    k3 = ar.t[:, KT_O:KT_O + 8192].rearrange("p (kc u) -> p kc u", u=2048)
    v3 = ar.t[:, V_O:V_O + 8192].rearrange("p (kt u) -> p kt u", u=512)
    pbf = ar.view(P_O, 2048, BF16)
    pt = ar.view(PT_O, 2048, BF16)
    att3 = ar.t[:, ATT_O:ATT_O + 4096].rearrange("p (hc u) -> p hc u", u=1024)
    ISQ = 1.0 / np.sqrt(512.0)

    last_attn_ev = 0
    for t in range(8):
        for h in range(8):
            hc, ho = h // 2, 64 * (h % 2)
            if last_attn_ev:
                P.wait("tensor", "dv", last_attn_ev)
            for kb in range(4):
                P.op("tensor", lambda e, kb=kb, hc=hc, ho=ho, t=t:
                     e.matmul(T["psA"][:, kb * 512:kb * 512 + 512],
                              q3[ho:ho + 64, hc, t * 128:t * 128 + 128],
                              k3[ho:ho + 64, hc, kb * 512:kb * 512 + 512], start=True, stop=True))
            vs = P.mark("tensor", "pe")
            P.wait("vector", "pe", vs)
            vmax = P.op("vector", lambda e: e.reduce_max(st(0), T["psA"][:], axis=AX.X, negate=True), "dv")
            P.wait("scalar", "dv", vmax)
            vexp = P.op("scalar", lambda e: e.activation(pbf[:], T["psA"][:], AF.Exp,
                                                         bias=st(0), accum_out=st(1)), "ac")
            P.wait("vector", "ac", vexp)
            P.op("vector", lambda e: e.reciprocal(st(2), st(1)))
            P.op("vector", lambda e: e.drain())
            vscale = P.op("vector", lambda e: e.tensor_scalar(pbf[:], pbf[:], st(2), ISQ, OP.mult, OP.mult), "dv")
            P.wait("tensor", "dv", vscale)
            pairs = [(pbf[:, c * 128:c * 128 + 128], pt[:, c * 128:c * 128 + 128])
                     for c in range(16)]
            emit_transposes(P, T, pairs)
            P.wait("tensor", "dv", P.counts["dv"])
            for kt in range(16):
                P.op("tensor", lambda e, kt=kt, ho=ho, h=h:
                     e.matmul(T["psA"][ho:ho + 64, 0:128],
                              v3[:, kt, h * 64:h * 64 + 64],
                              pt[:, kt * 128:kt * 128 + 128], start=(kt == 0), stop=(kt == 15)))
            vav = P.mark("tensor", "pe")
            P.wait("vector", "pe", vav)
            last_attn_ev = P.op("vector", lambda e, ho=ho, hc=hc, t=t:
                                e.tensor_copy(att3[ho:ho + 64, hc, t * 128:t * 128 + 128],
                                              T["psA"][ho:ho + 64, 0:128]), "dv")


def emit_bigffn(P, T, w_dram, nloads, chunks_per_load, lhs3, nchunk, brow_bf, resid_pair,
                evac_fn=None, wtile_w=2048):
    ar, sb, bank = T["ar"], T["sb"], T["bank"]
    rin, rout = resid_pair
    wbuf = [ar.view(W0_O, wtile_w, BF16), ar.view(W1_O, wtile_w, BF16)]
    rload = [ar.view(R0_O, 1024, F32), ar.view(R0_O + 2048, 1024, F32)]
    rst = [ar.view(R1_O, 1024, F32), ar.view(R1_O + 2048, 1024, F32)]
    lastw = [0, 0]
    lastr = [0, 0]
    lastst = [0, 0]

    for ob in range(8):
        for l in range(nloads):
            wb = l % 2
            if lastw[wb]:
                P.wait("gpsimd", "pe", lastw[wb])
            vw = P.dma("gpsimd", wbuf[wb][:, 0:wtile_w], w_dram[ob, l], "gd")
            P.wait("tensor", "gd", vw)
            for t in range(8):
                if l == 0 and brow_bf is not None:
                    P.op("tensor", lambda e, t=t, ob=ob, brow_bf=brow_bf:
                         e.matmul(bank(t), sb["ones"][0:1, 0:128],
                                  brow_bf[0:1, ob * 512:ob * 512 + 512], start=True, stop=False))
                for fc in range(chunks_per_load):
                    k = l * chunks_per_load + fc
                    st_flag = (l == 0 and fc == 0 and brow_bf is None)
                    sp_flag = (k == nchunk - 1)
                    P.op("tensor", lambda e, t=t, k=k, fc=fc, wb=wb, st_flag=st_flag, sp_flag=sp_flag:
                         e.matmul(bank(t), lhs3[:, k, t * 128:t * 128 + 128],
                                  wbuf[wb][:, fc * 512:fc * 512 + 512],
                                  start=st_flag, stop=sp_flag))
            lastw[wb] = P.mark("tensor", "pe")
        vmm = lastw[(nloads - 1) % 2]
        for t in range(8):
            sl = t % 2
            if lastr[sl]:
                P.wait("sync", "dv", lastr[sl])
            if lastst[sl]:
                P.wait("sync", "sd", lastst[sl])
            vr = P.dma("sync", rload[sl][:, 0:512], rin[t * 128:t * 128 + 128, ob * 512:ob * 512 + 512], "sd")
            P.wait("vector", "pe", vmm)
            P.wait("vector", "sd", vr)
            if lastst[sl]:
                P.wait("vector", "sd", lastst[sl])
            if evac_fn is None:
                vd = P.op("vector", lambda e, t=t, sl=sl:
                          e.tensor_tensor(rst[sl][:, 0:512], bank(t), rload[sl][:, 0:512], OP.add), "dv")
            else:
                vd = evac_fn(P, T, t, ob, rload[sl][:, 0:512], rst[sl][:, 0:512])
            lastr[sl] = vd
            P.wait("sync", "dv", vd)
            lastst[sl] = P.dma("sync", rout[t * 128:t * 128 + 128, ob * 512:ob * 512 + 512],
                               rst[sl][:, 0:512], "sd")
        P.wait("tensor", "dv", P.counts["dv"])


def emit_l0oproj(P, T, resid_pair):
    ar, sb = T["ar"], T["sb"]
    P.dma("gpsimd", sb["rows3"][0:1, 0:4096], T["ob0"], "gd")
    att3 = T["ar"].t[:, ATT_O:ATT_O + 4096].rearrange("p (hc u) -> p hc u", u=1024)
    emit_bigffn(P, T, T["wo0"], 1, 4, att3, 4, sb["rows3"], resid_pair)


def emit_l0sel(P, T):
    ar, sb, bank, bankbf = T["ar"], T["sb"], T["bank"], T["bankbf"]
    st = lambda i: sb["stat"][:, i:i + 1]
    xnt = _xnt3(T)
    centbf = ar.view(W0_O, 2048, BF16)
    hitsbf = ar.view(W1_O, 64, BF16)
    onescol = ar.view(W1_O + 1024, 1, BF16)
    E = ar.view(KT_O, 64, F32)
    P.dma("gpsimd", centbf[:], T["cent"], "gd")
    vc = P.counts["gd"]
    va = P.dma("sync", sb["arowi"][:], T["assign"], "sd")
    P.wait("vector", "sd", va)
    P.op("vector", lambda e: e.tensor_copy(sb["arow"][:], sb["arowi"][:]))
    P.op("vector", lambda e: e.drain())
    vabf = P.op("vector", lambda e: e.tensor_copy(sb["arbf"][:], sb["arow"][:]), "dv")
    P.op("vector", lambda e: e.memset(onescol[:], 1.0))
    P.op("vector", lambda e: e.drain())
    vones = P.mark("vector", "dv")
    P.wait("tensor", "dv", vabf)
    for jb in range(8):
        P.wait("tensor", "dv", P.counts["dv"])
        P.op("tensor", lambda e, jb=jb:
             e.matmul(bank(1)[0:64, 0:512], sb["ones"][0:1, 0:64],
                      sb["arbf"][0:1, jb * 512:jb * 512 + 512], start=True, stop=True))
        vb = P.mark("tensor", "pe")
        P.wait("vector", "pe", vb)
        P.op("vector", lambda e, jb=jb:
             e.tensor_scalar(sb["onehot"][:, jb * 512:jb * 512 + 512], bank(1)[0:64, :],
                             sb["iota"][:], None, OP.is_equal), "dv")
    voh = P.counts["dv"]
    P.wait("tensor", "gd", vc)
    for t in range(8):
        P.wait("tensor", "ac", P.counts["ac"])
        for f in range(32):
            P.op("tensor", lambda e, t=t, f=f:
                 e.matmul(bank(0)[:, 0:64], xnt[:, f, t * 128:t * 128 + 128],
                          centbf[:, f * 64:f * 64 + 64], start=(f == 0), stop=(f == 31)))
        vd = P.mark("tensor", "pe")
        P.wait("vector", "pe", vd)
        vmax = P.op("vector", lambda e: e.reduce_max(st(0), bank(0)[:, 0:64], axis=AX.X, negate=True), "dv")
        P.op("vector", lambda e: e.drain())
        vm10 = P.op("vector", lambda e: e.tensor_scalar(st(1), st(0), 10.0, None, OP.mult), "dv")
        P.wait("scalar", "dv", vm10)
        vexp = P.op("scalar", lambda e: e.activation(E[:, 0:64], bank(0)[:, 0:64], AF.Exp,
                                                     scale=10.0, bias=st(1), accum_out=st(2)), "ac")
        P.wait("vector", "ac", vexp)
        P.op("vector", lambda e, t=t: e.tensor_scalar(sb["qsel"][:, t:t + 1], st(2), 100.0, None, OP.is_lt))
        P.op("vector", lambda e: e.tensor_scalar(st(3), st(2), 0.01, None, OP.mult))
        P.op("vector", lambda e: e.drain())
        vh = P.op("vector", lambda e: e.tensor_scalar(hitsbf[:], E[:, 0:64], st(3), None, OP.is_gt), "dv")
        P.wait("tensor", "dv", vh)
        P.wait("tensor", "dv", vones)
        P.op("tensor", lambda e, t=t:
             e.matmul(bank(2)[0:64, 0:1], hitsbf[:], onescol[:], start=(t == 0), stop=(t == 7)))
    vcnt = P.mark("tensor", "pe")
    P.wait("vector", "pe", vcnt)
    vc64 = P.op("vector", lambda e: e.tensor_copy(sb["c64a"][0:64, 0:1], bank(2)[0:64, 0:1]), "dv")
    P.wait("sync", "dv", vc64)
    vst = P.dma("sync", T["hits_in"][:, :], sb["c64a"][0:64, 0:1], "sd")
    P.wait("gpsimd", "sd", vst)
    vcc = P.collective("AllReduce", OP.add, ALL_GROUPS, T["hits_in"].opt(), T["hits_out"].opt(), "cchits")
    P.wait("sync", "cchits", vcc)
    vrb = P.dma("sync", sb["c64a"][0:64, 1:2], T["hits_out"][:, :], "sd")
    P.wait("vector", "sd", vrb)
    P.op("vector", lambda e: e.tensor_scalar(sb["c64bf"][:], sb["c64a"][0:64, 1:2], 0.0, None, OP.is_gt))
    P.op("vector", lambda e: e.drain())
    vsel = P.mark("vector", "dv")
    rselbc = ar.view(Q_O, 4096, BF16)
    rrow = sb["rows"]
    P.wait("tensor", "dv", vsel)
    P.wait("tensor", "dv", voh)
    for jb in range(8):
        P.wait("tensor", "dv", P.counts["dv"])
        P.op("tensor", lambda e, jb=jb:
             e.matmul(bank(3)[0:1, 0:512], sb["c64bf"][:], sb["onehot"][:, jb * 512:jb * 512 + 512],
                      start=True, stop=True))
        vr = P.mark("tensor", "pe")
        P.wait("vector", "pe", vr)
        vcp = P.op("vector", lambda e, jb=jb:
                   e.tensor_copy(rrow[0:1, jb * 512:jb * 512 + 512], bank(3)[0:1, 0:512]), "dv")
        P.wait("tensor", "dv", vcp)
        P.op("tensor", lambda e, jb=jb:
             e.matmul(bank(4)[:, 0:512], sb["ones"][0:1, 0:128], rrow[0:1, jb * 512:jb * 512 + 512],
                      start=True, stop=True))
        vbb = P.mark("tensor", "pe")
        P.wait("vector", "pe", vbb)
        P.op("vector", lambda e, jb=jb:
             e.tensor_copy(rselbc[:, jb * 512:jb * 512 + 512], bank(4)[:, 0:512]), "dv")


def emit_l0dense(P, T, resid_pair):
    ar, sb = T["ar"], T["sb"]
    P.dma("gpsimd", sb["rows3"][0:1, 0:4096], T["fb"], "gd")
    xnt = _xnt3(T)
    rselbc = ar.view(Q_O, 4096, BF16)

    def evac(P, T, t, ob, rslice, dst):
        v1 = P.op("vector", lambda e, t=t, ob=ob, dst=dst:
                  e.scalar_tensor_tensor(dst, T["bank"](t), sb["qsel"][:, t:t + 1],
                                         rselbc[:, ob * 512:ob * 512 + 512], OP.mult, OP.mult), "dv")
        v2 = P.op("vector", lambda e, dst=dst, rslice=rslice:
                  e.tensor_tensor(dst, dst, rslice, OP.add), "dv")
        return v2

    emit_bigffn(P, T, T["wf"], 8, 4, xnt, 32, sb["rows3"], resid_pair, evac_fn=evac)


def emit_ln3_stage(P, T):
    ar = T["ar"]
    xnt = _xnt3(T)
    P.wait("sync", "dv", P.counts["dv"])
    for b in range(2):
        P.dma("sync", T["cc1_in"][128 * b:128 * b + 128, :],
              xnt[:, :, 512 * b:512 * b + 512], "sd")
    vs = P.counts["sd"]
    P.wait("gpsimd", "sd", vs)
    T["v_cc1"] = P.collective("AllGather", OP.bypass, PAIR_GROUPS,
                              T["cc1_in"].opt(), T["cc1_out"].opt(), "cc1")


CC1_ROW = {0: 0, 1: 256, 2: 384, 3: 128}


def emit_l1kv(P, T):
    ar, sb, bank = T["ar"], T["sb"], T["bank"]
    P.wait("sync", "cc1", T["v_cc1"])
    xnb = ar.t[:, XNT_O:XNT_O + 16384].rearrange("p (f u) -> p f u", u=512)
    wres = ar.t[:, XNT_O + 16384:XNT_O + 32768]
    k3 = ar.t[:, KT_O:KT_O + 8192].rearrange("p (g u) -> p g u", u=2048)
    v2 = ar.t[:, V_O:V_O + 8192]

    for gk in range(4):
        P.dma("gpsimd", wres[:, gk * 4096:(gk + 1) * 4096], T["wk1"][gk], "gd")
    vwk = P.counts["gd"]
    P.wait("tensor", "gd", vwk)
    lastpe = P.counts["pe"]
    for tb in range(4):
        P.wait("sync", "pe", lastpe)
        vx = P.dma("sync", ar.t[:, XNT_O:XNT_O + 16384],
                   T["cc1_out"][CC1_ROW[tb]:CC1_ROW[tb] + 128, :], "sd")
        P.wait("tensor", "sd", vx)
        for gk in range(4):
            for f in range(32):
                P.op("tensor", lambda e, gk=gk, f=f:
                     e.matmul(bank(gk), wres[:, gk * 4096 + f * 128: gk * 4096 + (f + 1) * 128],
                              xnb[:, f, :], start=(f == 0), stop=(f == 31)))
        lastpe = P.mark("tensor", "pe")
        P.wait("scalar", "pe", lastpe)
        for gk in range(4):
            P.op("scalar", lambda e, gk=gk, tb=tb:
                 e.activation(k3[:, gk, tb * 512:tb * 512 + 512], bank(gk), AF.Identity), "ac")
        P.wait("tensor", "ac", P.counts["ac"])
    P.wait("gpsimd", "ac", P.counts["ac"])
    P.wait("gpsimd", "pe", lastpe)
    for l in range(8):
        P.dma("gpsimd", wres[:, l * 2048:(l + 1) * 2048], T["wv1"][l], "gd")
    vwv = P.counts["gd"]
    P.wait("tensor", "gd", vwv)
    for tb in range(4):
        P.wait("sync", "pe", lastpe)
        vx = P.dma("sync", ar.t[:, XNT_O:XNT_O + 16384],
                   T["cc1_out"][CC1_ROW[tb]:CC1_ROW[tb] + 128, :], "sd")
        P.wait("tensor", "sd", vx)
        for tt in range(4):
            for f in range(32):
                P.op("tensor", lambda e, tt=tt, f=f:
                     e.matmul(bank(4 + tt), xnb[:, f, tt * 128:tt * 128 + 128],
                              wres[:, f * 512:(f + 1) * 512], start=(f == 0), stop=(f == 31)))
        lastpe = P.mark("tensor", "pe")
        P.wait("scalar", "pe", lastpe)
        for tt in range(4):
            kt = tb * 4 + tt
            P.op("scalar", lambda e, tt=tt, kt=kt:
                 e.activation(v2[:, kt * 512:(kt + 1) * 512], bank(4 + tt), AF.Identity), "ac")
        P.wait("tensor", "ac", P.counts["ac"])


def _mJ(J):
    if J < 4:
        return J * 128
    if J >= 12:
        return 512 + (J - 12) * 128
    if J < 8:
        return 1024 + (J - 4) * 128
    return 1536 + (J - 8) * 128


def emit_l1groups(P, T):
    ar, sb, bank, bankbf = T["ar"], T["sb"], T["bank"], T["bankbf"]
    st = lambda i: sb["stat"][:, i:i + 1]
    xnb = ar.t[:, XNT_O:XNT_O + 16384].rearrange("p (f u) -> p f u", u=512)
    wres = ar.t[:, XNT_O + 16384:XNT_O + 32768]
    q3 = ar.t[:, Q_O:Q_O + 8192].rearrange("p (cc u) -> p cc u", u=2048)
    k3 = ar.t[:, KT_O:KT_O + 8192].rearrange("p (g u) -> p g u", u=2048)
    v3 = ar.t[:, V_O:V_O + 8192].rearrange("p (kt u) -> p kt u", u=512)
    att3 = ar.t[:, ATT_O:ATT_O + 8192].rearrange("p (hh u) -> p hh u", u=2048)
    pbf = ar.view(P_O, 2048, BF16)
    pt = ar.view(PT_O, 2048, BF16)
    CS = 1.0 / np.sqrt(128.0)
    T["v_cc2"] = {}
    vcm = P.dma("sync", sb["cmask"][:], T["cmask_d"], "sd")
    P.wait("vector", "sd", vcm)

    for g in range(4):
        P.wait("gpsimd", "pe", P.counts["pe"])
        for cc in range(4):
            P.dma("gpsimd", wres[:, cc * 4096:(cc + 1) * 4096], T["wq1"][g * 4 + cc], "gd")
        vw = P.counts["gd"]
        P.wait("tensor", "gd", vw)
        lastpe = P.counts["pe"]
        for tb in range(4):
            P.wait("sync", "pe", lastpe)
            vx = P.dma("sync", ar.t[:, XNT_O:XNT_O + 16384],
                       T["cc1_out"][CC1_ROW[tb]:CC1_ROW[tb] + 128, :], "sd")
            P.wait("tensor", "sd", vx)
            for cc in range(4):
                for f in range(32):
                    P.op("tensor", lambda e, cc=cc, f=f:
                         e.matmul(bank(cc), wres[:, cc * 4096 + f * 128: cc * 4096 + (f + 1) * 128],
                                  xnb[:, f, :], start=(f == 0), stop=(f == 31)))
            lastpe = P.mark("tensor", "pe")
            P.wait("scalar", "pe", lastpe)
            for cc in range(4):
                P.op("scalar", lambda e, cc=cc, tb=tb:
                     e.activation(q3[:, cc, tb * 512:tb * 512 + 512], bank(cc), AF.Identity), "ac")
            P.wait("tensor", "ac", P.counts["ac"])
        last_attn_ev = 0
        for hh in range(4):
            for J in range(16):
                nkb = J // 4 + 1
                if last_attn_ev:
                    P.wait("tensor", "dv", last_attn_ev)
                for kb in range(nkb):
                    P.op("tensor", lambda e, kb=kb, hh=hh, J=J, g=g:
                         e.matmul(T["psA"][:, kb * 512:kb * 512 + 512],
                                  q3[:, hh, J * 128:J * 128 + 128],
                                  k3[:, g, kb * 512:kb * 512 + 512], start=True, stop=True))
                vs = P.mark("tensor", "pe")
                dk, r = J // 4, J % 4
                P.wait("vector", "pe", vs)
                P.op("vector", lambda e, dk=dk, r=r:
                     e.tensor_tensor(T["psA"][:, dk * 512:dk * 512 + 512],
                                     T["psA"][:, dk * 512:dk * 512 + 512],
                                     sb["cmask"][:, r * 512:r * 512 + 512], OP.add))
                P.op("vector", lambda e: e.drain())
                vmax = P.op("vector", lambda e, nkb=nkb:
                            e.reduce_max(st(0), T["psA"][:, 0:nkb * 512], axis=AX.X, negate=True), "dv")
                P.op("vector", lambda e: e.drain())
                vsc = P.op("vector", lambda e: e.tensor_scalar(st(1), st(0), CS, None, OP.mult), "dv")
                P.wait("scalar", "dv", vsc)
                vexp = P.op("scalar", lambda e, nkb=nkb:
                            e.activation(pbf[:, 0:nkb * 512], T["psA"][:, 0:nkb * 512], AF.Exp,
                                         scale=CS, bias=st(1), accum_out=st(2)), "ac")
                P.wait("vector", "ac", vexp)
                P.op("vector", lambda e: e.reciprocal(st(3), st(2)))
                P.op("vector", lambda e: e.drain())
                vscale = P.op("vector", lambda e, nkb=nkb:
                              e.tensor_scalar(pbf[:, 0:nkb * 512], pbf[:, 0:nkb * 512],
                                              st(3), None, OP.mult), "dv")
                P.wait("tensor", "dv", vscale)
                nkt = nkb * 4
                pairs = [(pbf[:, c * 128:c * 128 + 128], pt[:, c * 128:c * 128 + 128])
                         for c in range(nkt)]
                emit_transposes(P, T, pairs)
                P.wait("tensor", "dv", P.counts["dv"])
                for kt in range(nkt):
                    P.op("tensor", lambda e, kt=kt, g=g, nkt=nkt:
                         e.matmul(T["psB"][:, 1024:1152], v3[:, kt, g * 128:g * 128 + 128],
                                  pt[:, kt * 128:kt * 128 + 128], start=(kt == 0), stop=(kt == nkt - 1)))
                vav = P.mark("tensor", "pe")
                P.wait("vector", "pe", vav)
                last_attn_ev = P.op("vector", lambda e, hh=hh, J=J:
                                    e.tensor_copy(att3[:, hh, _mJ(J):_mJ(J) + 128],
                                                  T["psB"][:, 1024:1152]), "dv")
        P.wait("sync", "dv", last_attn_ev)
        vsg = P.dma("sync", T[f"cc2_in{g}"][:, :], ar.t[:, ATT_O:ATT_O + 8192], "sd")
        P.wait("gpsimd", "sd", vsg)
        T["v_cc2"][g] = P.collective("AllGather", OP.bypass, PAIR_GROUPS,
                                     T[f"cc2_in{g}"].opt(), T[f"cc2_out{g}"].opt(), f"cc2_{g}")


def emit_l1oproj(P, T, resid_pair):
    ar, sb = T["ar"], T["sb"]
    bl = sb["blend"]
    bb = [ar.view(W1_O, 1024, BF16), ar.view(W1_O + 1024, 1024, BF16)]
    tmp = ar.view(W1_O + 2048, 1024, BF16)
    own3 = ar.t[:, XNT_O:XNT_O + 32768].rearrange("p (lc u) -> p lc u", u=1024)
    for g in range(4):
        P.wait("sync", f"cc2_{g}", T["v_cc2"][g])
    lastbl = 0
    for lc in range(32):
        mine = lc < 16
        g, cc = (lc % 16) // 4, (lc % 16) % 4
        s0, s1 = (0, 1) if mine else (1, 0)
        if lastbl:
            P.wait("sync", "dv", lastbl)
        v0 = P.dma("sync", bb[0][:], T[f"cc2_out{g}"][128 * s0:128 * s0 + 128,
                                                      cc * 2048:cc * 2048 + 1024], "sd")
        v1 = P.dma("sync", bb[1][:], T[f"cc2_out{g}"][128 * s1:128 * s1 + 128,
                                                      cc * 2048 + 1024:cc * 2048 + 2048], "sd")
        P.wait("vector", "sd", v1)
        P.op("vector", lambda e: e.tensor_scalar(tmp[:], bb[0][:], bl[:, 0:1], None, OP.mult))
        P.op("vector", lambda e: e.drain())
        lastbl = P.op("vector", lambda e, lc=lc:
                      e.scalar_tensor_tensor(own3[:, lc, :], bb[1][:], bl[:, 1:2], tmp[:],
                                             OP.mult, OP.add), "dv")
    P.wait("tensor", "dv", lastbl)
    P.wait("gpsimd", "dv", lastbl)
    emit_bigffn(P, T, T["wo1"], 8, 4, own3, 32, None, resid_pair)


def emit_l1ffnA(P, T):
    ar, sb, bank = T["ar"], T["sb"], T["bank"]
    xnt = _xnt3(T)
    wbuf = [ar.view(W0_O, 4096, BF16), ar.view(W1_O, 4096, BF16)]
    sil = ar.view(P_O, 1024, F32)
    mbf = ar.view(ATT_O, 1024, BF16)
    P.dma("gpsimd", sb["rows"][0:1, 0:11008], T["bg"], "gd")
    P.dma("gpsimd", sb["rows2"][0:1, 0:11008], T["bu"], "gd")
    vrows = P.counts["gd"]
    last_use = [0, 0]
    lastm = 0
    for ib in range(86):
        A = ib % 2
        B = 1 - A
        if last_use[A]:
            P.wait("gpsimd", "pe", last_use[A])
        vg = P.dma("gpsimd", wbuf[A][:], T["wg"][ib], "gd")
        P.wait("tensor", "gd", vg)
        if ib == 0:
            P.wait("tensor", "gd", vrows)
        for th in range(2):
            P.op("tensor", lambda e, ib=ib, th=th:
                 e.matmul(T["psA"][:, th * 512:th * 512 + 512], sb["rows"][0:1, ib * 128:ib * 128 + 128],
                          sb["ones"][0:1, th * 512:th * 512 + 512], start=True, stop=False))
            for f in range(32):
                P.op("tensor", lambda e, f=f, th=th, A=A:
                     e.matmul(T["psA"][:, th * 512:th * 512 + 512], wbuf[A][:, f * 128:f * 128 + 128],
                              xnt[:, f, th * 512:th * 512 + 512], start=False, stop=(f == 31)))
        vgdone = P.mark("tensor", "pe")
        last_use[A] = vgdone
        if last_use[B]:
            P.wait("gpsimd", "pe", last_use[B])
        vu = P.dma("gpsimd", wbuf[B][:], T["wu"][ib], "gd")
        P.wait("tensor", "gd", vu)
        for th in range(2):
            P.op("tensor", lambda e, ib=ib, th=th:
                 e.matmul(T["psA"][:, 1024 + th * 512:1024 + th * 512 + 512],
                          sb["rows2"][0:1, ib * 128:ib * 128 + 128],
                          sb["ones"][0:1, th * 512:th * 512 + 512], start=True, stop=False))
            for f in range(32):
                P.op("tensor", lambda e, f=f, th=th, B=B:
                     e.matmul(T["psA"][:, 1024 + th * 512:1024 + th * 512 + 512],
                              wbuf[B][:, f * 128:f * 128 + 128],
                              xnt[:, f, th * 512:th * 512 + 512], start=False, stop=(f == 31)))
        vudone = P.mark("tensor", "pe")
        last_use[B] = vudone
        P.wait("scalar", "pe", vgdone)
        vsil = P.op("scalar", lambda e: e.activation(sil[:], T["psA"][:, 0:1024], AF.Silu), "ac")
        P.wait("vector", "ac", vsil)
        P.wait("vector", "pe", vudone)
        if lastm:
            P.wait("vector", "sd", lastm)
        vm = P.op("vector", lambda e: e.tensor_tensor(mbf[:], sil[:], T["psA"][:, 1024:2048], OP.mult), "dv")
        P.wait("sync", "dv", vm)
        lastm = P.dma("sync", T["m_d"][ib * 128:(ib + 1) * 128, :], mbf[:], "sd")
        P.wait("tensor", "ac", P.counts["ac"])
        P.wait("tensor", "dv", P.counts["dv"])


def emit_l1ffnB(P, T, resid_pair):
    ar, sb, bank = T["ar"], T["sb"], T["bank"]
    rin, rout = resid_pair
    P.dma("gpsimd", sb["rows3"][0:1, 0:4096], T["bd"], "gd")
    vrows = P.counts["gd"]
    mb = [ar.t[:, ATT_O:ATT_O + 1024], ar.t[:, ATT_O + 1024:ATT_O + 2048]]
    wb_ = [ar.view(W0_O, 512, BF16), ar.view(W1_O, 512, BF16)]
    rload = [ar.view(R0_O, 1024, F32), ar.view(R0_O + 2048, 1024, F32)]
    rst = [ar.view(R1_O, 1024, F32), ar.view(R1_O + 2048, 1024, F32)]
    lastm = [0, 0]
    lastw = [0, 0]
    lastr = [0, 0]
    lastst = [0, 0]
    for ob in range(8):
        for ic in range(86):
            sl = ic % 2
            if lastm[sl]:
                P.wait("sync", "pe", lastm[sl])
            vm = P.dma("sync", mb[sl], T["m_d"][ic * 128:(ic + 1) * 128, :], "sd")
            if lastw[sl]:
                P.wait("gpsimd", "pe", lastw[sl])
            vw = P.dma("gpsimd", wb_[sl][:], T["wd"][ob, ic], "gd")
            P.wait("tensor", "sd", vm)
            P.wait("tensor", "gd", vw)
            if ic == 0:
                P.wait("tensor", "gd", vrows)
                for t in range(8):
                    P.op("tensor", lambda e, t=t, ob=ob:
                         e.matmul(bank(t), sb["ones"][0:1, 0:128],
                                  sb["rows3"][0:1, ob * 512:ob * 512 + 512], start=True, stop=False))
            for t in range(8):
                P.op("tensor", lambda e, t=t, ic=ic, sl=sl:
                     e.matmul(bank(t), mb[sl][:, t * 128:t * 128 + 128], wb_[sl][:],
                              start=False, stop=(ic == 85)))
            vpe = P.mark("tensor", "pe")
            lastm[sl] = vpe
            lastw[sl] = vpe
        vmm = P.counts["pe"]
        for t in range(8):
            sl = t % 2
            if lastr[sl]:
                P.wait("sync", "dv", lastr[sl])
            vr = P.dma("sync", rload[sl][:, 0:512], rin[t * 128:t * 128 + 128, ob * 512:ob * 512 + 512], "sd")
            P.wait("vector", "pe", vmm)
            P.wait("vector", "sd", vr)
            if lastst[sl]:
                P.wait("vector", "sd", lastst[sl])
            vd = P.op("vector", lambda e, t=t, sl=sl:
                      e.tensor_tensor(rst[sl][:, 0:512], bank(t), rload[sl][:, 0:512], OP.add), "dv")
            lastr[sl] = vd
            P.wait("sync", "dv", vd)
            lastst[sl] = P.dma("sync", rout[t * 128:t * 128 + 128, ob * 512:ob * 512 + 512],
                               rst[sl][:, 0:512], "sd")
        P.wait("tensor", "dv", P.counts["dv"])


def emit_phase(P, T, ph, rpair, fused):
    if ph in ("ln1", "ln2", "ln3", "ln4"):
        emit_ln(P, T, rpair[0])
        if ph == "ln3":
            emit_ln3_stage(P, T)
    elif ph == "l0kvq":
        emit_l0kvq(P, T)
    elif ph == "l0attn":
        emit_l0attn(P, T)
    elif ph == "l0oproj":
        emit_l0oproj(P, T, (T[rpair[0]], T[rpair[1]]))
    elif ph == "l0sel":
        emit_l0sel(P, T)
    elif ph == "l0dense":
        emit_l0dense(P, T, (T[rpair[0]], T[rpair[1]]))
    elif ph == "l1kv":
        emit_l1kv(P, T)
    elif ph == "l1groups":
        emit_l1groups(P, T)
    elif ph == "l1oproj":
        emit_l1oproj(P, T, (T[rpair[0]], T[rpair[1]]))
    elif ph == "l1ffnA":
        emit_l1ffnA(P, T)
    elif ph == "l1ffnB":
        emit_l1ffnB(P, T, (T[rpair[0]], T[rpair[1]]))
    elif ph == "dump":
        import os
        off = int(os.environ.get("DUMP_OFF", str(XNT_O)))
        ln_ = int(os.environ.get("DUMP_LEN", "32768"))
        ng = ln_ // 4096
        out3 = T["rout"].rearrange("(g p) d -> p g d", p=128)
        src3 = T["ar"].t[:, off:off + ng * 4096].rearrange("p (g d) -> p g d", d=4096)
        P.dma("gpsimd", out3[:, 0:ng, :], src3, "gd")
        P.wait("gpsimd", "gd", P.counts["gd"])
    elif ph == "dumpR":
        out3 = T["rout"].rearrange("(g p) d -> p g d", p=128)
        src2 = T["ar"].t[:, R0_O:R0_O + 8192].bitcast(F32)
        P.dma("sync", T["rout"][0:128, :], src2, "sd")
        P.wait("sync", "sd", P.counts["sd"])
    else:
        raise ValueError(ph)



def _regroup(a, g):
    n, p, w = a.shape
    return np.ascontiguousarray(a.reshape(n // g, g, p, w).transpose(0, 2, 1, 3).reshape(n // g, p, g * w))


def _host_arrays(inputs):
    sh, cores = prep_inputs(inputs)
    sh["wv0"] = _regroup(sh["wv0"], 4)
    sh["cent"] = np.ascontiguousarray(sh["cent"].transpose(1, 0, 2).reshape(128, 2048))
    sh["wo0"] = _regroup4(sh["wo0"])
    sh["wf"] = _regroup4(sh["wf"])
    for c in cores:
        c["wv1"] = _regroup(c["wv1"], 4)
        c["wo1"] = _regroup4(c["wo1"])
    return sh, cores


def _regroup4(a):
    nob, nk, p, w = a.shape
    return np.ascontiguousarray(a.reshape(nob, nk // 4, 4, p, w).transpose(0, 1, 3, 2, 4)
                                .reshape(nob, nk // 4, p, 4 * w))


LAUNCHES = ["A", "B", "C", "D"]
import os as _os_mod
FUSED = _os_mod.environ.get("MK_UNFUSED", "") == ""


def kernel(**inputs):
    sh, cores = _host_arrays(inputs)
    if FUSED:
        nc = build_launch("ALL")
        innames = set(a.name[:-4] if a.name.endswith("_set") else a.name
                      for a in nc.m.functions[0].allocations
                      if getattr(a, "kind", None) == "ExternalInput")
        innames.discard("partition_id")
        in_maps = []
        for ci in range(NCORES):
            m = {k: v for k, v in sh.items() if k in innames}
            m.update({k: v for k, v in cores[ci].items() if k in innames})
            m["rin"] = cores[ci]["x"]
            in_maps.append(m)
        res = run_bass_kernel_spmd(nc, in_maps, core_ids=list(range(NCORES)))
        return scatter_output([{"out": res.results[ci]["out"]} for ci in range(NCORES)])
    r_cur = [c["x"] for c in cores]
    for li, which in enumerate(LAUNCHES):
        nc = build_launch(which)
        innames = set(a.name[:-4] if a.name.endswith("_set") else a.name
                      for a in nc.m.functions[0].allocations
                      if getattr(a, "kind", None) == "ExternalInput")
        innames.discard("partition_id")
        in_maps = []
        for ci in range(NCORES):
            m = {}
            for k, v in sh.items():
                if k in innames:
                    m[k] = v
            for k, v in cores[ci].items():
                if k in innames:
                    m[k] = v
            m["rin"] = r_cur[ci]
            in_maps.append(m)
        res = run_bass_kernel_spmd(nc, in_maps, core_ids=list(range(NCORES)))
        r_cur = [res.results[ci]["out"] for ci in range(NCORES)]
    return scatter_output([{"out": r} for r in r_cur])


# version 03
# speedup vs baseline: 1.0437x; 1.0437x over previous
import numpy as np
import ml_dtypes
from contextlib import ExitStack
from collections import defaultdict

import concourse.bass as bass
import concourse.mybir as mybir
from concourse.bass_utils import run_bass_kernel_spmd
from concourse.alu_op_type import AluOpType as OP

F32 = mybir.dt.float32
BF16 = mybir.dt.bfloat16
I32 = mybir.dt.int32
AF = mybir.ActivationFunctionType
AX = mybir.AxisListType
BFNP = ml_dtypes.bfloat16

D = 4096
TOK = 1024
NCORES = 8
SEQ = 2048
NEG = -1.0e30


def _tile_lhsT(wT, nchunk_f, nblk_d):
    F, O = wT.shape
    assert F == nchunk_f * 128 and O == nblk_d * 128
    return np.ascontiguousarray(
        wT.reshape(nchunk_f, 128, nblk_d, 128).transpose(2, 1, 0, 3).reshape(nblk_d, 128, nchunk_f * 128))


def _tile_rhs(wT, nchunk_f, width):
    F, O = wT.shape
    assert F == nchunk_f * 128
    return np.ascontiguousarray(wT.reshape(nchunk_f, 128, O))


def _tile_obff(wT, nchunk_k, nob, obw=512):
    K, O = wT.shape
    assert K == nchunk_k * 128 and O == nob * obw
    return np.ascontiguousarray(
        wT.reshape(nchunk_k, 128, nob, obw).transpose(2, 0, 1, 3).reshape(nob, nchunk_k, 128, obw))


def prep_inputs(inp):
    f32 = lambda a: np.ascontiguousarray(np.asarray(a), dtype=np.float32)
    x = f32(inp["L0_attn_x"])

    sh = {}
    Wqk = f32(inp["L0_attn_Wqk_w"]).reshape(8, 64, 2, D)
    Wq = Wqk[:, :, 0, :].reshape(512, D)
    Wk = Wqk[:, :, 1, :].reshape(512, D)
    qkb = f32(inp["L0_attn_Wqk_b"]).reshape(8, 64, 2)
    sh["wq0"] = _tile_lhsT(np.ascontiguousarray(Wq.T), 32, 4)
    sh["wk0"] = _tile_lhsT(np.ascontiguousarray(Wk.T), 32, 4)
    sh["qb0"] = np.ascontiguousarray(qkb[:, :, 0].reshape(1, 512))
    sh["kb0"] = np.ascontiguousarray(qkb[:, :, 1].reshape(1, 512))
    sh["wv0"] = _tile_rhs(np.ascontiguousarray(f32(inp["L0_attn_Wv_w"]).T), 32, 512)
    sh["vb0"] = f32(inp["L0_attn_Wv_b"]).reshape(1, 512)
    sh["wo0"] = _tile_obff(np.ascontiguousarray(f32(inp["L0_attn_Wo_w"]).T), 4, 8)
    sh["ob0"] = f32(inp["L0_attn_Wo_b"]).reshape(1, D)
    sh["wf"] = _tile_obff(np.ascontiguousarray(f32(inp["L0_ffn_weight"]).T), 32, 8)
    sh["fb"] = f32(inp["L0_ffn_bias"]).reshape(1, D)
    sh["cent"] = _tile_rhs(np.ascontiguousarray(f32(inp["L0_ffn_centroids"]).T), 32, 64)
    sh["assign"] = np.ascontiguousarray(np.asarray(inp["L0_ffn_assignments"], dtype=np.int32).reshape(1, D))
    sh["wg"] = _tile_lhsT(np.ascontiguousarray(f32(inp["L1_ffn_w_gate"]).T), 32, 86)
    sh["wu"] = _tile_lhsT(np.ascontiguousarray(f32(inp["L1_ffn_w_up"]).T), 32, 86)
    sh["bg"] = f32(inp["L1_ffn_b_gate"]).reshape(1, 11008)
    sh["bu"] = f32(inp["L1_ffn_b_up"]).reshape(1, 11008)
    sh["wd"] = _tile_obff(np.ascontiguousarray(f32(inp["L1_ffn_w_down"]).T), 86, 8)
    sh["bd"] = f32(inp["L1_ffn_b_down"]).reshape(1, D)
    sh["ident"] = np.eye(128, dtype=np.float32)
    sh["ones"] = np.ones((1, 1024), dtype=np.float32)
    sh["iota64"] = np.arange(64, dtype=np.float32).reshape(64, 1)
    cm = np.zeros((4, 128, 512), dtype=np.float32)
    for r in range(4):
        i = np.arange(128)[:, None]
        j = np.arange(512)[None, :]
        cm[r] = np.where(j <= r * 128 + i, 0.0, NEG)
    sh["cmask"] = np.ascontiguousarray(cm.transpose(1, 0, 2).reshape(128, 2048))

    wq1_full = f32(inp["L1_attn_w_q"])
    wk1_full = f32(inp["L1_attn_w_k"])
    wv1_full = f32(inp["L1_attn_w_v"])
    wo1_full = f32(inp["L1_attn_w_o"])

    cores = []
    for c in range(NCORES):
        b, h = c // 2, c % 2
        d = {}
        if h == 0:
            tok_idx = np.r_[0:512, 1536:2048]
            heads = list(range(16))
            kvh = list(range(4))
        else:
            tok_idx = np.r_[512:1536]
            heads = list(range(16, 32))
            kvh = list(range(4, 8))
        d["x"] = np.ascontiguousarray(x[b][tok_idx])
        wq_m = np.stack([
            _tile_lhsT(np.ascontiguousarray(wq1_full[:, hh * 128:(hh + 1) * 128]), 32, 1)[0]
            for hh in heads])
        d["wq1"] = np.ascontiguousarray(wq_m)
        wk_m = np.stack([
            _tile_lhsT(np.ascontiguousarray(wk1_full[:, g * 128:(g + 1) * 128]), 32, 1)[0]
            for g in kvh])
        d["wk1"] = np.ascontiguousarray(wk_m)
        vcols = np.concatenate([np.arange(g * 128, (g + 1) * 128) for g in kvh])
        d["wv1"] = _tile_rhs(np.ascontiguousarray(wv1_full[:, vcols]), 32, 512)
        order = heads + [hh for hh in range(32) if hh not in heads]
        wo_p = np.concatenate([wo1_full[hh * 128:(hh + 1) * 128] for hh in order], axis=0)
        d["wo1"] = _tile_obff(wo_p, 32, 8)
        bl = np.zeros((128, 2), dtype=np.float32)
        bl[:, 0] = 1.0 if h == 0 else 0.0
        bl[:, 1] = 0.0 if h == 0 else 1.0
        d["blend"] = bl
        cores.append(d)
    return sh, cores


def scatter_output(res_list):
    out = np.zeros((4, 2048, D), dtype=np.float32)
    for c in range(NCORES):
        b, h = c // 2, c % 2
        r = res_list[c]["out"]
        if h == 0:
            out[b, 0:512] = r[0:512]
            out[b, 1536:2048] = r[512:1024]
        else:
            out[b, 512:1536] = r
    return out



ENGINES = ("tensor", "vector", "scalar", "sync", "gpsimd")


class Plan:
    def __init__(self, nc, stack):
        self.nc = nc
        self.stack = stack
        self.streams = {e: [] for e in ENGINES}
        self.sems = {}
        self.counts = defaultdict(int)

    def sem(self, name):
        if name not in self.sems:
            self.sems[name] = self.stack.enter_context(self.nc.semaphore(name))
        return self.sems[name]

    def emit(self, eng, fn):
        self.streams[eng].append([fn, False])

    def wait(self, eng, sname, val):
        s = self.sem(sname)
        self.emit(eng, lambda e, s=s, v=val: e.wait_ge(s, v))

    def dma(self, eng, out_ap, in_ap, sname):
        s = self.sem(sname)
        self.counts[sname] += 16
        v = self.counts[sname]
        self.emit(eng, lambda e, o=out_ap, i=in_ap, s=s: e.dma_start(out=o, in_=i).then_inc(s, 16))
        self.streams[eng][-1][1] = True
        return v

    def op(self, eng, fn, sname=None):
        if sname is None:
            self.emit(eng, fn)
            return None
        s = self.sem(sname)
        self.counts[sname] += 1
        v = self.counts[sname]

        def wrapped(e, fn=fn, s=s):
            fn(e).then_inc(s, 1)
        self.emit(eng, wrapped)
        self.streams[eng][-1][1] = True
        return v

    def collective(self, kind, op, groups, in_ap, out_ap, sname):
        s = self.sem(sname)
        self.counts[sname] += 1
        v = self.counts[sname]

        def fn(e, s=s):
            e.collective_compute(kind, op, replica_groups=groups,
                                 ins=[in_ap], outs=[out_ap]).then_inc(s)
        self.emit("gpsimd", fn)
        self.streams["gpsimd"][-1][1] = True
        return v

    def mark(self, eng, sname):
        s = self.sem(sname)
        self.counts[sname] += 1
        v = self.counts[sname]
        last = self.streams[eng][-1]
        assert not last[1], f"mark({eng}): last instruction already carries an inc"
        oldfn = last[0]

        def wrapped(e, oldfn=oldfn, s=s):
            inst = oldfn(e)
            inst.then_inc(s, 1)
        last[0] = wrapped
        last[1] = True
        return v

    def barrier_mark(self, eng, sname):
        s = self.sem(sname)
        self.counts[sname] += 1
        v = self.counts[sname]
        self.emit(eng, lambda e: e.drain())
        self.emit(eng, lambda e, s=s: e.nop().then_inc(s, 1))
        return v


class Arena:
    def __init__(self, nc, stack, n_bf16):
        self.t = stack.enter_context(nc.sbuf_tensor("arena", [128, n_bf16], BF16))
        self.n = n_bf16

    def view(self, off_bf16, length, dtype):
        if dtype == BF16:
            ap = self.t[:, off_bf16:off_bf16 + length]
            return ap
        elif dtype == F32:
            ap = self.t[:, off_bf16:off_bf16 + 2 * length].bitcast(F32)
            return ap
        raise ValueError(dtype)


PAIR_GROUPS = [[0, 1], [2, 3], [4, 5], [6, 7]]
ALL_GROUPS = [list(range(8))]


XNT_O, XNT_L = 0, 32768
W0_O, W0_L = 32768, 4096
W1_O, W1_L = 36864, 4096
KT_O, KT_L = 40960, 8192
V_O, V_L = 49152, 8192
Q_O, Q_L = 57344, 8192
P_O, P_L = 65536, 2048
PT_O, PT_L = 67584, 2048
ATT_O, ATT_L = 69632, 8192
R0_O = 77824
R1_O = 86016
ARENA_LEN = 94208

ALL_ENG = list(ENGINES)


def build_launch(which):
    nc = bass.Bass("TRN2", target_bir_lowering=False)
    stack = ExitStack()

    def din(name, shape, dt=F32):
        return nc.dram_tensor(name, list(shape), dt, kind="ExternalInput").ap()

    def dout(name, shape, dt=F32):
        return nc.dram_tensor(name, list(shape), dt, kind="ExternalOutput").ap()

    def dint(name, shape, dt=F32):
        return nc.dram_tensor(name, list(shape), dt).ap()

    T = {}
    fused = which == "ALL"
    segs = {"A": ["ln1", "l0kvq", "l0attn", "l0oproj"],
            "B": ["ln2", "l0sel", "l0dense"],
            "C": ["ln3", "l1kv", "l1groups", "l1oproj"],
            "D": ["ln4", "l1ffnA", "l1ffnB"]}
    if isinstance(which, (list, tuple)):
        phases = list(which)
        fused = False
    else:
        phases = segs[which] if not fused else segs["A"] + segs["B"] + segs["C"] + segs["D"]

    need = set(phases)

    if fused:
        T["rin"] = din("rin", (TOK, D))
        T["ra"] = dint("Ra", (TOK, D))
        T["rb"] = dint("Rb", (TOK, D))
        T["rout"] = dout("out", (TOK, D))
        resid = {"ln1": ("rin", None), "l0oproj": ("rin", "ra"),
                 "ln2": ("ra", None), "l0dense": ("ra", "rb"),
                 "ln3": ("rb", None), "l1oproj": ("rb", "ra"),
                 "ln4": ("ra", None), "l1ffnB": ("ra", "rout")}
    else:
        T["rin"] = din("rin", (TOK, D))
        T["rout"] = dout("out", (TOK, D))
        resid = {}
        for p in ["ln1", "ln2", "ln3", "ln4"]:
            resid[p] = ("rin", None)
        for p in ["l0oproj", "l0dense", "l1oproj", "l1ffnB"]:
            resid[p] = ("rin", "rout")

    T["ident_d"] = din("ident", (128, 128))
    T["ones_d"] = din("ones", (1, 1024))

    if "l0kvq" in need:
        T["wq0"] = din("wq0", (4, 128, 4096)); T["wk0"] = din("wk0", (4, 128, 4096))
        T["wv0"] = din("wv0", (8, 128, 2048))
        T["qb0"] = din("qb0", (1, 512)); T["kb0"] = din("kb0", (1, 512)); T["vb0"] = din("vb0", (1, 512))
        T["cc0k_in"] = dint("cc0k_in", (128, 4096), BF16)
        T["cc0k_out"] = dint("cc0k_out", (256, 4096), BF16)
        T["cc0v_in"] = dint("cc0v_in", (128, 4096), BF16)
        T["cc0v_out"] = dint("cc0v_out", (256, 4096), BF16)
    if "l0oproj" in need:
        T["wo0"] = din("wo0", (8, 1, 128, 2048))
        T["ob0"] = din("ob0", (1, D))
    if "l0sel" in need:
        T["cent"] = din("cent", (128, 2048))
        T["assign"] = din("assign", (1, D), I32)
        T["iota64_d"] = din("iota64", (64, 1))
        T["hits_in"] = dint("hits_in", (64, 1))
        T["hits_out"] = dint("hits_out", (64, 1))
    if "l0dense" in need:
        T["wf"] = din("wf", (8, 8, 128, 2048))
        T["fb"] = din("fb", (1, D))
    if "ln3" in need or "l1kv" in need:
        T["cc1_in"] = dint("cc1_in", (256, 16384), BF16)
        T["cc1_out"] = dint("cc1_out", (512, 16384), BF16)
    if "l1kv" in need:
        T["wk1"] = din("wk1", (4, 128, 4096)); T["wv1"] = din("wv1", (8, 128, 2048))
    if "l1groups" in need:
        T["wq1"] = din("wq1", (16, 128, 4096))
        T["cmask_d"] = din("cmask", (128, 2048))
        for g in range(4):
            T[f"cc2_in{g}"] = dint(f"cc2_in{g}", (128, 8192), BF16)
            T[f"cc2_out{g}"] = dint(f"cc2_out{g}", (256, 8192), BF16)
    if "l1oproj" in need:
        T["wo1"] = din("wo1", (8, 8, 128, 2048))
        T["blend_d"] = din("blend", (128, 2))
    if "l1ffnA" in need:
        T["wg"] = din("wg", (86, 128, 4096)); T["wu"] = din("wu", (86, 128, 4096))
        T["bg"] = din("bg", (1, 11008)); T["bu"] = din("bu", (1, 11008))
        T["m_d"] = dint("m_d", (86 * 128, 1024), BF16)
    if "l1ffnB" in need:
        if "m_d" not in T:
            T["m_d"] = dint("m_d", (86 * 128, 1024), BF16)
        T["wd"] = din("wd", (8, 86, 128, 512))
        T["bd"] = din("bd", (1, D))

    with stack:
        ar = Arena(nc, stack, ARENA_LEN)
        e = stack.enter_context
        sb = {}
        sb["ident"] = e(nc.sbuf_tensor("identsb", [128, 128], BF16))
        sb["ones"] = e(nc.sbuf_tensor("onessb", [1, 1024], BF16))
        sb["stat"] = e(nc.sbuf_tensor("stat", [128, 32], F32))
        sb["qsel"] = e(nc.sbuf_tensor("qselsb", [128, 8], F32))
        sb["blend"] = e(nc.sbuf_tensor("blendsb", [128, 2], F32))
        sb["iota"] = e(nc.sbuf_tensor("iotasb", [64, 1], F32))
        sb["c64a"] = e(nc.sbuf_tensor("c64a", [64, 2], F32))
        sb["c64bf"] = e(nc.sbuf_tensor("c64bf", [64, 1], BF16))
        psA = e(nc.psum_tensor("psA", [128, 2048], F32))
        psB = e(nc.psum_tensor("psB", [128, 2048], F32))

        P = Plan(nc, stack)
        sb["rows"] = ar.t[0:1, KT_O:KT_O + 11008]
        sb["rows2"] = ar.t[0:1, KT_O + 11008:KT_O + 22016]
        sb["rows3"] = ar.t[0:1, KT_O + 22016:KT_O + 26112]
        sb["cmask"] = ar.t[:, W0_O:W0_O + 4096].bitcast(F32)
        sb["onehot"] = ar.t[0:64, ATT_O:ATT_O + 4096]
        sb["arowi"] = ar.t[0:1, KT_O + 1024:KT_O + 9216].bitcast(I32)
        sb["arow"] = ar.t[0:1, KT_O + 9216:KT_O + 17408].bitcast(F32)
        sb["arbf"] = ar.t[0:1, V_O + 1024:V_O + 5120]
        T["sb"] = sb
        T["ar"] = ar
        T["psA"] = psA
        T["psB"] = psB

        def bank(i):
            return (psA if i < 4 else psB)[:, (i % 4) * 512:(i % 4) * 512 + 512]

        def bankbf(i):
            return (psA if i < 4 else psB)[:].bitcast(BF16)[:, (i % 4) * 1024:(i % 4) * 1024 + 1024]

        T["bank"] = bank
        T["bankbf"] = bankbf

        P.dma("gpsimd", sb["ident"][:], T["ident_d"], "gd")
        P.dma("gpsimd", sb["ones"][:], T["ones_d"], "gd")
        if "l1oproj" in need:
            P.dma("sync", sb["blend"][:], T["blend_d"], "sd")
        if "l0sel" in need:
            P.dma("sync", sb["iota"][:], T["iota64_d"], "sd")
        pre_gd = P.counts["gd"]
        pre_sd = P.counts["sd"]
        for eng in ("tensor", "vector", "scalar"):
            P.wait(eng, "gd", pre_gd)
            P.wait(eng, "sd", pre_sd)
        P.wait("gpsimd", "sd", pre_sd)
        P.wait("sync", "gd", pre_gd)

        bar_n = [0]

        def barrier():
            bar_n[0] += 1
            P.wait("sync", "sd", P.counts["sd"])
            P.wait("gpsimd", "gd", P.counts["gd"])
            for nm, cnt in list(P.counts.items()):
                if (nm.startswith("mS") or nm.startswith("wS")) and cnt:
                    P.wait("sync" if nm.startswith("mS") else "gpsimd", nm, cnt)
            vals = {}
            for eng in ALL_ENG:
                vals[eng] = P.barrier_mark(eng, "bar_" + eng)
            for eng in ALL_ENG:
                for oth in ALL_ENG:
                    if oth != eng:
                        P.wait(eng, "bar_" + oth, vals[oth])
        T["barrier"] = barrier

        for ph in phases:
            emit_phase(P, T, ph, resid.get(ph), fused)
            barrier()

        P.wait("sync", "sd", P.counts["sd"])
        P.wait("gpsimd", "gd", P.counts["gd"])

        with nc.Block() as block:
            @block.tensor
            def _(t):
                for fn, _ in P.streams["tensor"]:
                    fn(t)

            @block.vector
            def _(v):
                for fn, _ in P.streams["vector"]:
                    fn(v)

            @block.scalar
            def _(s):
                for fn, _ in P.streams["scalar"]:
                    fn(s)

            @block.sync
            def _(s):
                for fn, _ in P.streams["sync"]:
                    fn(s)

            @block.gpsimd
            def _(g):
                for fn, _ in P.streams["gpsimd"]:
                    fn(g)

    return nc



def _xnt3(T):
    return T["ar"].t[:, XNT_O:XNT_O + 32768].rearrange("p (c u) -> p c u", u=1024)


def emit_transposes(P, T, pairs, setA=(0, 1, 2, 3), setB=(4, 5, 6, 7), evac_sname="dv"):
    bankbf = T["bankbf"]
    sb = T["sb"]
    sets = [list(setA), list(setB)]
    P.wait("tensor", evac_sname, P.counts[evac_sname])
    last_ev = [0, 0]
    vev = 0
    i = 0
    rnd = 0
    while i < len(pairs):
        s = rnd % 2
        banks = sets[s]
        group = pairs[i:i + len(banks)]
        if last_ev[s]:
            P.wait("tensor", evac_sname, last_ev[s])
        for j, (srcap, dstap) in enumerate(group):
            P.op("tensor", lambda e, srcap=srcap, bk=banks[j]:
                 e.transpose(bankbf(bk)[:, 0:128], srcap, sb["ident"][:]))
        vtr = P.mark("tensor", "pe")
        P.wait("vector", "pe", vtr)
        for j, (srcap, dstap) in enumerate(group):
            if j < len(group) - 1:
                P.op("vector", lambda e, dstap=dstap, bk=banks[j]:
                     e.tensor_copy(dstap, bankbf(bk)[:, 0:128]))
            else:
                vev = P.op("vector", lambda e, dstap=dstap, bk=banks[j]:
                           e.tensor_copy(dstap, bankbf(bk)[:, 0:128]), evac_sname)
        last_ev[s] = vev
        i += len(banks)
        rnd += 1
    return vev


def emit_ln(P, T, src_name):
    import os
    depth = int(os.environ.get("LN_DEPTH", "4"))
    ar, sb, bank, bankbf = T["ar"], T["sb"], T["bank"], T["bankbf"]
    src = T[src_name]
    st = lambda i: sb["stat"][:, i:i + 1]
    xnt = _xnt3(T)
    rbuf = [ar.view(R0_O, 4096, F32), ar.view(R1_O, 4096, F32)]
    sq = ar.view(KT_O, 4096, F32)
    xbf = ar.view(W0_O, 4096, BF16)

    P.op("vector", lambda e: e.memset(st(7), 1e-5))
    P.op("vector", lambda e: e.drain())
    last_xn_ac = [0, 0]

    for t in range(8):
        bi = t % 2
        if last_xn_ac[bi]:
            P.wait("sync", "ac", last_xn_ac[bi])
        vload = P.dma("sync", rbuf[bi][:], src[t * 128:(t + 1) * 128, :], "sd")
        rt = rbuf[bi]
        P.wait("vector", "sd", vload)
        P.op("vector", lambda e, rt=rt: e.reduce_sum(st(0), rt[:], axis=AX.X))
        P.op("vector", lambda e: e.drain())
        P.op("vector", lambda e: e.tensor_scalar(st(1), st(0), 1.0 / D, None, OP.mult))
        P.op("vector", lambda e: e.drain())
        P.wait("scalar", "sd", vload)
        vsq = P.op("scalar", lambda e, rt=rt: e.activation(sq[:], rt[:], AF.Square, accum_out=st(2)), "ac")
        P.wait("vector", "ac", vsq)
        P.op("vector", lambda e: e.tensor_tensor(st(3), st(1), st(1), OP.mult))
        P.op("vector", lambda e: e.drain())
        vvar = P.op("vector", lambda e: e.scalar_tensor_tensor(st(4), st(2), 1.0 / D, st(3), OP.mult, OP.subtract), "dv")
        P.wait("scalar", "dv", vvar)
        vsqv = P.op("scalar", lambda e: e.activation(st(5), st(4), AF.Sqrt, bias=st(7)), "ac")
        P.wait("vector", "ac", vsqv)
        P.op("vector", lambda e: e.reciprocal(st(6), st(5)))
        P.op("vector", lambda e: e.drain())
        vnb = P.op("vector", lambda e: e.tensor_scalar(st(8), st(1), -1.0, st(6), OP.mult, OP.mult), "dv")
        if depth < 2:
            last_xn_ac[bi] = vsq
            continue
        P.wait("scalar", "dv", vnb)
        P.wait("scalar", "pe", P.counts["pe"])
        vxn = P.op("scalar", lambda e, rt=rt: e.activation(xbf[:], rt[:], AF.Identity, bias=st(8), scale=st(6)), "ac")
        last_xn_ac[bi] = vxn
        if depth < 3:
            continue
        P.wait("tensor", "ac", vxn)
        pairs = [(xbf[:, c * 128:(c + 1) * 128], xnt[:, c, t * 128:t * 128 + 128])
                 for c in range(32)]
        emit_transposes(P, T, pairs)


def emit_l0kvq(P, T):
    ar, sb, bank, bankbf = T["ar"], T["sb"], T["bank"], T["bankbf"]
    xnt = _xnt3(T)
    wbuf = [ar.view(W0_O, 4096, BF16), ar.view(W1_O, 4096, BF16)]
    qtmp = ar.view(Q_O, 4096, BF16)
    vtmp = ar.view(Q_O + 4096, 4096, BF16)
    rows = sb["rows"]
    rows2 = sb["rows2"]
    rows3 = sb["rows3"]
    P.dma("gpsimd", rows[0:1, 0:512], T["kb0"], "gd")
    P.dma("gpsimd", rows2[0:1, 0:512], T["qb0"], "gd")
    P.dma("gpsimd", rows3[0:1, 0:512], T["vb0"], "gd")
    vrows = P.counts["gd"]

    def proj_fm(wd_dram, nchunk, brow, dst_view, dst_off_fn, sname_stage, init_lastuse=(0, 0)):
        stage_dmas = []
        lastuse = list(init_lastuse)
        P.wait("tensor", "ac", P.counts["ac"])
        for kc in range(nchunk):
            wb = kc % 2
            if lastuse[wb]:
                P.wait("gpsimd", "pe", lastuse[wb])
            vw = P.dma("gpsimd", wbuf[wb][:], wd_dram[kc], "gd")
            P.wait("tensor", "gd", vw)
            P.wait("tensor", "gd", vrows)
            for th in range(2):
                bk = (kc * 2 + th) % 4
                P.op("tensor", lambda e, bk=bk, kc=kc, brow=brow, th=th:
                     e.matmul(bank(bk), brow[0:1, kc * 128:kc * 128 + 128],
                              sb["ones"][0:1, th * 512:th * 512 + 512], start=True, stop=False))
                for f in range(32):
                    P.op("tensor", lambda e, bk=bk, f=f, th=th, wb=wb:
                         e.matmul(bank(bk), wbuf[wb][:, f * 128:(f + 1) * 128],
                                  xnt[:, f, th * 512:th * 512 + 512], start=False, stop=(f == 31)))
                vmm = P.mark("tensor", "pe")
                if th == 1:
                    lastuse[wb] = vmm
                P.wait("vector", "pe", vmm)
                vev = P.op("vector", lambda e, bk=bk, kc=kc, th=th, dst_view=dst_view:
                           e.tensor_copy(dst_view[:, kc * 1024 + th * 512: kc * 1024 + th * 512 + 512],
                                         bank(bk)), "ac")
            if sname_stage is not None:
                P.wait("sync", "ac", P.counts["ac"])
                stage_dmas.append(P.dma("sync", T[sname_stage][:, kc * 1024:(kc + 1) * 1024],
                                        dst_view[:, kc * 1024:(kc + 1) * 1024], "sd"))
        return lastuse

    luK = proj_fm(T["wk0"], 4, rows, qtmp, None, "cc0k_in")

    P.wait("tensor", "ac", P.counts["ac"])
    lastw = list(luK)
    for l in range(8):
        wb = l % 2
        if lastw[wb]:
            P.wait("gpsimd", "pe", lastw[wb])
        vw = P.dma("gpsimd", wbuf[wb][:, 0:2048], T["wv0"][l], "gd")
        P.wait("tensor", "gd", vw)
        for t in range(8):
            if l == 0:
                P.op("tensor", lambda e, t=t:
                     e.matmul(bank(t), sb["ones"][0:1, 0:128],
                              rows3[0:1, 0:512], start=True, stop=False))
            for fc in range(4):
                f = l * 4 + fc
                P.op("tensor", lambda e, t=t, f=f, fc=fc, wb=wb:
                     e.matmul(bank(t), xnt[:, f, t * 128:t * 128 + 128],
                              wbuf[wb][:, fc * 512:fc * 512 + 512], start=False, stop=(f == 31)))
        lastw[wb] = P.mark("tensor", "pe")
    vvdone = P.counts["pe"]
    stv = []
    for t in range(8):
        P.wait("scalar", "pe", vvdone)
        P.op("scalar", lambda e, t=t: e.activation(vtmp[:, t * 512:(t + 1) * 512], bank(t), AF.Identity), "ac")
    P.wait("sync", "ac", P.counts["ac"])
    P.dma("sync", T["cc0v_in"][:, :], vtmp[:], "sd")
    vstage = P.counts["sd"]

    P.wait("gpsimd", "sd", vstage)
    vk = P.collective("AllGather", OP.bypass, PAIR_GROUPS, T["cc0k_in"].opt(), T["cc0k_out"].opt(), "cck")
    vv = P.collective("AllGather", OP.bypass, PAIR_GROUPS, T["cc0v_in"].opt(), T["cc0v_out"].opt(), "ccv")

    import os as _os
    if not _os.environ.get("KVQ_NOQ"):
        P.wait("vector", "sd", vstage)
        qdst = ar.view(Q_O, 4096, BF16)
        proj_fm(T["wq0"], 4, rows2, qdst, None, None, init_lastuse=lastw)

    P.wait("sync", "pe", P.counts["pe"])
    P.wait("sync", "cck", vk)
    kvk = ar.t[:, KT_O:KT_O + 8192].rearrange("p (kc u) -> p kc u", u=2048)
    for s in range(2):
        P.dma("sync", kvk[:, :, 1024 * s:1024 * s + 1024],
              T["cc0k_out"][128 * s:128 * s + 128, :].rearrange("p (kc u) -> p kc u", u=1024), "sd")
    P.wait("sync", "ccv", vv)
    for s in range(2):
        P.dma("sync", ar.t[:, V_O + 4096 * s:V_O + 4096 * s + 4096],
              T["cc0v_out"][128 * s:128 * s + 128, :], "sd")
    vready = P.counts["sd"]
    P.wait("tensor", "sd", vready)
    P.wait("vector", "sd", vready)
    P.wait("scalar", "sd", vready)


def emit_l0attn(P, T):
    ar, sb, bank, bankbf = T["ar"], T["sb"], T["bank"], T["bankbf"]
    st = lambda i: sb["stat"][:, i:i + 1]
    q3 = ar.t[:, Q_O:Q_O + 4096].rearrange("p (hc u) -> p hc u", u=1024)
    k3 = ar.t[:, KT_O:KT_O + 8192].rearrange("p (kc u) -> p kc u", u=2048)
    v3 = ar.t[:, V_O:V_O + 8192].rearrange("p (kt u) -> p kt u", u=512)
    pbf = ar.view(P_O, 2048, BF16)
    pt = ar.view(PT_O, 2048, BF16)
    att3 = ar.t[:, ATT_O:ATT_O + 4096].rearrange("p (hc u) -> p hc u", u=1024)
    ISQ = 1.0 / np.sqrt(512.0)

    last_attn_ev = 0
    for t in range(8):
        for h in range(8):
            hc, ho = h // 2, 64 * (h % 2)
            if last_attn_ev:
                P.wait("tensor", "dv", last_attn_ev)
            for kb in range(4):
                P.op("tensor", lambda e, kb=kb, hc=hc, ho=ho, t=t:
                     e.matmul(T["psA"][:, kb * 512:kb * 512 + 512],
                              q3[ho:ho + 64, hc, t * 128:t * 128 + 128],
                              k3[ho:ho + 64, hc, kb * 512:kb * 512 + 512], start=True, stop=True))
            vs = P.mark("tensor", "pe")
            P.wait("vector", "pe", vs)
            vmax = P.op("vector", lambda e: e.reduce_max(st(0), T["psA"][:], axis=AX.X, negate=True), "dv")
            P.wait("scalar", "dv", vmax)
            vexp = P.op("scalar", lambda e: e.activation(pbf[:], T["psA"][:], AF.Exp,
                                                         bias=st(0), accum_out=st(1)), "ac")
            P.wait("vector", "ac", vexp)
            P.op("vector", lambda e: e.reciprocal(st(2), st(1)))
            P.op("vector", lambda e: e.drain())
            vscale = P.op("vector", lambda e: e.tensor_scalar(pbf[:], pbf[:], st(2), ISQ, OP.mult, OP.mult), "dv")
            P.wait("tensor", "dv", vscale)
            pairs = [(pbf[:, c * 128:c * 128 + 128], pt[:, c * 128:c * 128 + 128])
                     for c in range(16)]
            emit_transposes(P, T, pairs)
            P.wait("tensor", "dv", P.counts["dv"])
            for kt in range(16):
                P.op("tensor", lambda e, kt=kt, ho=ho, h=h:
                     e.matmul(T["psA"][ho:ho + 64, 0:128],
                              v3[:, kt, h * 64:h * 64 + 64],
                              pt[:, kt * 128:kt * 128 + 128], start=(kt == 0), stop=(kt == 15)))
            vav = P.mark("tensor", "pe")
            P.wait("vector", "pe", vav)
            last_attn_ev = P.op("vector", lambda e, ho=ho, hc=hc, t=t:
                                e.tensor_copy(att3[ho:ho + 64, hc, t * 128:t * 128 + 128],
                                              T["psA"][ho:ho + 64, 0:128]), "dv")


def emit_bigffn(P, T, w_dram, nloads, chunks_per_load, lhs3, nchunk, brow_bf, resid_pair,
                evac_fn=None, wtile_w=2048):
    ar, sb, bank = T["ar"], T["sb"], T["bank"]
    rin, rout = resid_pair
    wbuf = [ar.view(W0_O, wtile_w, BF16), ar.view(W1_O, wtile_w, BF16)]
    rload = [ar.view(R0_O, 1024, F32), ar.view(R0_O + 2048, 1024, F32)]
    rst = [ar.view(R1_O, 1024, F32), ar.view(R1_O + 2048, 1024, F32)]
    lastw = [0, 0]
    lastr = [0, 0]
    lastst = [0, 0]

    for ob in range(8):
        for l in range(nloads):
            wb = l % 2
            if lastw[wb]:
                P.wait("gpsimd", "pe", lastw[wb])
            vw = P.dma("gpsimd", wbuf[wb][:, 0:wtile_w], w_dram[ob, l], "gd")
            P.wait("tensor", "gd", vw)
            for t in range(8):
                if l == 0 and brow_bf is not None:
                    P.op("tensor", lambda e, t=t, ob=ob, brow_bf=brow_bf:
                         e.matmul(bank(t), sb["ones"][0:1, 0:128],
                                  brow_bf[0:1, ob * 512:ob * 512 + 512], start=True, stop=False))
                for fc in range(chunks_per_load):
                    k = l * chunks_per_load + fc
                    st_flag = (l == 0 and fc == 0 and brow_bf is None)
                    sp_flag = (k == nchunk - 1)
                    P.op("tensor", lambda e, t=t, k=k, fc=fc, wb=wb, st_flag=st_flag, sp_flag=sp_flag:
                         e.matmul(bank(t), lhs3[:, k, t * 128:t * 128 + 128],
                                  wbuf[wb][:, fc * 512:fc * 512 + 512],
                                  start=st_flag, stop=sp_flag))
            lastw[wb] = P.mark("tensor", "pe")
        vmm = lastw[(nloads - 1) % 2]
        for t in range(8):
            sl = t % 2
            if lastr[sl]:
                P.wait("sync", "dv", lastr[sl])
            if lastst[sl]:
                P.wait("sync", "sd", lastst[sl])
            vr = P.dma("sync", rload[sl][:, 0:512], rin[t * 128:t * 128 + 128, ob * 512:ob * 512 + 512], "sd")
            P.wait("vector", "pe", vmm)
            P.wait("vector", "sd", vr)
            if lastst[sl]:
                P.wait("vector", "sd", lastst[sl])
            if evac_fn is None:
                vd = P.op("vector", lambda e, t=t, sl=sl:
                          e.tensor_tensor(rst[sl][:, 0:512], bank(t), rload[sl][:, 0:512], OP.add), "dv")
            else:
                vd = evac_fn(P, T, t, ob, rload[sl][:, 0:512], rst[sl][:, 0:512])
            lastr[sl] = vd
            P.wait("sync", "dv", vd)
            lastst[sl] = P.dma("sync", rout[t * 128:t * 128 + 128, ob * 512:ob * 512 + 512],
                               rst[sl][:, 0:512], "sd")
        P.wait("tensor", "dv", P.counts["dv"])


def emit_l0oproj(P, T, resid_pair):
    ar, sb = T["ar"], T["sb"]
    P.dma("gpsimd", sb["rows3"][0:1, 0:4096], T["ob0"], "gd")
    att3 = T["ar"].t[:, ATT_O:ATT_O + 4096].rearrange("p (hc u) -> p hc u", u=1024)
    emit_bigffn(P, T, T["wo0"], 1, 4, att3, 4, sb["rows3"], resid_pair)


def emit_l0sel(P, T):
    ar, sb, bank, bankbf = T["ar"], T["sb"], T["bank"], T["bankbf"]
    st = lambda i: sb["stat"][:, i:i + 1]
    xnt = _xnt3(T)
    centbf = ar.view(W0_O, 2048, BF16)
    hitsbf = ar.view(W1_O, 64, BF16)
    onescol = ar.view(W1_O + 1024, 1, BF16)
    E = ar.view(KT_O, 64, F32)
    P.dma("gpsimd", centbf[:], T["cent"], "gd")
    vc = P.counts["gd"]
    va = P.dma("sync", sb["arowi"][:], T["assign"], "sd")
    P.wait("vector", "sd", va)
    P.op("vector", lambda e: e.tensor_copy(sb["arow"][:], sb["arowi"][:]))
    P.op("vector", lambda e: e.drain())
    vabf = P.op("vector", lambda e: e.tensor_copy(sb["arbf"][:], sb["arow"][:]), "dv")
    P.op("vector", lambda e: e.memset(onescol[:], 1.0))
    P.op("vector", lambda e: e.drain())
    vones = P.mark("vector", "dv")
    P.wait("tensor", "dv", vabf)
    for jb in range(8):
        P.wait("tensor", "dv", P.counts["dv"])
        P.op("tensor", lambda e, jb=jb:
             e.matmul(bank(1)[0:64, 0:512], sb["ones"][0:1, 0:64],
                      sb["arbf"][0:1, jb * 512:jb * 512 + 512], start=True, stop=True))
        vb = P.mark("tensor", "pe")
        P.wait("vector", "pe", vb)
        P.op("vector", lambda e, jb=jb:
             e.tensor_scalar(sb["onehot"][:, jb * 512:jb * 512 + 512], bank(1)[0:64, :],
                             sb["iota"][:], None, OP.is_equal), "dv")
    voh = P.counts["dv"]
    P.wait("tensor", "gd", vc)
    for t in range(8):
        P.wait("tensor", "ac", P.counts["ac"])
        for f in range(32):
            P.op("tensor", lambda e, t=t, f=f:
                 e.matmul(bank(0)[:, 0:64], xnt[:, f, t * 128:t * 128 + 128],
                          centbf[:, f * 64:f * 64 + 64], start=(f == 0), stop=(f == 31)))
        vd = P.mark("tensor", "pe")
        P.wait("vector", "pe", vd)
        vmax = P.op("vector", lambda e: e.reduce_max(st(0), bank(0)[:, 0:64], axis=AX.X, negate=True), "dv")
        P.op("vector", lambda e: e.drain())
        vm10 = P.op("vector", lambda e: e.tensor_scalar(st(1), st(0), 10.0, None, OP.mult), "dv")
        P.wait("scalar", "dv", vm10)
        vexp = P.op("scalar", lambda e: e.activation(E[:, 0:64], bank(0)[:, 0:64], AF.Exp,
                                                     scale=10.0, bias=st(1), accum_out=st(2)), "ac")
        P.wait("vector", "ac", vexp)
        P.op("vector", lambda e, t=t: e.tensor_scalar(sb["qsel"][:, t:t + 1], st(2), 100.0, None, OP.is_lt))
        P.op("vector", lambda e: e.tensor_scalar(st(3), st(2), 0.01, None, OP.mult))
        P.op("vector", lambda e: e.drain())
        vh = P.op("vector", lambda e: e.tensor_scalar(hitsbf[:], E[:, 0:64], st(3), None, OP.is_gt), "dv")
        P.wait("tensor", "dv", vh)
        P.wait("tensor", "dv", vones)
        P.op("tensor", lambda e, t=t:
             e.matmul(bank(2)[0:64, 0:1], hitsbf[:], onescol[:], start=(t == 0), stop=(t == 7)))
    vcnt = P.mark("tensor", "pe")
    P.wait("vector", "pe", vcnt)
    vc64 = P.op("vector", lambda e: e.tensor_copy(sb["c64a"][0:64, 0:1], bank(2)[0:64, 0:1]), "dv")
    P.wait("sync", "dv", vc64)
    vst = P.dma("sync", T["hits_in"][:, :], sb["c64a"][0:64, 0:1], "sd")
    P.wait("gpsimd", "sd", vst)
    vcc = P.collective("AllReduce", OP.add, ALL_GROUPS, T["hits_in"].opt(), T["hits_out"].opt(), "cchits")
    P.wait("sync", "cchits", vcc)
    vrb = P.dma("sync", sb["c64a"][0:64, 1:2], T["hits_out"][:, :], "sd")
    P.wait("vector", "sd", vrb)
    P.op("vector", lambda e: e.tensor_scalar(sb["c64bf"][:], sb["c64a"][0:64, 1:2], 0.0, None, OP.is_gt))
    P.op("vector", lambda e: e.drain())
    vsel = P.mark("vector", "dv")
    rselbc = ar.view(Q_O, 4096, BF16)
    rrow = sb["rows"]
    P.wait("tensor", "dv", vsel)
    P.wait("tensor", "dv", voh)
    for jb in range(8):
        P.wait("tensor", "dv", P.counts["dv"])
        P.op("tensor", lambda e, jb=jb:
             e.matmul(bank(3)[0:1, 0:512], sb["c64bf"][:], sb["onehot"][:, jb * 512:jb * 512 + 512],
                      start=True, stop=True))
        vr = P.mark("tensor", "pe")
        P.wait("vector", "pe", vr)
        vcp = P.op("vector", lambda e, jb=jb:
                   e.tensor_copy(rrow[0:1, jb * 512:jb * 512 + 512], bank(3)[0:1, 0:512]), "dv")
        P.wait("tensor", "dv", vcp)
        P.op("tensor", lambda e, jb=jb:
             e.matmul(bank(4)[:, 0:512], sb["ones"][0:1, 0:128], rrow[0:1, jb * 512:jb * 512 + 512],
                      start=True, stop=True))
        vbb = P.mark("tensor", "pe")
        P.wait("vector", "pe", vbb)
        P.op("vector", lambda e, jb=jb:
             e.tensor_copy(rselbc[:, jb * 512:jb * 512 + 512], bank(4)[:, 0:512]), "dv")


def emit_l0dense(P, T, resid_pair):
    ar, sb = T["ar"], T["sb"]
    P.dma("gpsimd", sb["rows3"][0:1, 0:4096], T["fb"], "gd")
    xnt = _xnt3(T)
    rselbc = ar.view(Q_O, 4096, BF16)

    def evac(P, T, t, ob, rslice, dst):
        v1 = P.op("vector", lambda e, t=t, ob=ob, dst=dst:
                  e.scalar_tensor_tensor(dst, T["bank"](t), sb["qsel"][:, t:t + 1],
                                         rselbc[:, ob * 512:ob * 512 + 512], OP.mult, OP.mult), "dv")
        v2 = P.op("vector", lambda e, dst=dst, rslice=rslice:
                  e.tensor_tensor(dst, dst, rslice, OP.add), "dv")
        return v2

    emit_bigffn(P, T, T["wf"], 8, 4, xnt, 32, sb["rows3"], resid_pair, evac_fn=evac)


def emit_ln3_stage(P, T):
    ar = T["ar"]
    xnt = _xnt3(T)
    P.wait("sync", "dv", P.counts["dv"])
    for b in range(2):
        P.dma("sync", T["cc1_in"][128 * b:128 * b + 128, :],
              xnt[:, :, 512 * b:512 * b + 512], "sd")
    vs = P.counts["sd"]
    P.wait("gpsimd", "sd", vs)
    T["v_cc1"] = P.collective("AllGather", OP.bypass, PAIR_GROUPS,
                              T["cc1_in"].opt(), T["cc1_out"].opt(), "cc1")


CC1_ROW = {0: 0, 1: 256, 2: 384, 3: 128}


def emit_l1kv(P, T):
    ar, sb, bank = T["ar"], T["sb"], T["bank"]
    P.wait("sync", "cc1", T["v_cc1"])
    xnb = ar.t[:, XNT_O:XNT_O + 16384].rearrange("p (f u) -> p f u", u=512)
    wres = ar.t[:, XNT_O + 16384:XNT_O + 32768]
    k3 = ar.t[:, KT_O:KT_O + 8192].rearrange("p (g u) -> p g u", u=2048)
    v2 = ar.t[:, V_O:V_O + 8192]

    for gk in range(4):
        P.dma("gpsimd", wres[:, gk * 4096:(gk + 1) * 4096], T["wk1"][gk], "gd")
    vwk = P.counts["gd"]
    P.wait("tensor", "gd", vwk)
    lastpe = P.counts["pe"]
    for tb in range(4):
        P.wait("sync", "pe", lastpe)
        vx = P.dma("sync", ar.t[:, XNT_O:XNT_O + 16384],
                   T["cc1_out"][CC1_ROW[tb]:CC1_ROW[tb] + 128, :], "sd")
        P.wait("tensor", "sd", vx)
        for gk in range(4):
            for f in range(32):
                P.op("tensor", lambda e, gk=gk, f=f:
                     e.matmul(bank(gk), wres[:, gk * 4096 + f * 128: gk * 4096 + (f + 1) * 128],
                              xnb[:, f, :], start=(f == 0), stop=(f == 31)))
        lastpe = P.mark("tensor", "pe")
        P.wait("scalar", "pe", lastpe)
        for gk in range(4):
            P.op("scalar", lambda e, gk=gk, tb=tb:
                 e.activation(k3[:, gk, tb * 512:tb * 512 + 512], bank(gk), AF.Identity), "ac")
        P.wait("tensor", "ac", P.counts["ac"])
    P.wait("gpsimd", "ac", P.counts["ac"])
    P.wait("gpsimd", "pe", lastpe)
    for l in range(8):
        P.dma("gpsimd", wres[:, l * 2048:(l + 1) * 2048], T["wv1"][l], "gd")
    vwv = P.counts["gd"]
    P.wait("tensor", "gd", vwv)
    for tb in range(4):
        P.wait("sync", "pe", lastpe)
        vx = P.dma("sync", ar.t[:, XNT_O:XNT_O + 16384],
                   T["cc1_out"][CC1_ROW[tb]:CC1_ROW[tb] + 128, :], "sd")
        P.wait("tensor", "sd", vx)
        for tt in range(4):
            for f in range(32):
                P.op("tensor", lambda e, tt=tt, f=f:
                     e.matmul(bank(4 + tt), xnb[:, f, tt * 128:tt * 128 + 128],
                              wres[:, f * 512:(f + 1) * 512], start=(f == 0), stop=(f == 31)))
        lastpe = P.mark("tensor", "pe")
        P.wait("scalar", "pe", lastpe)
        for tt in range(4):
            kt = tb * 4 + tt
            P.op("scalar", lambda e, tt=tt, kt=kt:
                 e.activation(v2[:, kt * 512:(kt + 1) * 512], bank(4 + tt), AF.Identity), "ac")
        P.wait("tensor", "ac", P.counts["ac"])


def _mJ(J):
    if J < 4:
        return J * 128
    if J >= 12:
        return 512 + (J - 12) * 128
    if J < 8:
        return 1024 + (J - 4) * 128
    return 1536 + (J - 8) * 128


def emit_l1groups(P, T):
    ar, sb, bank, bankbf = T["ar"], T["sb"], T["bank"], T["bankbf"]
    st = lambda i: sb["stat"][:, i:i + 1]
    xnb = ar.t[:, XNT_O:XNT_O + 16384].rearrange("p (f u) -> p f u", u=512)
    wres = ar.t[:, XNT_O + 16384:XNT_O + 32768]
    q3 = ar.t[:, Q_O:Q_O + 8192].rearrange("p (cc u) -> p cc u", u=2048)
    k3 = ar.t[:, KT_O:KT_O + 8192].rearrange("p (g u) -> p g u", u=2048)
    v3 = ar.t[:, V_O:V_O + 8192].rearrange("p (kt u) -> p kt u", u=512)
    att3 = ar.t[:, ATT_O:ATT_O + 8192].rearrange("p (hh u) -> p hh u", u=2048)
    pbf = ar.view(P_O, 2048, BF16)
    pt = ar.view(PT_O, 2048, BF16)
    CS = 1.0 / np.sqrt(128.0)
    T["v_cc2"] = {}
    vcm = P.dma("sync", sb["cmask"][:], T["cmask_d"], "sd")
    P.wait("vector", "sd", vcm)

    for g in range(4):
        P.wait("gpsimd", "pe", P.counts["pe"])
        for cc in range(4):
            P.dma("gpsimd", wres[:, cc * 4096:(cc + 1) * 4096], T["wq1"][g * 4 + cc], "gd")
        vw = P.counts["gd"]
        P.wait("tensor", "gd", vw)
        lastpe = P.counts["pe"]
        for tb in range(4):
            P.wait("sync", "pe", lastpe)
            vx = P.dma("sync", ar.t[:, XNT_O:XNT_O + 16384],
                       T["cc1_out"][CC1_ROW[tb]:CC1_ROW[tb] + 128, :], "sd")
            P.wait("tensor", "sd", vx)
            for cc in range(4):
                for f in range(32):
                    P.op("tensor", lambda e, cc=cc, f=f:
                         e.matmul(bank(cc), wres[:, cc * 4096 + f * 128: cc * 4096 + (f + 1) * 128],
                                  xnb[:, f, :], start=(f == 0), stop=(f == 31)))
            lastpe = P.mark("tensor", "pe")
            P.wait("scalar", "pe", lastpe)
            for cc in range(4):
                P.op("scalar", lambda e, cc=cc, tb=tb:
                     e.activation(q3[:, cc, tb * 512:tb * 512 + 512], bank(cc), AF.Identity), "ac")
            P.wait("tensor", "ac", P.counts["ac"])
        last_attn_ev = 0
        for hh in range(4):
            for J in range(16):
                nkb = J // 4 + 1
                if last_attn_ev:
                    P.wait("tensor", "dv", last_attn_ev)
                for kb in range(nkb):
                    P.op("tensor", lambda e, kb=kb, hh=hh, J=J, g=g:
                         e.matmul(T["psA"][:, kb * 512:kb * 512 + 512],
                                  q3[:, hh, J * 128:J * 128 + 128],
                                  k3[:, g, kb * 512:kb * 512 + 512], start=True, stop=True))
                vs = P.mark("tensor", "pe")
                dk, r = J // 4, J % 4
                P.wait("vector", "pe", vs)
                P.op("vector", lambda e, dk=dk, r=r:
                     e.tensor_tensor(T["psA"][:, dk * 512:dk * 512 + 512],
                                     T["psA"][:, dk * 512:dk * 512 + 512],
                                     sb["cmask"][:, r * 512:r * 512 + 512], OP.add))
                P.op("vector", lambda e: e.drain())
                vmax = P.op("vector", lambda e, nkb=nkb:
                            e.reduce_max(st(0), T["psA"][:, 0:nkb * 512], axis=AX.X, negate=True), "dv")
                P.op("vector", lambda e: e.drain())
                vsc = P.op("vector", lambda e: e.tensor_scalar(st(1), st(0), CS, None, OP.mult), "dv")
                P.wait("scalar", "dv", vsc)
                vexp = P.op("scalar", lambda e, nkb=nkb:
                            e.activation(pbf[:, 0:nkb * 512], T["psA"][:, 0:nkb * 512], AF.Exp,
                                         scale=CS, bias=st(1), accum_out=st(2)), "ac")
                P.wait("vector", "ac", vexp)
                P.op("vector", lambda e: e.reciprocal(st(3), st(2)))
                P.op("vector", lambda e: e.drain())
                vscale = P.op("vector", lambda e, nkb=nkb:
                              e.tensor_scalar(pbf[:, 0:nkb * 512], pbf[:, 0:nkb * 512],
                                              st(3), None, OP.mult), "dv")
                P.wait("tensor", "dv", vscale)
                nkt = nkb * 4
                pairs = [(pbf[:, c * 128:c * 128 + 128], pt[:, c * 128:c * 128 + 128])
                         for c in range(nkt)]
                emit_transposes(P, T, pairs)
                P.wait("tensor", "dv", P.counts["dv"])
                for kt in range(nkt):
                    P.op("tensor", lambda e, kt=kt, g=g, nkt=nkt:
                         e.matmul(T["psB"][:, 1024:1152], v3[:, kt, g * 128:g * 128 + 128],
                                  pt[:, kt * 128:kt * 128 + 128], start=(kt == 0), stop=(kt == nkt - 1)))
                vav = P.mark("tensor", "pe")
                P.wait("vector", "pe", vav)
                last_attn_ev = P.op("vector", lambda e, hh=hh, J=J:
                                    e.tensor_copy(att3[:, hh, _mJ(J):_mJ(J) + 128],
                                                  T["psB"][:, 1024:1152]), "dv")
        P.wait("sync", "dv", last_attn_ev)
        vsg = P.dma("sync", T[f"cc2_in{g}"][:, :], ar.t[:, ATT_O:ATT_O + 8192], "sd")
        P.wait("gpsimd", "sd", vsg)
        T["v_cc2"][g] = P.collective("AllGather", OP.bypass, PAIR_GROUPS,
                                     T[f"cc2_in{g}"].opt(), T[f"cc2_out{g}"].opt(), f"cc2_{g}")


def emit_l1oproj(P, T, resid_pair):
    ar, sb = T["ar"], T["sb"]
    bl = sb["blend"]
    bb = [ar.view(W1_O, 1024, BF16), ar.view(W1_O + 1024, 1024, BF16)]
    tmp = ar.view(W1_O + 2048, 1024, BF16)
    own3 = ar.t[:, XNT_O:XNT_O + 32768].rearrange("p (lc u) -> p lc u", u=1024)
    for g in range(4):
        P.wait("sync", f"cc2_{g}", T["v_cc2"][g])
    lastbl = 0
    for lc in range(32):
        mine = lc < 16
        g, cc = (lc % 16) // 4, (lc % 16) % 4
        s0, s1 = (0, 1) if mine else (1, 0)
        if lastbl:
            P.wait("sync", "dv", lastbl)
        v0 = P.dma("sync", bb[0][:], T[f"cc2_out{g}"][128 * s0:128 * s0 + 128,
                                                      cc * 2048:cc * 2048 + 1024], "sd")
        v1 = P.dma("sync", bb[1][:], T[f"cc2_out{g}"][128 * s1:128 * s1 + 128,
                                                      cc * 2048 + 1024:cc * 2048 + 2048], "sd")
        P.wait("vector", "sd", v1)
        P.op("vector", lambda e: e.tensor_scalar(tmp[:], bb[0][:], bl[:, 0:1], None, OP.mult))
        P.op("vector", lambda e: e.drain())
        lastbl = P.op("vector", lambda e, lc=lc:
                      e.scalar_tensor_tensor(own3[:, lc, :], bb[1][:], bl[:, 1:2], tmp[:],
                                             OP.mult, OP.add), "dv")
    P.wait("tensor", "dv", lastbl)
    P.wait("gpsimd", "dv", lastbl)
    emit_bigffn(P, T, T["wo1"], 8, 4, own3, 32, None, resid_pair)


def emit_l1ffnA(P, T):
    ar, sb, bank = T["ar"], T["sb"], T["bank"]
    xnt = _xnt3(T)
    wbuf = [ar.view(W0_O, 4096, BF16), ar.view(W1_O, 4096, BF16)]
    sil = [ar.view(P_O, 1024, F32), ar.view(PT_O, 1024, F32)]
    mbf = [ar.view(ATT_O, 1024, BF16), ar.view(ATT_O + 1024, 1024, BF16)]
    P.dma("gpsimd", sb["rows"][0:1, 0:11008], T["bg"], "gd")
    P.dma("gpsimd", sb["rows2"][0:1, 0:11008], T["bu"], "gd")
    vrows = P.counts["gd"]
    last_use = [0, 0]
    last_cons = [0, 0]
    lastm = [0, 0]
    for ib in range(86):
        A = ib % 2
        B = 1 - A
        ps = T["psA"] if A == 0 else T["psB"]
        if last_use[A]:
            P.wait("gpsimd", "pe", last_use[A])
        vg = P.dma("gpsimd", wbuf[A][:], T["wg"][ib], "gd")
        P.wait("tensor", "gd", vg)
        if ib == 0:
            P.wait("tensor", "gd", vrows)
        if last_cons[A]:
            P.wait("tensor", "dv", last_cons[A])
        for th in range(2):
            P.op("tensor", lambda e, ib=ib, th=th, ps=ps:
                 e.matmul(ps[:, th * 512:th * 512 + 512], sb["rows"][0:1, ib * 128:ib * 128 + 128],
                          sb["ones"][0:1, th * 512:th * 512 + 512], start=True, stop=False))
            for f in range(32):
                P.op("tensor", lambda e, f=f, th=th, A=A, ps=ps:
                     e.matmul(ps[:, th * 512:th * 512 + 512], wbuf[A][:, f * 128:f * 128 + 128],
                              xnt[:, f, th * 512:th * 512 + 512], start=False, stop=(f == 31)))
        vgdone = P.mark("tensor", "pe")
        last_use[A] = vgdone
        if last_use[B]:
            P.wait("gpsimd", "pe", last_use[B])
        vu = P.dma("gpsimd", wbuf[B][:], T["wu"][ib], "gd")
        P.wait("tensor", "gd", vu)
        for th in range(2):
            P.op("tensor", lambda e, ib=ib, th=th, ps=ps:
                 e.matmul(ps[:, 1024 + th * 512:1024 + th * 512 + 512],
                          sb["rows2"][0:1, ib * 128:ib * 128 + 128],
                          sb["ones"][0:1, th * 512:th * 512 + 512], start=True, stop=False))
            for f in range(32):
                P.op("tensor", lambda e, f=f, th=th, B=B, ps=ps:
                     e.matmul(ps[:, 1024 + th * 512:1024 + th * 512 + 512],
                              wbuf[B][:, f * 128:f * 128 + 128],
                              xnt[:, f, th * 512:th * 512 + 512], start=False, stop=(f == 31)))
        vudone = P.mark("tensor", "pe")
        last_use[B] = vudone
        P.wait("scalar", "pe", vgdone)
        if lastm[A]:
            P.wait("scalar", "sd", lastm[A])
        vsil = P.op("scalar", lambda e, A=A, ps=ps: e.activation(sil[A][:], ps[:, 0:1024], AF.Silu), "ac")
        P.wait("vector", "ac", vsil)
        P.wait("vector", "pe", vudone)
        if lastm[A]:
            P.wait("vector", "sd", lastm[A])
        vm = P.op("vector", lambda e, A=A, ps=ps: e.tensor_tensor(mbf[A][:], sil[A][:], ps[:, 1024:2048], OP.mult), "dv")
        last_cons[A] = vm
        P.wait("sync", "dv", vm)
        lastm[A] = P.dma("sync", T["m_d"][ib * 128:(ib + 1) * 128, :], mbf[A][:], "sd")
        import os as _o
        if _o.environ.get("FFNA_SER"):
            P.wait("tensor", "dv", P.counts["dv"])
            P.wait("tensor", "ac", P.counts["ac"])


def emit_l1ffnB(P, T, resid_pair):
    ar, sb, bank = T["ar"], T["sb"], T["bank"]
    rin, rout = resid_pair
    P.dma("gpsimd", sb["rows3"][0:1, 0:4096], T["bd"], "gd")
    vrows = P.counts["gd"]
    NB = 4
    mb = [ar.t[:, ATT_O + i * 1024:ATT_O + (i + 1) * 1024] for i in range(NB)]
    wb_ = [ar.view(W0_O + i * 512, 512, BF16) for i in range(NB)]
    rload = [ar.view(R0_O, 1024, F32), ar.view(R0_O + 2048, 1024, F32)]
    rst = [ar.view(R1_O, 1024, F32), ar.view(R1_O + 2048, 1024, F32)]
    lastmm = {}
    lastr = [0, 0]
    lastst = [0, 0]
    mv = {}
    wv = {}

    def issue_loads(ob, ic):
        sl = ic % NB
        key = (ob, ic)
        if ic - NB >= 0:
            prev = (ob, ic - NB)
        else:
            pmap = {0: 84, 1: 85, 2: 82, 3: 83}
            prev = (ob - 1, pmap[ic % NB])
        if prev[0] >= 0:
            assert prev in lastmm, f"prefetch ran ahead of slot occupant {prev}"
            P.wait("sync", "pe", lastmm[prev])
            P.wait("gpsimd", "pe", lastmm[prev])
        mv[key] = P.dma("sync", mb[sl], T["m_d"][ic * 128:(ic + 1) * 128, :], f"mS{sl}")
        wv[key] = P.dma("gpsimd", wb_[sl][:], T["wd"][ob, ic], f"wS{sl}")

    for ob in range(8):
        for ic in range(86):
            sl = ic % NB
            if (ob, ic) not in mv:
                issue_loads(ob, ic)
            pmap = {0: 84, 1: 85, 2: 82, 3: 83}
            for ahead in range(1, NB):
                ic2 = ic + ahead
                ob2, ic2 = (ob, ic2) if ic2 < 86 else (ob + 1, ic2 - 86)
                if ob2 >= 8 or (ob2, ic2) in mv:
                    continue
                prev2 = (ob2, ic2 - NB) if ic2 - NB >= 0 else (ob2 - 1, pmap[ic2 % NB])
                if prev2[0] >= 0 and prev2 not in lastmm:
                    continue
                issue_loads(ob2, ic2)
            P.wait("tensor", f"mS{sl}", mv[(ob, ic)])
            P.wait("tensor", f"wS{sl}", wv[(ob, ic)])
            if ic == 0:
                P.wait("tensor", "gd", vrows)
                for t in range(8):
                    P.op("tensor", lambda e, t=t, ob=ob:
                         e.matmul(bank(t), sb["ones"][0:1, 0:128],
                                  sb["rows3"][0:1, ob * 512:ob * 512 + 512], start=True, stop=False))
            for t in range(8):
                P.op("tensor", lambda e, t=t, ic=ic, sl=sl:
                     e.matmul(bank(t), mb[sl][:, t * 128:t * 128 + 128], wb_[sl][:],
                              start=False, stop=(ic == 85)))
            lastmm[(ob, ic)] = P.mark("tensor", "pe")
            import os as _o
            if _o.environ.get("FFNB_SER"):
                P.wait("sync", "pe", lastmm[(ob, ic)])
                P.wait("gpsimd", "pe", lastmm[(ob, ic)])
        vmm = P.counts["pe"]
        for t in range(8):
            sl = t % 2
            if lastr[sl]:
                P.wait("sync", "dv", lastr[sl])
            vr = P.dma("sync", rload[sl][:, 0:512], rin[t * 128:t * 128 + 128, ob * 512:ob * 512 + 512], "sd")
            P.wait("vector", "pe", vmm)
            P.wait("vector", "sd", vr)
            if lastst[sl]:
                P.wait("vector", "sd", lastst[sl])
            vd = P.op("vector", lambda e, t=t, sl=sl:
                      e.tensor_tensor(rst[sl][:, 0:512], bank(t), rload[sl][:, 0:512], OP.add), "dv")
            lastr[sl] = vd
            P.wait("sync", "dv", vd)
            lastst[sl] = P.dma("sync", rout[t * 128:t * 128 + 128, ob * 512:ob * 512 + 512],
                               rst[sl][:, 0:512], "sd")
        P.wait("tensor", "dv", P.counts["dv"])


def emit_phase(P, T, ph, rpair, fused):
    if ph in ("ln1", "ln2", "ln3", "ln4"):
        emit_ln(P, T, rpair[0])
        if ph == "ln3":
            emit_ln3_stage(P, T)
    elif ph == "l0kvq":
        emit_l0kvq(P, T)
    elif ph == "l0attn":
        emit_l0attn(P, T)
    elif ph == "l0oproj":
        emit_l0oproj(P, T, (T[rpair[0]], T[rpair[1]]))
    elif ph == "l0sel":
        emit_l0sel(P, T)
    elif ph == "l0dense":
        emit_l0dense(P, T, (T[rpair[0]], T[rpair[1]]))
    elif ph == "l1kv":
        emit_l1kv(P, T)
    elif ph == "l1groups":
        emit_l1groups(P, T)
    elif ph == "l1oproj":
        emit_l1oproj(P, T, (T[rpair[0]], T[rpair[1]]))
    elif ph == "l1ffnA":
        emit_l1ffnA(P, T)
    elif ph == "l1ffnB":
        emit_l1ffnB(P, T, (T[rpair[0]], T[rpair[1]]))
    elif ph == "dump":
        import os
        off = int(os.environ.get("DUMP_OFF", str(XNT_O)))
        ln_ = int(os.environ.get("DUMP_LEN", "32768"))
        ng = ln_ // 4096
        out3 = T["rout"].rearrange("(g p) d -> p g d", p=128)
        src3 = T["ar"].t[:, off:off + ng * 4096].rearrange("p (g d) -> p g d", d=4096)
        P.dma("gpsimd", out3[:, 0:ng, :], src3, "gd")
        P.wait("gpsimd", "gd", P.counts["gd"])
    elif ph == "dumpR":
        out3 = T["rout"].rearrange("(g p) d -> p g d", p=128)
        src2 = T["ar"].t[:, R0_O:R0_O + 8192].bitcast(F32)
        P.dma("sync", T["rout"][0:128, :], src2, "sd")
        P.wait("sync", "sd", P.counts["sd"])
    else:
        raise ValueError(ph)



def _regroup(a, g):
    n, p, w = a.shape
    return np.ascontiguousarray(a.reshape(n // g, g, p, w).transpose(0, 2, 1, 3).reshape(n // g, p, g * w))


def _host_arrays(inputs):
    sh, cores = prep_inputs(inputs)
    sh["wv0"] = _regroup(sh["wv0"], 4)
    sh["cent"] = np.ascontiguousarray(sh["cent"].transpose(1, 0, 2).reshape(128, 2048))
    sh["wo0"] = _regroup4(sh["wo0"])
    sh["wf"] = _regroup4(sh["wf"])
    for c in cores:
        c["wv1"] = _regroup(c["wv1"], 4)
        c["wo1"] = _regroup4(c["wo1"])
    return sh, cores


def _regroup4(a):
    nob, nk, p, w = a.shape
    return np.ascontiguousarray(a.reshape(nob, nk // 4, 4, p, w).transpose(0, 1, 3, 2, 4)
                                .reshape(nob, nk // 4, p, 4 * w))


LAUNCHES = ["A", "B", "C", "D"]
import os as _os_mod
FUSED = _os_mod.environ.get("MK_UNFUSED", "") == ""


def kernel(**inputs):
    sh, cores = _host_arrays(inputs)
    if FUSED:
        nc = build_launch("ALL")
        innames = set(a.name[:-4] if a.name.endswith("_set") else a.name
                      for a in nc.m.functions[0].allocations
                      if getattr(a, "kind", None) == "ExternalInput")
        innames.discard("partition_id")
        in_maps = []
        for ci in range(NCORES):
            m = {k: v for k, v in sh.items() if k in innames}
            m.update({k: v for k, v in cores[ci].items() if k in innames})
            m["rin"] = cores[ci]["x"]
            in_maps.append(m)
        res = run_bass_kernel_spmd(nc, in_maps, core_ids=list(range(NCORES)))
        return scatter_output([{"out": res.results[ci]["out"]} for ci in range(NCORES)])
    r_cur = [c["x"] for c in cores]
    for li, which in enumerate(LAUNCHES):
        nc = build_launch(which)
        innames = set(a.name[:-4] if a.name.endswith("_set") else a.name
                      for a in nc.m.functions[0].allocations
                      if getattr(a, "kind", None) == "ExternalInput")
        innames.discard("partition_id")
        in_maps = []
        for ci in range(NCORES):
            m = {}
            for k, v in sh.items():
                if k in innames:
                    m[k] = v
            for k, v in cores[ci].items():
                if k in innames:
                    m[k] = v
            m["rin"] = r_cur[ci]
            in_maps.append(m)
        res = run_bass_kernel_spmd(nc, in_maps, core_ids=list(range(NCORES)))
        r_cur = [res.results[ci]["out"] for ci in range(NCORES)]
    return scatter_output([{"out": r} for r in r_cur])


# version 05
# speedup vs baseline: 1.0729x; 1.0280x over previous
import numpy as np
import ml_dtypes
from contextlib import ExitStack
from collections import defaultdict

import concourse.bass as bass
import concourse.mybir as mybir
from concourse.bass_utils import run_bass_kernel_spmd
from concourse.alu_op_type import AluOpType as OP

F32 = mybir.dt.float32
BF16 = mybir.dt.bfloat16
I32 = mybir.dt.int32
AF = mybir.ActivationFunctionType
AX = mybir.AxisListType
BFNP = ml_dtypes.bfloat16

D = 4096
TOK = 1024
NCORES = 8
SEQ = 2048
NEG = -1.0e30


def _tile_lhsT(wT, nchunk_f, nblk_d):
    F, O = wT.shape
    assert F == nchunk_f * 128 and O == nblk_d * 128
    return np.ascontiguousarray(
        wT.reshape(nchunk_f, 128, nblk_d, 128).transpose(2, 1, 0, 3).reshape(nblk_d, 128, nchunk_f * 128))


def _tile_rhs(wT, nchunk_f, width):
    F, O = wT.shape
    assert F == nchunk_f * 128
    return np.ascontiguousarray(wT.reshape(nchunk_f, 128, O))


def _tile_obff(wT, nchunk_k, nob, obw=512):
    K, O = wT.shape
    assert K == nchunk_k * 128 and O == nob * obw
    return np.ascontiguousarray(
        wT.reshape(nchunk_k, 128, nob, obw).transpose(2, 0, 1, 3).reshape(nob, nchunk_k, 128, obw))


def prep_inputs(inp):
    f32 = lambda a: np.ascontiguousarray(np.asarray(a), dtype=np.float32)
    x = f32(inp["L0_attn_x"])

    sh = {}
    Wqk = f32(inp["L0_attn_Wqk_w"]).reshape(8, 64, 2, D)
    Wq = Wqk[:, :, 0, :].reshape(512, D)
    Wk = Wqk[:, :, 1, :].reshape(512, D)
    qkb = f32(inp["L0_attn_Wqk_b"]).reshape(8, 64, 2)
    sh["wq0"] = _tile_lhsT(np.ascontiguousarray(Wq.T), 32, 4)
    sh["wk0"] = _tile_lhsT(np.ascontiguousarray(Wk.T), 32, 4)
    sh["qb0"] = np.ascontiguousarray(qkb[:, :, 0].reshape(1, 512))
    sh["kb0"] = np.ascontiguousarray(qkb[:, :, 1].reshape(1, 512))
    sh["wv0"] = _tile_rhs(np.ascontiguousarray(f32(inp["L0_attn_Wv_w"]).T), 32, 512)
    sh["vb0"] = f32(inp["L0_attn_Wv_b"]).reshape(1, 512)
    sh["wo0"] = _tile_obff(np.ascontiguousarray(f32(inp["L0_attn_Wo_w"]).T), 4, 8)
    sh["ob0"] = f32(inp["L0_attn_Wo_b"]).reshape(1, D)
    sh["wf"] = _tile_obff(np.ascontiguousarray(f32(inp["L0_ffn_weight"]).T), 32, 8)
    sh["fb"] = f32(inp["L0_ffn_bias"]).reshape(1, D)
    sh["cent"] = _tile_rhs(np.ascontiguousarray(f32(inp["L0_ffn_centroids"]).T), 32, 64)
    sh["assign"] = np.ascontiguousarray(np.asarray(inp["L0_ffn_assignments"], dtype=np.int32).reshape(1, D))
    sh["wg"] = _tile_lhsT(np.ascontiguousarray(f32(inp["L1_ffn_w_gate"]).T), 32, 86)
    sh["wu"] = _tile_lhsT(np.ascontiguousarray(f32(inp["L1_ffn_w_up"]).T), 32, 86)
    sh["bg"] = f32(inp["L1_ffn_b_gate"]).reshape(1, 11008)
    sh["bu"] = f32(inp["L1_ffn_b_up"]).reshape(1, 11008)
    sh["wd"] = _tile_obff(np.ascontiguousarray(f32(inp["L1_ffn_w_down"]).T), 86, 8)
    sh["bd"] = f32(inp["L1_ffn_b_down"]).reshape(1, D)
    sh["ident"] = np.eye(128, dtype=np.float32)
    sh["ones"] = np.ones((1, 1024), dtype=np.float32)
    sh["iota64"] = np.arange(64, dtype=np.float32).reshape(64, 1)
    cm = np.zeros((4, 128, 512), dtype=np.float32)
    for r in range(4):
        i = np.arange(128)[:, None]
        j = np.arange(512)[None, :]
        cm[r] = np.where(j <= r * 128 + i, 0.0, NEG)
    sh["cmask"] = np.ascontiguousarray(cm.transpose(1, 0, 2).reshape(128, 2048))

    wq1_full = f32(inp["L1_attn_w_q"])
    wk1_full = f32(inp["L1_attn_w_k"])
    wv1_full = f32(inp["L1_attn_w_v"])
    wo1_full = f32(inp["L1_attn_w_o"])

    cores = []
    for c in range(NCORES):
        b, h = c // 2, c % 2
        d = {}
        if h == 0:
            tok_idx = np.r_[0:512, 1536:2048]
            heads = list(range(16))
            kvh = list(range(4))
        else:
            tok_idx = np.r_[512:1536]
            heads = list(range(16, 32))
            kvh = list(range(4, 8))
        d["x"] = np.ascontiguousarray(x[b][tok_idx])
        wq_m = np.stack([
            _tile_lhsT(np.ascontiguousarray(wq1_full[:, hh * 128:(hh + 1) * 128]), 32, 1)[0]
            for hh in heads])
        d["wq1"] = np.ascontiguousarray(wq_m)
        wk_m = np.stack([
            _tile_lhsT(np.ascontiguousarray(wk1_full[:, g * 128:(g + 1) * 128]), 32, 1)[0]
            for g in kvh])
        d["wk1"] = np.ascontiguousarray(wk_m)
        vcols = np.concatenate([np.arange(g * 128, (g + 1) * 128) for g in kvh])
        d["wv1"] = _tile_rhs(np.ascontiguousarray(wv1_full[:, vcols]), 32, 512)
        order = heads + [hh for hh in range(32) if hh not in heads]
        wo_p = np.concatenate([wo1_full[hh * 128:(hh + 1) * 128] for hh in order], axis=0)
        d["wo1"] = _tile_obff(wo_p, 32, 8)
        bl = np.zeros((128, 2), dtype=np.float32)
        bl[:, 0] = 1.0 if h == 0 else 0.0
        bl[:, 1] = 0.0 if h == 0 else 1.0
        d["blend"] = bl
        cores.append(d)
    return sh, cores


def scatter_output(res_list):
    out = np.zeros((4, 2048, D), dtype=np.float32)
    for c in range(NCORES):
        b, h = c // 2, c % 2
        r = res_list[c]["out"]
        if h == 0:
            out[b, 0:512] = r[0:512]
            out[b, 1536:2048] = r[512:1024]
        else:
            out[b, 512:1536] = r
    return out



ENGINES = ("tensor", "vector", "scalar", "sync", "gpsimd")


class Plan:
    def __init__(self, nc, stack):
        self.nc = nc
        self.stack = stack
        self.streams = {e: [] for e in ENGINES}
        self.sems = {}
        self.counts = defaultdict(int)

    def sem(self, name):
        if name not in self.sems:
            self.sems[name] = self.stack.enter_context(self.nc.semaphore(name))
        return self.sems[name]

    def emit(self, eng, fn):
        self.streams[eng].append([fn, False])

    def wait(self, eng, sname, val):
        s = self.sem(sname)
        self.emit(eng, lambda e, s=s, v=val: e.wait_ge(s, v))

    def dma(self, eng, out_ap, in_ap, sname):
        s = self.sem(sname)
        self.counts[sname] += 16
        v = self.counts[sname]
        self.emit(eng, lambda e, o=out_ap, i=in_ap, s=s: e.dma_start(out=o, in_=i).then_inc(s, 16))
        self.streams[eng][-1][1] = True
        return v

    def op(self, eng, fn, sname=None):
        if sname is None:
            self.emit(eng, fn)
            return None
        s = self.sem(sname)
        self.counts[sname] += 1
        v = self.counts[sname]

        def wrapped(e, fn=fn, s=s):
            fn(e).then_inc(s, 1)
        self.emit(eng, wrapped)
        self.streams[eng][-1][1] = True
        return v

    def collective(self, kind, op, groups, in_ap, out_ap, sname):
        s = self.sem(sname)
        self.counts[sname] += 1
        v = self.counts[sname]

        def fn(e, s=s):
            e.collective_compute(kind, op, replica_groups=groups,
                                 ins=[in_ap], outs=[out_ap]).then_inc(s)
        self.emit("gpsimd", fn)
        self.streams["gpsimd"][-1][1] = True
        return v

    def mark(self, eng, sname):
        s = self.sem(sname)
        self.counts[sname] += 1
        v = self.counts[sname]
        last = self.streams[eng][-1]
        assert not last[1], f"mark({eng}): last instruction already carries an inc"
        oldfn = last[0]

        def wrapped(e, oldfn=oldfn, s=s):
            inst = oldfn(e)
            inst.then_inc(s, 1)
        last[0] = wrapped
        last[1] = True
        return v

    def barrier_mark(self, eng, sname):
        s = self.sem(sname)
        self.counts[sname] += 1
        v = self.counts[sname]
        self.emit(eng, lambda e: e.drain())
        self.emit(eng, lambda e, s=s: e.nop().then_inc(s, 1))
        return v


class Arena:
    def __init__(self, nc, stack, n_bf16):
        self.t = stack.enter_context(nc.sbuf_tensor("arena", [128, n_bf16], BF16))
        self.n = n_bf16

    def view(self, off_bf16, length, dtype):
        if dtype == BF16:
            ap = self.t[:, off_bf16:off_bf16 + length]
            return ap
        elif dtype == F32:
            ap = self.t[:, off_bf16:off_bf16 + 2 * length].bitcast(F32)
            return ap
        raise ValueError(dtype)


PAIR_GROUPS = [[0, 1], [2, 3], [4, 5], [6, 7]]
ALL_GROUPS = [list(range(8))]


XNT_O, XNT_L = 0, 32768
W0_O, W0_L = 32768, 4096
W1_O, W1_L = 36864, 4096
KT_O, KT_L = 40960, 8192
V_O, V_L = 49152, 8192
Q_O, Q_L = 57344, 8192
P_O, P_L = 65536, 2048
PT_O, PT_L = 67584, 2048
ATT_O, ATT_L = 69632, 8192
R0_O = 77824
R1_O = 86016
ARENA_LEN = 94208

ALL_ENG = list(ENGINES)


def build_launch(which):
    nc = bass.Bass("TRN2", target_bir_lowering=False)
    stack = ExitStack()

    def din(name, shape, dt=F32):
        return nc.dram_tensor(name, list(shape), dt, kind="ExternalInput").ap()

    def dout(name, shape, dt=F32):
        return nc.dram_tensor(name, list(shape), dt, kind="ExternalOutput").ap()

    def dint(name, shape, dt=F32):
        return nc.dram_tensor(name, list(shape), dt).ap()

    T = {}
    fused = which == "ALL"
    segs = {"A": ["ln1", "l0kvq", "l0attn", "l0oproj"],
            "B": ["ln2", "l0sel", "l0dense"],
            "C": ["ln3", "l1kv", "l1groups", "l1oproj"],
            "D": ["ln4", "l1ffnA", "l1ffnB"]}
    if isinstance(which, (list, tuple)):
        phases = list(which)
        fused = False
    else:
        phases = segs[which] if not fused else segs["A"] + segs["B"] + segs["C"] + segs["D"]

    need = set(phases)

    if fused:
        T["rin"] = din("rin", (TOK, D))
        T["ra"] = dint("Ra", (TOK, D))
        T["rb"] = dint("Rb", (TOK, D))
        T["rout"] = dout("out", (TOK, D))
        resid = {"ln1": ("rin", None), "l0oproj": ("rin", "ra"),
                 "ln2": ("ra", None), "l0dense": ("ra", "rb"),
                 "ln3": ("rb", None), "l1oproj": ("rb", "ra"),
                 "ln4": ("ra", None), "l1ffnB": ("ra", "rout")}
    else:
        T["rin"] = din("rin", (TOK, D))
        T["rout"] = dout("out", (TOK, D))
        resid = {}
        for p in ["ln1", "ln2", "ln3", "ln4"]:
            resid[p] = ("rin", None)
        for p in ["l0oproj", "l0dense", "l1oproj", "l1ffnB"]:
            resid[p] = ("rin", "rout")

    T["ident_d"] = din("ident", (128, 128))
    T["ones_d"] = din("ones", (1, 1024))

    if "l0kvq" in need:
        T["wq0"] = din("wq0", (4, 128, 4096)); T["wk0"] = din("wk0", (4, 128, 4096))
        T["wv0"] = din("wv0", (8, 128, 2048))
        T["qb0"] = din("qb0", (1, 512)); T["kb0"] = din("kb0", (1, 512)); T["vb0"] = din("vb0", (1, 512))
        T["cc0k_in"] = dint("cc0k_in", (128, 4096), BF16)
        T["cc0k_out"] = dint("cc0k_out", (256, 4096), BF16)
        T["cc0v_in"] = dint("cc0v_in", (128, 4096), BF16)
        T["cc0v_out"] = dint("cc0v_out", (256, 4096), BF16)
    if "l0oproj" in need:
        T["wo0"] = din("wo0", (8, 1, 128, 2048))
        T["ob0"] = din("ob0", (1, D))
    if "l0sel" in need:
        T["cent"] = din("cent", (128, 2048))
        T["assign"] = din("assign", (1, D), I32)
        T["iota64_d"] = din("iota64", (64, 1))
        T["hits_in"] = dint("hits_in", (64, 1))
        T["hits_out"] = dint("hits_out", (64, 1))
    if "l0dense" in need:
        T["wf"] = din("wf", (8, 8, 128, 2048))
        T["fb"] = din("fb", (1, D))
    if "ln3" in need or "l1kv" in need:
        T["cc1_in"] = dint("cc1_in", (256, 16384), BF16)
        T["cc1_out"] = dint("cc1_out", (512, 16384), BF16)
    if "l1kv" in need:
        T["wk1"] = din("wk1", (4, 128, 4096)); T["wv1"] = din("wv1", (8, 128, 2048))
    if "l1groups" in need:
        T["wq1"] = din("wq1", (16, 128, 4096))
        T["cmask_d"] = din("cmask", (128, 2048))
        for g in range(4):
            T[f"cc2_in{g}"] = dint(f"cc2_in{g}", (128, 8192), BF16)
            T[f"cc2_out{g}"] = dint(f"cc2_out{g}", (256, 8192), BF16)
    if "l1oproj" in need:
        T["wo1"] = din("wo1", (8, 8, 128, 2048))
        T["blend_d"] = din("blend", (128, 2))
    if "l1ffnA" in need:
        T["wg"] = din("wg", (86, 128, 4096)); T["wu"] = din("wu", (86, 128, 4096))
        T["bg"] = din("bg", (1, 11008)); T["bu"] = din("bu", (1, 11008))
        T["m_d"] = dint("m_d", (86 * 128, 1024), BF16)
    if "l1ffnB" in need:
        if "m_d" not in T:
            T["m_d"] = dint("m_d", (86 * 128, 1024), BF16)
        T["wd"] = din("wd", (8, 86, 128, 512))
        T["bd"] = din("bd", (1, D))

    with stack:
        ar = Arena(nc, stack, ARENA_LEN)
        e = stack.enter_context
        sb = {}
        sb["ident"] = e(nc.sbuf_tensor("identsb", [128, 128], BF16))
        sb["ones"] = e(nc.sbuf_tensor("onessb", [1, 1024], BF16))
        sb["stat"] = e(nc.sbuf_tensor("stat", [128, 32], F32))
        sb["qsel"] = e(nc.sbuf_tensor("qselsb", [128, 8], F32))
        sb["blend"] = e(nc.sbuf_tensor("blendsb", [128, 2], F32))
        sb["iota"] = e(nc.sbuf_tensor("iotasb", [64, 1], F32))
        sb["c64a"] = e(nc.sbuf_tensor("c64a", [64, 2], F32))
        sb["c64bf"] = e(nc.sbuf_tensor("c64bf", [64, 1], BF16))
        psA = e(nc.psum_tensor("psA", [128, 2048], F32))
        psB = e(nc.psum_tensor("psB", [128, 2048], F32))

        P = Plan(nc, stack)
        sb["rows"] = ar.t[0:1, KT_O:KT_O + 11008]
        sb["rows2"] = ar.t[0:1, KT_O + 11008:KT_O + 22016]
        sb["rows3"] = ar.t[0:1, KT_O + 22016:KT_O + 26112]
        sb["cmask"] = ar.t[:, W0_O:W0_O + 4096].bitcast(F32)
        sb["onehot"] = ar.t[0:64, ATT_O:ATT_O + 4096]
        sb["arowi"] = ar.t[0:1, KT_O + 1024:KT_O + 9216].bitcast(I32)
        sb["arow"] = ar.t[0:1, KT_O + 9216:KT_O + 17408].bitcast(F32)
        sb["arbf"] = ar.t[0:1, V_O + 1024:V_O + 5120]
        T["sb"] = sb
        T["ar"] = ar
        T["psA"] = psA
        T["psB"] = psB

        def bank(i):
            return (psA if i < 4 else psB)[:, (i % 4) * 512:(i % 4) * 512 + 512]

        def bankbf(i):
            return (psA if i < 4 else psB)[:].bitcast(BF16)[:, (i % 4) * 1024:(i % 4) * 1024 + 1024]

        T["bank"] = bank
        T["bankbf"] = bankbf

        P.dma("gpsimd", sb["ident"][:], T["ident_d"], "gd")
        P.dma("gpsimd", sb["ones"][:], T["ones_d"], "gd")
        if "l1oproj" in need:
            P.dma("sync", sb["blend"][:], T["blend_d"], "sd")
        if "l0sel" in need:
            P.dma("sync", sb["iota"][:], T["iota64_d"], "sd")
        pre_gd = P.counts["gd"]
        pre_sd = P.counts["sd"]
        for eng in ("tensor", "vector", "scalar"):
            P.wait(eng, "gd", pre_gd)
            P.wait(eng, "sd", pre_sd)
        P.wait("gpsimd", "sd", pre_sd)
        P.wait("sync", "gd", pre_gd)

        bar_n = [0]

        def barrier():
            bar_n[0] += 1
            P.wait("sync", "sd", P.counts["sd"])
            P.wait("gpsimd", "gd", P.counts["gd"])
            for nm, cnt in list(P.counts.items()):
                if (nm.startswith("mS") or nm.startswith("wS")) and cnt:
                    P.wait("sync" if nm.startswith("mS") else "gpsimd", nm, cnt)
            vals = {}
            for eng in ALL_ENG:
                vals[eng] = P.barrier_mark(eng, "bar_" + eng)
            for eng in ALL_ENG:
                for oth in ALL_ENG:
                    if oth != eng:
                        P.wait(eng, "bar_" + oth, vals[oth])
        T["barrier"] = barrier

        for ph in phases:
            emit_phase(P, T, ph, resid.get(ph), fused)
            barrier()

        P.wait("sync", "sd", P.counts["sd"])
        P.wait("gpsimd", "gd", P.counts["gd"])

        with nc.Block() as block:
            @block.tensor
            def _(t):
                for fn, _ in P.streams["tensor"]:
                    fn(t)

            @block.vector
            def _(v):
                for fn, _ in P.streams["vector"]:
                    fn(v)

            @block.scalar
            def _(s):
                for fn, _ in P.streams["scalar"]:
                    fn(s)

            @block.sync
            def _(s):
                for fn, _ in P.streams["sync"]:
                    fn(s)

            @block.gpsimd
            def _(g):
                for fn, _ in P.streams["gpsimd"]:
                    fn(g)

    return nc



def _xnt3(T):
    return T["ar"].t[:, XNT_O:XNT_O + 32768].rearrange("p (c u) -> p c u", u=1024)


def emit_transposes(P, T, pairs, setA=(0, 1, 2, 3), setB=(4, 5, 6, 7), evac_sname="dv"):
    bankbf = T["bankbf"]
    sb = T["sb"]
    sets = [list(setA), list(setB)]
    P.wait("tensor", evac_sname, P.counts[evac_sname])
    last_ev = [0, 0]
    vev = 0
    i = 0
    rnd = 0
    while i < len(pairs):
        s = rnd % 2
        banks = sets[s]
        group = pairs[i:i + len(banks)]
        if last_ev[s]:
            P.wait("tensor", evac_sname, last_ev[s])
        for j, (srcap, dstap) in enumerate(group):
            P.op("tensor", lambda e, srcap=srcap, bk=banks[j]:
                 e.transpose(bankbf(bk)[:, 0:128], srcap, sb["ident"][:]))
        vtr = P.mark("tensor", "pe")
        P.wait("vector", "pe", vtr)
        for j, (srcap, dstap) in enumerate(group):
            if j < len(group) - 1:
                P.op("vector", lambda e, dstap=dstap, bk=banks[j]:
                     e.tensor_copy(dstap, bankbf(bk)[:, 0:128]))
            else:
                vev = P.op("vector", lambda e, dstap=dstap, bk=banks[j]:
                           e.tensor_copy(dstap, bankbf(bk)[:, 0:128]), evac_sname)
        last_ev[s] = vev
        i += len(banks)
        rnd += 1
    return vev


def emit_ln(P, T, src_name):
    import os
    depth = int(os.environ.get("LN_DEPTH", "4"))
    ar, sb, bank, bankbf = T["ar"], T["sb"], T["bank"], T["bankbf"]
    src = T[src_name]
    st = lambda i: sb["stat"][:, i:i + 1]
    xnt = _xnt3(T)
    rbuf = [ar.view(R0_O, 4096, F32), ar.view(R1_O, 4096, F32)]
    sq = ar.view(KT_O, 4096, F32)
    xbf = ar.view(W0_O, 4096, BF16)

    P.op("vector", lambda e: e.memset(st(7), 1e-5))
    P.op("vector", lambda e: e.drain())
    last_xn_ac = [0, 0]

    for t in range(8):
        bi = t % 2
        if last_xn_ac[bi]:
            P.wait("sync", "ac", last_xn_ac[bi])
        vload = P.dma("sync", rbuf[bi][:], src[t * 128:(t + 1) * 128, :], "sd")
        rt = rbuf[bi]
        P.wait("vector", "sd", vload)
        P.op("vector", lambda e, rt=rt: e.reduce_sum(st(0), rt[:], axis=AX.X))
        P.op("vector", lambda e: e.drain())
        P.op("vector", lambda e: e.tensor_scalar(st(1), st(0), 1.0 / D, None, OP.mult))
        P.op("vector", lambda e: e.drain())
        P.wait("scalar", "sd", vload)
        vsq = P.op("scalar", lambda e, rt=rt: e.activation(sq[:], rt[:], AF.Square, accum_out=st(2)), "ac")
        P.wait("vector", "ac", vsq)
        P.op("vector", lambda e: e.tensor_tensor(st(3), st(1), st(1), OP.mult))
        P.op("vector", lambda e: e.drain())
        vvar = P.op("vector", lambda e: e.scalar_tensor_tensor(st(4), st(2), 1.0 / D, st(3), OP.mult, OP.subtract), "dv")
        P.wait("scalar", "dv", vvar)
        vsqv = P.op("scalar", lambda e: e.activation(st(5), st(4), AF.Sqrt, bias=st(7)), "ac")
        P.wait("vector", "ac", vsqv)
        P.op("vector", lambda e: e.reciprocal(st(6), st(5)))
        P.op("vector", lambda e: e.drain())
        vnb = P.op("vector", lambda e: e.tensor_scalar(st(8), st(1), -1.0, st(6), OP.mult, OP.mult), "dv")
        if depth < 2:
            last_xn_ac[bi] = vsq
            continue
        P.wait("scalar", "dv", vnb)
        P.wait("scalar", "pe", P.counts["pe"])
        vxn = P.op("scalar", lambda e, rt=rt: e.activation(xbf[:], rt[:], AF.Identity, bias=st(8), scale=st(6)), "ac")
        last_xn_ac[bi] = vxn
        if depth < 3:
            continue
        P.wait("tensor", "ac", vxn)
        pairs = [(xbf[:, c * 128:(c + 1) * 128], xnt[:, c, t * 128:t * 128 + 128])
                 for c in range(32)]
        emit_transposes(P, T, pairs)


def emit_l0kvq(P, T):
    ar, sb, bank, bankbf = T["ar"], T["sb"], T["bank"], T["bankbf"]
    xnt = _xnt3(T)
    wbuf = [ar.view(W0_O, 4096, BF16), ar.view(W1_O, 4096, BF16)]
    qtmp = ar.view(Q_O, 4096, BF16)
    vtmp = ar.view(Q_O + 4096, 4096, BF16)
    rows = sb["rows"]
    rows2 = sb["rows2"]
    rows3 = sb["rows3"]
    P.dma("gpsimd", rows[0:1, 0:512], T["kb0"], "gd")
    P.dma("gpsimd", rows2[0:1, 0:512], T["qb0"], "gd")
    P.dma("gpsimd", rows3[0:1, 0:512], T["vb0"], "gd")
    vrows = P.counts["gd"]

    def proj_fm(wd_dram, nchunk, brow, dst_view, dst_off_fn, sname_stage, init_lastuse=(0, 0)):
        stage_dmas = []
        lastuse = list(init_lastuse)
        P.wait("tensor", "ac", P.counts["ac"])
        for kc in range(nchunk):
            wb = kc % 2
            if lastuse[wb]:
                P.wait("gpsimd", "pe", lastuse[wb])
            vw = P.dma("gpsimd", wbuf[wb][:], wd_dram[kc], "gd")
            P.wait("tensor", "gd", vw)
            P.wait("tensor", "gd", vrows)
            for th in range(2):
                bk = (kc * 2 + th) % 4
                P.op("tensor", lambda e, bk=bk, kc=kc, brow=brow, th=th:
                     e.matmul(bank(bk), brow[0:1, kc * 128:kc * 128 + 128],
                              sb["ones"][0:1, th * 512:th * 512 + 512], start=True, stop=False))
                for f in range(32):
                    P.op("tensor", lambda e, bk=bk, f=f, th=th, wb=wb:
                         e.matmul(bank(bk), wbuf[wb][:, f * 128:(f + 1) * 128],
                                  xnt[:, f, th * 512:th * 512 + 512], start=False, stop=(f == 31)))
                vmm = P.mark("tensor", "pe")
                if th == 1:
                    lastuse[wb] = vmm
                P.wait("vector", "pe", vmm)
                vev = P.op("vector", lambda e, bk=bk, kc=kc, th=th, dst_view=dst_view:
                           e.tensor_copy(dst_view[:, kc * 1024 + th * 512: kc * 1024 + th * 512 + 512],
                                         bank(bk)), "ac")
            if sname_stage is not None:
                P.wait("sync", "ac", P.counts["ac"])
                stage_dmas.append(P.dma("sync", T[sname_stage][:, kc * 1024:(kc + 1) * 1024],
                                        dst_view[:, kc * 1024:(kc + 1) * 1024], "sd"))
        return lastuse

    luK = proj_fm(T["wk0"], 4, rows, qtmp, None, "cc0k_in")

    P.wait("tensor", "ac", P.counts["ac"])
    lastw = list(luK)
    for l in range(8):
        wb = l % 2
        if lastw[wb]:
            P.wait("gpsimd", "pe", lastw[wb])
        vw = P.dma("gpsimd", wbuf[wb][:, 0:2048], T["wv0"][l], "gd")
        P.wait("tensor", "gd", vw)
        for t in range(8):
            if l == 0:
                P.op("tensor", lambda e, t=t:
                     e.matmul(bank(t), sb["ones"][0:1, 0:128],
                              rows3[0:1, 0:512], start=True, stop=False))
            for fc in range(4):
                f = l * 4 + fc
                P.op("tensor", lambda e, t=t, f=f, fc=fc, wb=wb:
                     e.matmul(bank(t), xnt[:, f, t * 128:t * 128 + 128],
                              wbuf[wb][:, fc * 512:fc * 512 + 512], start=False, stop=(f == 31)))
        lastw[wb] = P.mark("tensor", "pe")
    vvdone = P.counts["pe"]
    stv = []
    for t in range(8):
        P.wait("scalar", "pe", vvdone)
        P.op("scalar", lambda e, t=t: e.activation(vtmp[:, t * 512:(t + 1) * 512], bank(t), AF.Identity), "ac")
    P.wait("sync", "ac", P.counts["ac"])
    P.dma("sync", T["cc0v_in"][:, :], vtmp[:], "sd")
    vstage = P.counts["sd"]

    P.wait("gpsimd", "sd", vstage)
    vk = P.collective("AllGather", OP.bypass, PAIR_GROUPS, T["cc0k_in"].opt(), T["cc0k_out"].opt(), "cck")
    vv = P.collective("AllGather", OP.bypass, PAIR_GROUPS, T["cc0v_in"].opt(), T["cc0v_out"].opt(), "ccv")

    import os as _os
    if not _os.environ.get("KVQ_NOQ"):
        P.wait("vector", "sd", vstage)
        qdst = ar.view(Q_O, 4096, BF16)
        proj_fm(T["wq0"], 4, rows2, qdst, None, None, init_lastuse=lastw)

    P.wait("sync", "pe", P.counts["pe"])
    P.wait("sync", "cck", vk)
    kvk = ar.t[:, KT_O:KT_O + 8192].rearrange("p (kc u) -> p kc u", u=2048)
    for s in range(2):
        P.dma("sync", kvk[:, :, 1024 * s:1024 * s + 1024],
              T["cc0k_out"][128 * s:128 * s + 128, :].rearrange("p (kc u) -> p kc u", u=1024), "sd")
    P.wait("sync", "ccv", vv)
    for s in range(2):
        P.dma("sync", ar.t[:, V_O + 4096 * s:V_O + 4096 * s + 4096],
              T["cc0v_out"][128 * s:128 * s + 128, :], "sd")
    vready = P.counts["sd"]
    P.wait("tensor", "sd", vready)
    P.wait("vector", "sd", vready)
    P.wait("scalar", "sd", vready)


def emit_attention_pipe(P, T, iters):
    ar, sb, bankbf = T["ar"], T["sb"], T["bankbf"]
    st = lambda i: sb["stat"][:, i:i + 1]
    pbufs = [ar.view(P_O, 2048, BF16), ar.view(PT_O, 2048, BF16)]
    ptbufs = [ar.view(W1_O, 2048, BF16), ar.view(W1_O + 2048, 2048, BF16)]
    sets = [(0, 1, 2, 3), (4, 5, 6, 7)]
    setfree = [0, 0]
    trdone = [0, 0]
    pool_v = {}
    score_v = {}

    def S_of(par):
        return T["psA"] if par == 0 else T["psB"]

    def front_pe(i, it):
        par = i % 2
        S = S_of(par)
        if setfree[par]:
            P.wait("tensor", "dv", setfree[par])
        for kb in range(it["nkb"]):
            P.op("tensor", lambda e, kb=kb, it=it, S=S:
                 e.matmul(S[:, kb * 512:kb * 512 + 512], it["qs"], it["ks"](kb),
                          start=True, stop=True))
        score_v[i] = P.mark("tensor", "pe")

    def front_sm(i, it):
        par = i % 2
        S = S_of(par)
        ext = it["nkb"] * 512
        P.wait("vector", "pe", score_v[i])
        if it["mask"] is not None:
            dk, r = it["mask"]
            P.op("vector", lambda e, dk=dk, r=r, S=S:
                 e.tensor_tensor(S[:, dk * 512:dk * 512 + 512], S[:, dk * 512:dk * 512 + 512],
                                 sb["cmask"][:, r * 512:r * 512 + 512], OP.add))
            P.op("vector", lambda e: e.drain())
        P.op("vector", lambda e, S=S, ext=ext, par=par:
             e.reduce_max(st(16 + par), S[:, 0:ext], axis=AX.X))
        P.op("vector", lambda e: e.drain())
        vmx = P.op("vector", lambda e, it=it, par=par:
                   e.tensor_scalar(st(18 + par), st(16 + par), -it["c"], None, OP.mult), "dv")
        P.wait("scalar", "dv", vmx)
        if trdone[par]:
            P.wait("scalar", "pe", trdone[par])
        vexp = P.op("scalar", lambda e, S=S, ext=ext, par=par, it=it:
                    e.activation(pbufs[par][:, 0:ext], S[:, 0:ext], AF.Exp,
                                 scale=it["c"], bias=st(18 + par), accum_out=st(20 + par)), "ac")
        P.wait("vector", "ac", vexp)
        P.op("vector", lambda e, par=par: e.reciprocal(st(22 + par), st(20 + par)))
        P.op("vector", lambda e: e.drain())
        vrc = P.mark("vector", "dv")
        P.wait("gpsimd", "dv", vrc)
        pool_v[i] = P.op("gpsimd", lambda e, par=par, ext=ext, it=it:
                         e.tensor_scalar(pbufs[par][:, 0:ext], pbufs[par][:, 0:ext],
                                         st(22 + par), it["postmul"], OP.mult, OP.mult), "gp")

    def do_tail(i, it):
        par = i % 2
        S = S_of(par)
        set_ = sets[par]
        P.wait("tensor", "gp", pool_v[i])
        nkt = it["nkt"]
        halfs = [(set_[0], set_[1]), (set_[2], set_[3])]
        last_ev = [setfree[par], setfree[par]]
        c = 0
        rnd = 0
        while c < nkt:
            hs = halfs[rnd % 2]
            group = list(range(c, min(nkt, c + 2)))
            if last_ev[rnd % 2]:
                P.wait("tensor", "dv", last_ev[rnd % 2])
            for j, cc in enumerate(group):
                P.op("tensor", lambda e, cc=cc, bk=hs[j], par=par:
                     e.transpose(bankbf(bk)[:, 0:128], pbufs[par][:, cc * 128:cc * 128 + 128],
                                 sb["ident"][:]))
            vtr = P.mark("tensor", "pe")
            P.wait("vector", "pe", vtr)
            vev = 0
            for j, cc in enumerate(group):
                vev = P.op("vector", lambda e, cc=cc, bk=hs[j], par=par:
                           e.tensor_copy(ptbufs[par][:, cc * 128:cc * 128 + 128],
                                         bankbf(bk)[:, 0:128]), "dv")
            last_ev[rnd % 2] = vev
            c += len(group)
            rnd += 1
        trdone[par] = P.counts["pe"]
        P.wait("tensor", "dv", max(last_ev))
        po, pl = it.get("avpart", (0, 128))
        for kt in range(nkt):
            P.op("tensor", lambda e, kt=kt, it=it, S=S, po=po, pl=pl, par=par:
                 e.matmul(S[po:po + pl, 0:128], it["vs"](kt),
                          ptbufs[par][:, kt * 128:kt * 128 + 128],
                          start=(kt == 0), stop=(kt == nkt - 1)))
        vav = P.mark("tensor", "pe")
        P.wait("vector", "pe", vav)
        vev = P.op("vector", lambda e, it=it, S=S, po=po, pl=pl:
                   e.tensor_copy(it["dst"], S[po:po + pl, 0:128]), "dv")
        setfree[par] = vev

    for i, it in enumerate(iters):
        front_pe(i, it)
        if i > 0:
            do_tail(i - 1, iters[i - 1])
        front_sm(i, it)
    do_tail(len(iters) - 1, iters[-1])


def emit_l0attn(P, T):
    ar, sb = T["ar"], T["sb"]
    q3 = ar.t[:, Q_O:Q_O + 4096].rearrange("p (hc u) -> p hc u", u=1024)
    k3 = ar.t[:, KT_O:KT_O + 8192].rearrange("p (kc u) -> p kc u", u=2048)
    v3 = ar.t[:, V_O:V_O + 8192].rearrange("p (kt u) -> p kt u", u=512)
    att3 = ar.t[:, ATT_O:ATT_O + 4096].rearrange("p (hc u) -> p hc u", u=1024)
    ISQ = 1.0 / np.sqrt(512.0)
    iters = []
    for t in range(8):
        for h in range(8):
            hc, ho = h // 2, 64 * (h % 2)
            iters.append(dict(
                qs=q3[ho:ho + 64, hc, t * 128:t * 128 + 128],
                ks=(lambda kb, hc=hc, ho=ho: k3[ho:ho + 64, hc, kb * 512:kb * 512 + 512]),
                nkb=4,
                vs=(lambda kt, h=h: v3[:, kt, h * 64:h * 64 + 64]),
                nkt=16,
                mask=None, c=1.0, postmul=ISQ,
                avpart=(ho, 64),
                dst=att3[ho:ho + 64, hc, t * 128:t * 128 + 128],
            ))
    emit_attention_pipe(P, T, iters)


def emit_bigffn(P, T, w_dram, nloads, chunks_per_load, lhs3, nchunk, brow_bf, resid_pair,
                evac_fn=None, wtile_w=2048):
    ar, sb, bank = T["ar"], T["sb"], T["bank"]
    rin, rout = resid_pair
    wbuf = [ar.view(W0_O, wtile_w, BF16), ar.view(W1_O, wtile_w, BF16)]
    rload = [ar.view(R0_O, 1024, F32), ar.view(R0_O + 2048, 1024, F32)]
    rst = [ar.view(R1_O, 1024, F32), ar.view(R1_O + 2048, 1024, F32)]
    lastw = [0, 0]
    lastr = [0, 0]
    lastst = [0, 0]

    for ob in range(8):
        for l in range(nloads):
            wb = l % 2
            if lastw[wb]:
                P.wait("gpsimd", "pe", lastw[wb])
            vw = P.dma("gpsimd", wbuf[wb][:, 0:wtile_w], w_dram[ob, l], "gd")
            P.wait("tensor", "gd", vw)
            for t in range(8):
                if l == 0 and brow_bf is not None:
                    P.op("tensor", lambda e, t=t, ob=ob, brow_bf=brow_bf:
                         e.matmul(bank(t), sb["ones"][0:1, 0:128],
                                  brow_bf[0:1, ob * 512:ob * 512 + 512], start=True, stop=False))
                for fc in range(chunks_per_load):
                    k = l * chunks_per_load + fc
                    st_flag = (l == 0 and fc == 0 and brow_bf is None)
                    sp_flag = (k == nchunk - 1)
                    P.op("tensor", lambda e, t=t, k=k, fc=fc, wb=wb, st_flag=st_flag, sp_flag=sp_flag:
                         e.matmul(bank(t), lhs3[:, k, t * 128:t * 128 + 128],
                                  wbuf[wb][:, fc * 512:fc * 512 + 512],
                                  start=st_flag, stop=sp_flag))
            lastw[wb] = P.mark("tensor", "pe")
        vmm = lastw[(nloads - 1) % 2]
        for t in range(8):
            sl = t % 2
            if lastr[sl]:
                P.wait("sync", "dv", lastr[sl])
            if lastst[sl]:
                P.wait("sync", "sd", lastst[sl])
            vr = P.dma("sync", rload[sl][:, 0:512], rin[t * 128:t * 128 + 128, ob * 512:ob * 512 + 512], "sd")
            P.wait("vector", "pe", vmm)
            P.wait("vector", "sd", vr)
            if lastst[sl]:
                P.wait("vector", "sd", lastst[sl])
            if evac_fn is None:
                vd = P.op("vector", lambda e, t=t, sl=sl:
                          e.tensor_tensor(rst[sl][:, 0:512], bank(t), rload[sl][:, 0:512], OP.add), "dv")
            else:
                vd = evac_fn(P, T, t, ob, rload[sl][:, 0:512], rst[sl][:, 0:512])
            lastr[sl] = vd
            P.wait("sync", "dv", vd)
            lastst[sl] = P.dma("sync", rout[t * 128:t * 128 + 128, ob * 512:ob * 512 + 512],
                               rst[sl][:, 0:512], "sd")
        P.wait("tensor", "dv", P.counts["dv"])


def emit_l0oproj(P, T, resid_pair):
    ar, sb = T["ar"], T["sb"]
    P.dma("gpsimd", sb["rows3"][0:1, 0:4096], T["ob0"], "gd")
    att3 = T["ar"].t[:, ATT_O:ATT_O + 4096].rearrange("p (hc u) -> p hc u", u=1024)
    emit_bigffn(P, T, T["wo0"], 1, 4, att3, 4, sb["rows3"], resid_pair)


def emit_l0sel(P, T):
    ar, sb, bank, bankbf = T["ar"], T["sb"], T["bank"], T["bankbf"]
    st = lambda i: sb["stat"][:, i:i + 1]
    xnt = _xnt3(T)
    centbf = ar.view(W0_O, 2048, BF16)
    hitsbf = ar.view(W1_O, 64, BF16)
    onescol = ar.view(W1_O + 1024, 1, BF16)
    E = ar.view(KT_O, 64, F32)
    P.dma("gpsimd", centbf[:], T["cent"], "gd")
    vc = P.counts["gd"]
    va = P.dma("sync", sb["arowi"][:], T["assign"], "sd")
    P.wait("vector", "sd", va)
    P.op("vector", lambda e: e.tensor_copy(sb["arow"][:], sb["arowi"][:]))
    P.op("vector", lambda e: e.drain())
    vabf = P.op("vector", lambda e: e.tensor_copy(sb["arbf"][:], sb["arow"][:]), "dv")
    P.op("vector", lambda e: e.memset(onescol[:], 1.0))
    P.op("vector", lambda e: e.drain())
    vones = P.mark("vector", "dv")
    P.wait("tensor", "dv", vabf)
    for jb in range(8):
        P.wait("tensor", "dv", P.counts["dv"])
        P.op("tensor", lambda e, jb=jb:
             e.matmul(bank(1)[0:64, 0:512], sb["ones"][0:1, 0:64],
                      sb["arbf"][0:1, jb * 512:jb * 512 + 512], start=True, stop=True))
        vb = P.mark("tensor", "pe")
        P.wait("vector", "pe", vb)
        P.op("vector", lambda e, jb=jb:
             e.tensor_scalar(sb["onehot"][:, jb * 512:jb * 512 + 512], bank(1)[0:64, :],
                             sb["iota"][:], None, OP.is_equal), "dv")
    voh = P.counts["dv"]
    P.wait("tensor", "gd", vc)
    for t in range(8):
        P.wait("tensor", "ac", P.counts["ac"])
        for f in range(32):
            P.op("tensor", lambda e, t=t, f=f:
                 e.matmul(bank(0)[:, 0:64], xnt[:, f, t * 128:t * 128 + 128],
                          centbf[:, f * 64:f * 64 + 64], start=(f == 0), stop=(f == 31)))
        vd = P.mark("tensor", "pe")
        P.wait("vector", "pe", vd)
        vmax = P.op("vector", lambda e: e.reduce_max(st(0), bank(0)[:, 0:64], axis=AX.X, negate=True), "dv")
        P.op("vector", lambda e: e.drain())
        vm10 = P.op("vector", lambda e: e.tensor_scalar(st(1), st(0), 10.0, None, OP.mult), "dv")
        P.wait("scalar", "dv", vm10)
        vexp = P.op("scalar", lambda e: e.activation(E[:, 0:64], bank(0)[:, 0:64], AF.Exp,
                                                     scale=10.0, bias=st(1), accum_out=st(2)), "ac")
        P.wait("vector", "ac", vexp)
        P.op("vector", lambda e, t=t: e.tensor_scalar(sb["qsel"][:, t:t + 1], st(2), 100.0, None, OP.is_lt))
        P.op("vector", lambda e: e.tensor_scalar(st(3), st(2), 0.01, None, OP.mult))
        P.op("vector", lambda e: e.drain())
        vh = P.op("vector", lambda e: e.tensor_scalar(hitsbf[:], E[:, 0:64], st(3), None, OP.is_gt), "dv")
        P.wait("tensor", "dv", vh)
        P.wait("tensor", "dv", vones)
        P.op("tensor", lambda e, t=t:
             e.matmul(bank(2)[0:64, 0:1], hitsbf[:], onescol[:], start=(t == 0), stop=(t == 7)))
    vcnt = P.mark("tensor", "pe")
    P.wait("vector", "pe", vcnt)
    vc64 = P.op("vector", lambda e: e.tensor_copy(sb["c64a"][0:64, 0:1], bank(2)[0:64, 0:1]), "dv")
    P.wait("sync", "dv", vc64)
    vst = P.dma("sync", T["hits_in"][:, :], sb["c64a"][0:64, 0:1], "sd")
    P.wait("gpsimd", "sd", vst)
    vcc = P.collective("AllReduce", OP.add, ALL_GROUPS, T["hits_in"].opt(), T["hits_out"].opt(), "cchits")
    P.wait("sync", "cchits", vcc)
    vrb = P.dma("sync", sb["c64a"][0:64, 1:2], T["hits_out"][:, :], "sd")
    P.wait("vector", "sd", vrb)
    P.op("vector", lambda e: e.tensor_scalar(sb["c64bf"][:], sb["c64a"][0:64, 1:2], 0.0, None, OP.is_gt))
    P.op("vector", lambda e: e.drain())
    vsel = P.mark("vector", "dv")
    rselbc = ar.view(Q_O, 4096, BF16)
    rrow = sb["rows"]
    P.wait("tensor", "dv", vsel)
    P.wait("tensor", "dv", voh)
    for jb in range(8):
        P.wait("tensor", "dv", P.counts["dv"])
        P.op("tensor", lambda e, jb=jb:
             e.matmul(bank(3)[0:1, 0:512], sb["c64bf"][:], sb["onehot"][:, jb * 512:jb * 512 + 512],
                      start=True, stop=True))
        vr = P.mark("tensor", "pe")
        P.wait("vector", "pe", vr)
        vcp = P.op("vector", lambda e, jb=jb:
                   e.tensor_copy(rrow[0:1, jb * 512:jb * 512 + 512], bank(3)[0:1, 0:512]), "dv")
        P.wait("tensor", "dv", vcp)
        P.op("tensor", lambda e, jb=jb:
             e.matmul(bank(4)[:, 0:512], sb["ones"][0:1, 0:128], rrow[0:1, jb * 512:jb * 512 + 512],
                      start=True, stop=True))
        vbb = P.mark("tensor", "pe")
        P.wait("vector", "pe", vbb)
        P.op("vector", lambda e, jb=jb:
             e.tensor_copy(rselbc[:, jb * 512:jb * 512 + 512], bank(4)[:, 0:512]), "dv")


def emit_l0dense(P, T, resid_pair):
    ar, sb = T["ar"], T["sb"]
    P.dma("gpsimd", sb["rows3"][0:1, 0:4096], T["fb"], "gd")
    xnt = _xnt3(T)
    rselbc = ar.view(Q_O, 4096, BF16)

    def evac(P, T, t, ob, rslice, dst):
        v1 = P.op("vector", lambda e, t=t, ob=ob, dst=dst:
                  e.scalar_tensor_tensor(dst, T["bank"](t), sb["qsel"][:, t:t + 1],
                                         rselbc[:, ob * 512:ob * 512 + 512], OP.mult, OP.mult), "dv")
        v2 = P.op("vector", lambda e, dst=dst, rslice=rslice:
                  e.tensor_tensor(dst, dst, rslice, OP.add), "dv")
        return v2

    emit_bigffn(P, T, T["wf"], 8, 4, xnt, 32, sb["rows3"], resid_pair, evac_fn=evac)


def emit_ln3_stage(P, T):
    ar = T["ar"]
    xnt = _xnt3(T)
    P.wait("sync", "dv", P.counts["dv"])
    for b in range(2):
        P.dma("sync", T["cc1_in"][128 * b:128 * b + 128, :],
              xnt[:, :, 512 * b:512 * b + 512], "sd")
    vs = P.counts["sd"]
    P.wait("gpsimd", "sd", vs)
    T["v_cc1"] = P.collective("AllGather", OP.bypass, PAIR_GROUPS,
                              T["cc1_in"].opt(), T["cc1_out"].opt(), "cc1")


CC1_ROW = {0: 0, 1: 256, 2: 384, 3: 128}


def emit_l1kv(P, T):
    ar, sb, bank = T["ar"], T["sb"], T["bank"]
    P.wait("sync", "cc1", T["v_cc1"])
    xnb = ar.t[:, XNT_O:XNT_O + 16384].rearrange("p (f u) -> p f u", u=512)
    wres = ar.t[:, XNT_O + 16384:XNT_O + 32768]
    k3 = ar.t[:, KT_O:KT_O + 8192].rearrange("p (g u) -> p g u", u=2048)
    v2 = ar.t[:, V_O:V_O + 8192]

    for gk in range(4):
        P.dma("gpsimd", wres[:, gk * 4096:(gk + 1) * 4096], T["wk1"][gk], "gd")
    vwk = P.counts["gd"]
    P.wait("tensor", "gd", vwk)
    lastpe = P.counts["pe"]
    for tb in range(4):
        P.wait("sync", "pe", lastpe)
        vx = P.dma("sync", ar.t[:, XNT_O:XNT_O + 16384],
                   T["cc1_out"][CC1_ROW[tb]:CC1_ROW[tb] + 128, :], "sd")
        P.wait("tensor", "sd", vx)
        for gk in range(4):
            for f in range(32):
                P.op("tensor", lambda e, gk=gk, f=f:
                     e.matmul(bank(gk), wres[:, gk * 4096 + f * 128: gk * 4096 + (f + 1) * 128],
                              xnb[:, f, :], start=(f == 0), stop=(f == 31)))
        lastpe = P.mark("tensor", "pe")
        P.wait("scalar", "pe", lastpe)
        for gk in range(4):
            P.op("scalar", lambda e, gk=gk, tb=tb:
                 e.activation(k3[:, gk, tb * 512:tb * 512 + 512], bank(gk), AF.Identity), "ac")
        P.wait("tensor", "ac", P.counts["ac"])
    P.wait("gpsimd", "ac", P.counts["ac"])
    P.wait("gpsimd", "pe", lastpe)
    for l in range(8):
        P.dma("gpsimd", wres[:, l * 2048:(l + 1) * 2048], T["wv1"][l], "gd")
    vwv = P.counts["gd"]
    P.wait("tensor", "gd", vwv)
    for tb in range(4):
        P.wait("sync", "pe", lastpe)
        vx = P.dma("sync", ar.t[:, XNT_O:XNT_O + 16384],
                   T["cc1_out"][CC1_ROW[tb]:CC1_ROW[tb] + 128, :], "sd")
        P.wait("tensor", "sd", vx)
        for tt in range(4):
            for f in range(32):
                P.op("tensor", lambda e, tt=tt, f=f:
                     e.matmul(bank(4 + tt), xnb[:, f, tt * 128:tt * 128 + 128],
                              wres[:, f * 512:(f + 1) * 512], start=(f == 0), stop=(f == 31)))
        lastpe = P.mark("tensor", "pe")
        P.wait("scalar", "pe", lastpe)
        for tt in range(4):
            kt = tb * 4 + tt
            P.op("scalar", lambda e, tt=tt, kt=kt:
                 e.activation(v2[:, kt * 512:(kt + 1) * 512], bank(4 + tt), AF.Identity), "ac")
        P.wait("tensor", "ac", P.counts["ac"])


def _mJ(J):
    if J < 4:
        return J * 128
    if J >= 12:
        return 512 + (J - 12) * 128
    if J < 8:
        return 1024 + (J - 4) * 128
    return 1536 + (J - 8) * 128


def emit_l1groups(P, T):
    ar, sb, bank, bankbf = T["ar"], T["sb"], T["bank"], T["bankbf"]
    st = lambda i: sb["stat"][:, i:i + 1]
    xnb = ar.t[:, XNT_O:XNT_O + 16384].rearrange("p (f u) -> p f u", u=512)
    wres = ar.t[:, XNT_O + 16384:XNT_O + 32768]
    q3 = ar.t[:, Q_O:Q_O + 8192].rearrange("p (cc u) -> p cc u", u=2048)
    k3 = ar.t[:, KT_O:KT_O + 8192].rearrange("p (g u) -> p g u", u=2048)
    v3 = ar.t[:, V_O:V_O + 8192].rearrange("p (kt u) -> p kt u", u=512)
    att3 = ar.t[:, ATT_O:ATT_O + 8192].rearrange("p (hh u) -> p hh u", u=2048)
    pbf = ar.view(P_O, 2048, BF16)
    pt = ar.view(PT_O, 2048, BF16)
    CS = 1.0 / np.sqrt(128.0)
    T["v_cc2"] = {}
    vcm = P.dma("sync", sb["cmask"][:], T["cmask_d"], "sd")
    P.wait("vector", "sd", vcm)

    for g in range(4):
        P.wait("gpsimd", "pe", P.counts["pe"])
        for cc in range(4):
            P.dma("gpsimd", wres[:, cc * 4096:(cc + 1) * 4096], T["wq1"][g * 4 + cc], "gd")
        vw = P.counts["gd"]
        P.wait("tensor", "gd", vw)
        lastpe = P.counts["pe"]
        for tb in range(4):
            P.wait("sync", "pe", lastpe)
            vx = P.dma("sync", ar.t[:, XNT_O:XNT_O + 16384],
                       T["cc1_out"][CC1_ROW[tb]:CC1_ROW[tb] + 128, :], "sd")
            P.wait("tensor", "sd", vx)
            for cc in range(4):
                for f in range(32):
                    P.op("tensor", lambda e, cc=cc, f=f:
                         e.matmul(bank(cc), wres[:, cc * 4096 + f * 128: cc * 4096 + (f + 1) * 128],
                                  xnb[:, f, :], start=(f == 0), stop=(f == 31)))
            lastpe = P.mark("tensor", "pe")
            P.wait("scalar", "pe", lastpe)
            for cc in range(4):
                P.op("scalar", lambda e, cc=cc, tb=tb:
                     e.activation(q3[:, cc, tb * 512:tb * 512 + 512], bank(cc), AF.Identity), "ac")
            P.wait("tensor", "ac", P.counts["ac"])
        iters = []
        for hh in range(4):
            for J in range(16):
                nkb = J // 4 + 1
                iters.append(dict(
                    qs=q3[:, hh, J * 128:J * 128 + 128],
                    ks=(lambda kb, g=g: k3[:, g, kb * 512:kb * 512 + 512]),
                    nkb=nkb,
                    vs=(lambda kt, g=g: v3[:, kt, g * 128:g * 128 + 128]),
                    nkt=nkb * 4,
                    mask=(J // 4, J % 4), c=CS, postmul=1.0,
                    avpart=(0, 128),
                    dst=att3[:, hh, _mJ(J):_mJ(J) + 128],
                ))
        emit_attention_pipe(P, T, iters)
        P.wait("sync", "dv", P.counts["dv"])
        vsg = P.dma("sync", T[f"cc2_in{g}"][:, :], ar.t[:, ATT_O:ATT_O + 8192], "sd")
        P.wait("gpsimd", "sd", vsg)
        T["v_cc2"][g] = P.collective("AllGather", OP.bypass, PAIR_GROUPS,
                                     T[f"cc2_in{g}"].opt(), T[f"cc2_out{g}"].opt(), f"cc2_{g}")


def emit_l1oproj(P, T, resid_pair):
    ar, sb = T["ar"], T["sb"]
    bl = sb["blend"]
    bb = [ar.view(W1_O, 1024, BF16), ar.view(W1_O + 1024, 1024, BF16)]
    tmp = ar.view(W1_O + 2048, 1024, BF16)
    own3 = ar.t[:, XNT_O:XNT_O + 32768].rearrange("p (lc u) -> p lc u", u=1024)
    for g in range(4):
        P.wait("sync", f"cc2_{g}", T["v_cc2"][g])
    lastbl = 0
    for lc in range(32):
        mine = lc < 16
        g, cc = (lc % 16) // 4, (lc % 16) % 4
        s0, s1 = (0, 1) if mine else (1, 0)
        if lastbl:
            P.wait("sync", "dv", lastbl)
        v0 = P.dma("sync", bb[0][:], T[f"cc2_out{g}"][128 * s0:128 * s0 + 128,
                                                      cc * 2048:cc * 2048 + 1024], "sd")
        v1 = P.dma("sync", bb[1][:], T[f"cc2_out{g}"][128 * s1:128 * s1 + 128,
                                                      cc * 2048 + 1024:cc * 2048 + 2048], "sd")
        P.wait("vector", "sd", v1)
        P.op("vector", lambda e: e.tensor_scalar(tmp[:], bb[0][:], bl[:, 0:1], None, OP.mult))
        P.op("vector", lambda e: e.drain())
        lastbl = P.op("vector", lambda e, lc=lc:
                      e.scalar_tensor_tensor(own3[:, lc, :], bb[1][:], bl[:, 1:2], tmp[:],
                                             OP.mult, OP.add), "dv")
    P.wait("tensor", "dv", lastbl)
    P.wait("gpsimd", "dv", lastbl)
    emit_bigffn(P, T, T["wo1"], 8, 4, own3, 32, None, resid_pair)


def emit_l1ffnA(P, T):
    ar, sb, bank = T["ar"], T["sb"], T["bank"]
    xnt = _xnt3(T)
    wbuf = [ar.view(W0_O, 4096, BF16), ar.view(W1_O, 4096, BF16)]
    sil = [ar.view(P_O, 1024, F32), ar.view(PT_O, 1024, F32)]
    mbf = [ar.view(ATT_O, 1024, BF16), ar.view(ATT_O + 1024, 1024, BF16)]
    P.dma("gpsimd", sb["rows"][0:1, 0:11008], T["bg"], "gd")
    P.dma("gpsimd", sb["rows2"][0:1, 0:11008], T["bu"], "gd")
    vrows = P.counts["gd"]
    last_use = [0, 0]
    last_cons = [0, 0]
    lastm = [0, 0]
    for ib in range(86):
        A = ib % 2
        B = 1 - A
        ps = T["psA"] if A == 0 else T["psB"]
        if last_use[A]:
            P.wait("gpsimd", "pe", last_use[A])
        vg = P.dma("gpsimd", wbuf[A][:], T["wg"][ib], "gd")
        P.wait("tensor", "gd", vg)
        if ib == 0:
            P.wait("tensor", "gd", vrows)
        if last_cons[A]:
            P.wait("tensor", "dv", last_cons[A])
        for th in range(2):
            P.op("tensor", lambda e, ib=ib, th=th, ps=ps:
                 e.matmul(ps[:, th * 512:th * 512 + 512], sb["rows"][0:1, ib * 128:ib * 128 + 128],
                          sb["ones"][0:1, th * 512:th * 512 + 512], start=True, stop=False))
            for f in range(32):
                P.op("tensor", lambda e, f=f, th=th, A=A, ps=ps:
                     e.matmul(ps[:, th * 512:th * 512 + 512], wbuf[A][:, f * 128:f * 128 + 128],
                              xnt[:, f, th * 512:th * 512 + 512], start=False, stop=(f == 31)))
        vgdone = P.mark("tensor", "pe")
        last_use[A] = vgdone
        if last_use[B]:
            P.wait("gpsimd", "pe", last_use[B])
        vu = P.dma("gpsimd", wbuf[B][:], T["wu"][ib], "gd")
        P.wait("tensor", "gd", vu)
        for th in range(2):
            P.op("tensor", lambda e, ib=ib, th=th, ps=ps:
                 e.matmul(ps[:, 1024 + th * 512:1024 + th * 512 + 512],
                          sb["rows2"][0:1, ib * 128:ib * 128 + 128],
                          sb["ones"][0:1, th * 512:th * 512 + 512], start=True, stop=False))
            for f in range(32):
                P.op("tensor", lambda e, f=f, th=th, B=B, ps=ps:
                     e.matmul(ps[:, 1024 + th * 512:1024 + th * 512 + 512],
                              wbuf[B][:, f * 128:f * 128 + 128],
                              xnt[:, f, th * 512:th * 512 + 512], start=False, stop=(f == 31)))
        vudone = P.mark("tensor", "pe")
        last_use[B] = vudone
        P.wait("scalar", "pe", vgdone)
        if lastm[A]:
            P.wait("scalar", "sd", lastm[A])
        vsil = P.op("scalar", lambda e, A=A, ps=ps: e.activation(sil[A][:], ps[:, 0:1024], AF.Silu), "ac")
        P.wait("vector", "ac", vsil)
        P.wait("vector", "pe", vudone)
        if lastm[A]:
            P.wait("vector", "sd", lastm[A])
        vm = P.op("vector", lambda e, A=A, ps=ps: e.tensor_tensor(mbf[A][:], sil[A][:], ps[:, 1024:2048], OP.mult), "dv")
        last_cons[A] = vm
        P.wait("sync", "dv", vm)
        lastm[A] = P.dma("sync", T["m_d"][ib * 128:(ib + 1) * 128, :], mbf[A][:], "sd")
        import os as _o
        if _o.environ.get("FFNA_SER"):
            P.wait("tensor", "dv", P.counts["dv"])
            P.wait("tensor", "ac", P.counts["ac"])


def emit_l1ffnB(P, T, resid_pair):
    ar, sb, bank = T["ar"], T["sb"], T["bank"]
    rin, rout = resid_pair
    P.dma("gpsimd", sb["rows3"][0:1, 0:4096], T["bd"], "gd")
    vrows = P.counts["gd"]
    NB = 4
    mb = [ar.t[:, ATT_O + i * 1024:ATT_O + (i + 1) * 1024] for i in range(NB)]
    wb_ = [ar.view(W0_O + i * 512, 512, BF16) for i in range(NB)]
    rload = [ar.view(R0_O, 1024, F32), ar.view(R0_O + 2048, 1024, F32)]
    rst = [ar.view(R1_O, 1024, F32), ar.view(R1_O + 2048, 1024, F32)]
    lastmm = {}
    lastr = [0, 0]
    lastst = [0, 0]
    mv = {}
    wv = {}

    def issue_loads(ob, ic):
        sl = ic % NB
        key = (ob, ic)
        if ic - NB >= 0:
            prev = (ob, ic - NB)
        else:
            pmap = {0: 84, 1: 85, 2: 82, 3: 83}
            prev = (ob - 1, pmap[ic % NB])
        if prev[0] >= 0:
            assert prev in lastmm, f"prefetch ran ahead of slot occupant {prev}"
            P.wait("sync", "pe", lastmm[prev])
            P.wait("gpsimd", "pe", lastmm[prev])
        mv[key] = P.dma("sync", mb[sl], T["m_d"][ic * 128:(ic + 1) * 128, :], f"mS{sl}")
        wv[key] = P.dma("gpsimd", wb_[sl][:], T["wd"][ob, ic], f"wS{sl}")

    for ob in range(8):
        for ic in range(86):
            sl = ic % NB
            if (ob, ic) not in mv:
                issue_loads(ob, ic)
            pmap = {0: 84, 1: 85, 2: 82, 3: 83}
            for ahead in range(1, NB):
                ic2 = ic + ahead
                ob2, ic2 = (ob, ic2) if ic2 < 86 else (ob + 1, ic2 - 86)
                if ob2 >= 8 or (ob2, ic2) in mv:
                    continue
                prev2 = (ob2, ic2 - NB) if ic2 - NB >= 0 else (ob2 - 1, pmap[ic2 % NB])
                if prev2[0] >= 0 and prev2 not in lastmm:
                    continue
                issue_loads(ob2, ic2)
            P.wait("tensor", f"mS{sl}", mv[(ob, ic)])
            P.wait("tensor", f"wS{sl}", wv[(ob, ic)])
            if ic == 0:
                P.wait("tensor", "gd", vrows)
                for t in range(8):
                    P.op("tensor", lambda e, t=t, ob=ob:
                         e.matmul(bank(t), sb["ones"][0:1, 0:128],
                                  sb["rows3"][0:1, ob * 512:ob * 512 + 512], start=True, stop=False))
            for t in range(8):
                P.op("tensor", lambda e, t=t, ic=ic, sl=sl:
                     e.matmul(bank(t), mb[sl][:, t * 128:t * 128 + 128], wb_[sl][:],
                              start=False, stop=(ic == 85)))
            lastmm[(ob, ic)] = P.mark("tensor", "pe")
            import os as _o
            if _o.environ.get("FFNB_SER"):
                P.wait("sync", "pe", lastmm[(ob, ic)])
                P.wait("gpsimd", "pe", lastmm[(ob, ic)])
        vmm = P.counts["pe"]
        for t in range(8):
            sl = t % 2
            if lastr[sl]:
                P.wait("sync", "dv", lastr[sl])
            vr = P.dma("sync", rload[sl][:, 0:512], rin[t * 128:t * 128 + 128, ob * 512:ob * 512 + 512], "sd")
            P.wait("vector", "pe", vmm)
            P.wait("vector", "sd", vr)
            if lastst[sl]:
                P.wait("vector", "sd", lastst[sl])
            vd = P.op("vector", lambda e, t=t, sl=sl:
                      e.tensor_tensor(rst[sl][:, 0:512], bank(t), rload[sl][:, 0:512], OP.add), "dv")
            lastr[sl] = vd
            P.wait("sync", "dv", vd)
            lastst[sl] = P.dma("sync", rout[t * 128:t * 128 + 128, ob * 512:ob * 512 + 512],
                               rst[sl][:, 0:512], "sd")
        P.wait("tensor", "dv", P.counts["dv"])


def emit_phase(P, T, ph, rpair, fused):
    if ph in ("ln1", "ln2", "ln3", "ln4"):
        emit_ln(P, T, rpair[0])
        if ph == "ln3":
            emit_ln3_stage(P, T)
    elif ph == "l0kvq":
        emit_l0kvq(P, T)
    elif ph == "l0attn":
        emit_l0attn(P, T)
    elif ph == "l0oproj":
        emit_l0oproj(P, T, (T[rpair[0]], T[rpair[1]]))
    elif ph == "l0sel":
        emit_l0sel(P, T)
    elif ph == "l0dense":
        emit_l0dense(P, T, (T[rpair[0]], T[rpair[1]]))
    elif ph == "l1kv":
        emit_l1kv(P, T)
    elif ph == "l1groups":
        emit_l1groups(P, T)
    elif ph == "l1oproj":
        emit_l1oproj(P, T, (T[rpair[0]], T[rpair[1]]))
    elif ph == "l1ffnA":
        emit_l1ffnA(P, T)
    elif ph == "l1ffnB":
        emit_l1ffnB(P, T, (T[rpair[0]], T[rpair[1]]))
    elif ph == "dump":
        import os
        off = int(os.environ.get("DUMP_OFF", str(XNT_O)))
        ln_ = int(os.environ.get("DUMP_LEN", "32768"))
        ng = ln_ // 4096
        out3 = T["rout"].rearrange("(g p) d -> p g d", p=128)
        src3 = T["ar"].t[:, off:off + ng * 4096].rearrange("p (g d) -> p g d", d=4096)
        P.dma("gpsimd", out3[:, 0:ng, :], src3, "gd")
        P.wait("gpsimd", "gd", P.counts["gd"])
    elif ph == "dumpR":
        out3 = T["rout"].rearrange("(g p) d -> p g d", p=128)
        src2 = T["ar"].t[:, R0_O:R0_O + 8192].bitcast(F32)
        P.dma("sync", T["rout"][0:128, :], src2, "sd")
        P.wait("sync", "sd", P.counts["sd"])
    else:
        raise ValueError(ph)



def _regroup(a, g):
    n, p, w = a.shape
    return np.ascontiguousarray(a.reshape(n // g, g, p, w).transpose(0, 2, 1, 3).reshape(n // g, p, g * w))


def _host_arrays(inputs):
    sh, cores = prep_inputs(inputs)
    sh["wv0"] = _regroup(sh["wv0"], 4)
    sh["cent"] = np.ascontiguousarray(sh["cent"].transpose(1, 0, 2).reshape(128, 2048))
    sh["wo0"] = _regroup4(sh["wo0"])
    sh["wf"] = _regroup4(sh["wf"])
    for c in cores:
        c["wv1"] = _regroup(c["wv1"], 4)
        c["wo1"] = _regroup4(c["wo1"])
    return sh, cores


def _regroup4(a):
    nob, nk, p, w = a.shape
    return np.ascontiguousarray(a.reshape(nob, nk // 4, 4, p, w).transpose(0, 1, 3, 2, 4)
                                .reshape(nob, nk // 4, p, 4 * w))


LAUNCHES = ["A", "B", "C", "D"]
import os as _os_mod
FUSED = _os_mod.environ.get("MK_UNFUSED", "") == ""


def kernel(**inputs):
    sh, cores = _host_arrays(inputs)
    if FUSED:
        nc = build_launch("ALL")
        innames = set(a.name[:-4] if a.name.endswith("_set") else a.name
                      for a in nc.m.functions[0].allocations
                      if getattr(a, "kind", None) == "ExternalInput")
        innames.discard("partition_id")
        in_maps = []
        for ci in range(NCORES):
            m = {k: v for k, v in sh.items() if k in innames}
            m.update({k: v for k, v in cores[ci].items() if k in innames})
            m["rin"] = cores[ci]["x"]
            in_maps.append(m)
        res = run_bass_kernel_spmd(nc, in_maps, core_ids=list(range(NCORES)))
        return scatter_output([{"out": res.results[ci]["out"]} for ci in range(NCORES)])
    r_cur = [c["x"] for c in cores]
    for li, which in enumerate(LAUNCHES):
        nc = build_launch(which)
        innames = set(a.name[:-4] if a.name.endswith("_set") else a.name
                      for a in nc.m.functions[0].allocations
                      if getattr(a, "kind", None) == "ExternalInput")
        innames.discard("partition_id")
        in_maps = []
        for ci in range(NCORES):
            m = {}
            for k, v in sh.items():
                if k in innames:
                    m[k] = v
            for k, v in cores[ci].items():
                if k in innames:
                    m[k] = v
            m["rin"] = r_cur[ci]
            in_maps.append(m)
        res = run_bass_kernel_spmd(nc, in_maps, core_ids=list(range(NCORES)))
        r_cur = [res.results[ci]["out"] for ci in range(NCORES)]
    return scatter_output([{"out": r} for r in r_cur])
